# version 02
# baseline (speedup 1.0000x reference)
import numpy as np
import ml_dtypes
import concourse.bass as bass
import concourse.mybir as mybir
from concourse.bass_utils import run_bass_kernel_spmd

F32 = mybir.dt.float32
BF16 = mybir.dt.bfloat16
AF = mybir.ActivationFunctionType
ALU = mybir.AluOpType
AX = mybir.AxisListType
BF = ml_dtypes.bfloat16

NCORES = 8
T = 2048
D = 1024
NCH = 8
WIN = 512
NW = T // WIN

ENGINES = ("tensor", "scalar", "vector", "gpsimd", "sync")


class Rec:

    def __init__(self, nc):
        self.nc = nc
        self.steps = {e: [] for e in ENGINES}
        self.counts = {}
        self.gate = ()

    def emit(self, eng, fn, waits=(), inc=None, inc_n=1):
        ws = tuple((m[0], m[1]) for m in tuple(self.gate) + tuple(waits)
                   if m is not None)
        self.steps[eng].append((fn, ws, inc, inc_n))
        if inc is None:
            return None
        self.counts[inc] = self.counts.get(inc, 0) + inc_n
        return (inc, self.counts[inc])

    def dma(self, eng, out, in_, waits=(), sem="ld", **kw):
        return self.emit(eng, lambda e, o=out, i=in_, kw=kw: e.dma_start(out=o, in_=i, **kw),
                         waits=waits, inc=sem, inc_n=16)

    def finalize(self, final_waits):
        nc = self.nc
        self.emit("vector", lambda e: e.engine_nop(), waits=list(final_waits))
        handles = {n: nc.alloc_semaphore(n) for n in self.counts}
        for n, c in self.counts.items():
            assert c < 60000, f"semaphore {n} count {c} too large"
        with nc.Block() as block:
            for eng in ENGINES:
                steps = self.steps[eng]

                def body(e, steps=steps):
                    last = {}
                    for fn, waits, inc, inc_n in steps:
                        for (s, v) in waits:
                            if last.get(s, -1) < v:
                                e.wait_ge(handles[s], v)
                                last[s] = v
                        ins = fn(e)
                        if inc is not None:
                            ins.then_inc(handles[inc], inc_n)
                getattr(block, eng)(body)


def after(*ms):
    return [m for m in ms if m is not None]


class Ctx:
    def __init__(self, nc, rec, arena=None):
        self.nc = nc
        self.rec = rec
        self.buf = {}
        self.arena = arena
        self.ptr = arena[0] if arena else None
        self.pfx = ""
        self.ps_shared = None

    def reset(self, pfx):
        self.ptr = self.arena[0]
        self.pfx = pfx
        self.buf = {}

    def sb(self, name, shape, dtype):
        if self.arena is None:
            t = self.nc.alloc_sbuf_tensor(name, shape, dtype)
        else:
            nb = shape[1] * (4 if dtype == F32 else 2)
            off = (self.ptr + 31) & ~31
            assert off + nb <= self.arena[1], (self.pfx, name, off, nb)
            t = self.nc.alloc_sbuf_tensor_at(self.pfx + name, shape, dtype,
                                             offset=off)
            self.ptr = off + nb
        self.buf[name] = t
        return t

    def ps(self, name, shape, dtype=F32):
        if self.ps_shared is not None:
            self.buf[name] = self.ps_shared
            return self.ps_shared
        t = self.nc.alloc_psum_tensor(name, shape, dtype)
        self.buf[name] = t
        if self.arena is not None:
            self.ps_shared = t
        return t



def emit_ln(ctx, src, dst, width, cast_waits, ones_col_m, ones_row_m,
            g_ap=None, b_ap=None):
    nc, rec = ctx.nc, ctx.rec
    w = width
    XB, XSQ, SROW, PS = ctx.buf["XB"], ctx.buf["XSQ"], ctx.buf["SROW"], ctx.buf["PS"]
    SRB, XAF = ctx.buf["SRB"], ctx.buf["XAF"]
    ONC = ctx.buf["ONC"]
    sx = PS[0:1, 7 * 512:7 * 512 + w]
    sxx = PS[0:1, 6 * 512:6 * 512 + w]
    bA = PS[:, 4 * 512:4 * 512 + w]
    bB = PS[:, 5 * 512:5 * 512 + w]

    m_cast = []
    for c in range(NCH):
        m = rec.emit("vector", lambda e, c=c: e.tensor_copy(
            out=XB[:, c * w:(c + 1) * w], in_=src(c, 0, w)),
            waits=after(*(cast_waits(c) or [])), inc="dv")
        m_cast.append(m)
    m_sx = None
    for c in range(NCH):
        m_sx = rec.emit("tensor", lambda e, c=c: e.matmul(
            sx, ONC[:, 0:1], XB[:, c * w:(c + 1) * w],
            start=(c == 0), stop=(c == NCH - 1)),
            waits=after(m_cast[-1], ones_col_m) if c == 0 else (),
            inc="ts" if c == NCH - 1 else None)
    m_sxx = None
    for c in range(NCH):
        sqm = rec.emit("scalar", lambda e, c=c: e.activation(
            out=XSQ[:, 0:w], in_=XB[:, c * w:(c + 1) * w],
            func=AF.Square),
            waits=after(m_cast[c], m_sxx if c >= 1 else None), inc="ac")
        m_sxx = rec.emit("tensor", lambda e, c=c: e.matmul(
            sxx, ONC[:, 0:1], XSQ[:, 0:w],
            start=(c == 0), stop=(c == NCH - 1)),
            waits=after(sqm), inc="ts")
    m1 = rec.emit("vector", lambda e: e.tensor_scalar(
        out=SROW[0:1, 0:w], in0=sx, scalar1=1.0 / D, scalar2=None, op0=ALU.mult),
        waits=after(m_sx, m_sxx), inc="dv")
    m2 = rec.emit("vector", lambda e: e.tensor_scalar(
        out=SROW[0:1, w:2 * w], in0=sxx, scalar1=1.0 / D, scalar2=None, op0=ALU.mult),
        inc="dv")
    m3 = rec.emit("vector", lambda e: e.tensor_tensor(
        out=SROW[0:1, 3 * w:4 * w], in0=SROW[0:1, 0:w], in1=SROW[0:1, 0:w], op=ALU.mult),
        waits=after(m1), inc="dv")
    m4 = rec.emit("vector", lambda e: e.tensor_tensor(
        out=SROW[0:1, 2 * w:3 * w], in0=SROW[0:1, w:2 * w], in1=SROW[0:1, 3 * w:4 * w],
        op=ALU.subtract), waits=after(m2, m3), inc="dv")
    m_var = rec.emit("vector", lambda e: e.tensor_scalar(
        out=SROW[0:1, 2 * w:3 * w], in0=SROW[0:1, 2 * w:3 * w], scalar1=1e-5,
        scalar2=None, op0=ALU.add), waits=after(m4), inc="dv")
    m_ln2 = rec.emit("scalar", lambda e: e.activation(
        out=SROW[0:1, 2 * w:3 * w], in_=SROW[0:1, 2 * w:3 * w], func=AF.Ln),
        waits=after(m_var), inc="ac")
    m_rstd = rec.emit("scalar", lambda e: e.activation(
        out=SROW[0:1, 2 * w:3 * w], in_=SROW[0:1, 2 * w:3 * w], func=AF.Exp, scale=-0.5),
        waits=after(m_ln2), inc="ac")
    m_B = rec.emit("vector", lambda e: e.scalar_tensor_tensor(
        out=SROW[0:1, 3 * w:4 * w], in0=SROW[0:1, 0:w], scalar=-1.0,
        in1=SROW[0:1, 2 * w:3 * w], op0=ALU.mult, op1=ALU.mult),
        waits=after(m_rstd), inc="dv")
    m_hl = emit_hilo(ctx, SROW[0:1, 2 * w:3 * w], SRB[0:1, 0:w], SRB[0:1, w:2 * w], m_rstd)
    m_hlB = emit_hilo(ctx, SROW[0:1, 3 * w:4 * w], SRB[0:1, 2 * w:3 * w], SRB[0:1, 3 * w:4 * w], m_B)
    m_Ab = emit_bcast(ctx, bA, SRB[0:1, 0:w], SRB[0:1, w:2 * w], after(m_hl, ones_row_m))
    m_Bb = emit_bcast(ctx, bB, SRB[0:1, 2 * w:3 * w], SRB[0:1, 3 * w:4 * w], after(m_hlB))
    m_last = None
    for c in range(NCH):
        mm1 = rec.emit("vector", lambda e, c=c: e.tensor_tensor(
            out=XAF[:, 0:w], in0=src(c, 0, w), in1=bA, op=ALU.mult),
            waits=after(m_Ab if c == 0 else None, m_last), inc="dv")
        if g_ap is None:
            m_last = rec.emit("vector", lambda e, c=c: e.tensor_tensor(
                out=dst(c, 0, w), in0=XAF[:, 0:w], in1=bB, op=ALU.add),
                waits=after(m_Bb if c == 0 else None, mm1), inc="dv")
        else:
            XAF2 = ctx.buf["XAF2"]
            mmid = rec.emit("vector", lambda e: e.tensor_tensor(
                out=XAF2[:, 0:w], in0=XAF[:, 0:w], in1=bB, op=ALU.add),
                waits=after(m_Bb if c == 0 else None, mm1), inc="dv")
            m_last = rec.emit("vector", lambda e, c=c, s1=g_ap(c), s2=b_ap(c): e.tensor_scalar(
                out=dst(c, 0, w), in0=XAF2[:, 0:w], scalar1=s1, scalar2=s2,
                op0=ALU.mult, op1=ALU.add), waits=after(mmid), inc="dv")
    return m_last


def emit_hilo(ctx, row_f32, hi_bf, lo_bf, dep):
    rec = ctx.rec
    mh = rec.emit("vector", lambda e: e.tensor_copy(out=hi_bf, in_=row_f32),
                  waits=after(dep), inc="dv")
    return rec.emit("vector", lambda e: e.tensor_tensor(
        out=lo_bf, in0=row_f32, in1=hi_bf, op=ALU.subtract), waits=after(mh), inc="dv")


def emit_bcast(ctx, psum_dst, hi_bf, lo_bf, waits):
    rec = ctx.rec
    ONR = ctx.buf["ONR"]
    rec.emit("tensor", lambda e: e.matmul(
        psum_dst, ONR[0:1, 0:128], hi_bf, start=True, stop=False),
        waits=waits, inc=None)
    return rec.emit("tensor", lambda e: e.matmul(
        psum_dst, ONR[0:1, 0:128], lo_bf, start=False, stop=True), inc="ts")



def emit_s4(ctx, x_in, xo, wrT, wkT, wvT, mur, muk, onc, onr):
    nc, rec = ctx.nc, ctx.rec
    TX = T + 1
    X = ctx.sb("X", [128, NCH * TX], F32)
    XH = ctx.sb("XH", [128, NCH * (WIN + 1)], BF16)
    XB = ctx.sb("XB", [128, NCH * WIN], BF16)
    XSQ = ctx.sb("XSQ", [128, WIN], BF16)
    XR = ctx.sb("XR", [128, NCH * WIN], BF16)
    XK = ctx.sb("XK", [128, NCH * WIN], BF16)
    K2 = ctx.sb("K2", [128, NCH * WIN], BF16)
    R = ctx.sb("R", [128, NCH * WIN], BF16)
    TMP = ctx.sb("TMP", [128, NCH * WIN], BF16)
    TMPF = ctx.sb("TMPF", [128, 2 * WIN], F32)
    WR = ctx.sb("WR", [128, NCH * D], BF16)
    WK = ctx.sb("WK", [128, NCH * D], BF16)
    WV = ctx.sb("WV", [128, NCH * D], BF16)
    MU = ctx.sb("MU", [128, 4 * NCH], F32)
    ONC = ctx.sb("ONC", [128, 1], BF16)
    ONR = ctx.sb("ONR", [1, 128], BF16)
    SROW = ctx.sb("SROW", [1, 4 * WIN + 16], F32)
    SRB = ctx.sb("SRB", [1, 4 * WIN + 16], BF16)
    XAF = ctx.sb("XAF", [128, WIN], F32)
    PS = ctx.ps("PS", [128, 4096], F32)
    bank = lambda b: PS[:, b * 512:b * 512 + WIN]

    m_x = rec.dma("sync", X[:], x_in.rearrange("(c p) t -> p c t", p=128), sem="ld_x")
    m_wr = rec.dma("gpsimd", WR[:], wrT.rearrange("(k p) m -> p k m", p=128), sem="ld_wr")
    m_wk = rec.dma("gpsimd", WK[:], wkT.rearrange("(k p) m -> p k m", p=128), sem="ld_wk")
    m_wv = rec.dma("gpsimd", WV[:], wvT.rearrange("(k p) m -> p k m", p=128), sem="ld_wv")
    rec.dma("sync", MU[:, 0:NCH], mur, sem="ld_mu")
    m_mu = rec.dma("sync", MU[:, NCH:2 * NCH], muk, sem="ld_mu")
    m_onc = rec.dma("gpsimd", ONC[:], onc, sem="ld_onc")
    m_onr = rec.dma("gpsimd", ONR[:], onr, sem="ld_onr")
    m_mu1m = rec.emit("vector", lambda e: e.tensor_scalar(
        out=MU[:, 2 * NCH:4 * NCH], in0=MU[:, 0:2 * NCH],
        scalar1=-1.0, scalar2=1.0, op0=ALU.mult, op1=ALU.add),
        waits=after(m_mu), inc="dv")

    Xs = lambda c, a, b: X[:, c * TX + a:c * TX + b]
    XHs = lambda c, a, b: XH[:, c * (WIN + 1) + a:c * (WIN + 1) + b]

    m_resid = [None] * NCH
    m_xh_last = None
    m_store = None
    for w in range(NW):
        c0 = 1 + w * WIN
        carry = []
        if w > 0:
            for c in range(NCH):
                mcp = rec.emit("vector", lambda e, c=c: e.tensor_copy(
                    out=XHs(c, 0, 1), in_=XHs(c, WIN, WIN + 1)),
                    waits=after(m_xh_last), inc="dv")
                carry.append(mcp)
        if w == 0:
            cw = lambda c: [m_x, m_resid[c]]
        else:
            cw = lambda c, carry=carry: [carry[c], m_resid[c]]
        m_xh_last = emit_ln(ctx, lambda c, a, b, c0=c0: Xs(c, c0 + a, c0 + b),
                            lambda c, a, b: XHs(c, 1 + a, 1 + b),
                            WIN, cw, m_onc, m_onr)
        if w == 0:
            m_xh_last = emit_ln(ctx, lambda c, a, b: Xs(c, a, b),
                                lambda c, a, b: XHs(c, a, b),
                                1, lambda c, m=m_xh_last: [m], m_onc, m_onr)
        m_mix_last = None
        for c in range(NCH):
            for (dst, mu_i) in ((XR, 0), (XK, 1)):
                rec.emit("vector", lambda e, c=c, mu_i=mu_i: e.tensor_scalar(
                    out=TMP[:, c * WIN:(c + 1) * WIN], in0=XHs(c, 0, WIN),
                    scalar1=MU[:, (2 + mu_i) * NCH + c:(2 + mu_i) * NCH + c + 1],
                    scalar2=None, op0=ALU.mult),
                    waits=after(m_xh_last, m_mu1m) if c == 0 and mu_i == 0 else (),
                    inc="dv")
                m_mix_last = rec.emit("vector", lambda e, c=c, mu_i=mu_i, dst=dst: e.scalar_tensor_tensor(
                    out=dst[:, c * WIN:(c + 1) * WIN], in0=XHs(c, 1, WIN + 1),
                    scalar=MU[:, mu_i * NCH + c:mu_i * NCH + c + 1],
                    in1=TMP[:, c * WIN:(c + 1) * WIN], op0=ALU.mult, op1=ALU.add),
                    inc="dv")
        m_evac = {}
        for m in range(NCH):
            bk = m % 2
            mm = None
            for k in range(NCH):
                mm = rec.emit("tensor", lambda e, m=m, k=k, bk=bk: e.matmul(
                    bank(bk), WR[:, k * D + m * 128:k * D + (m + 1) * 128],
                    XR[:, k * WIN:(k + 1) * WIN], start=(k == 0), stop=(k == NCH - 1)),
                    waits=after(m_mix_last if m == 0 and k == 0 else None,
                                m_wr if m == 0 and k == 0 else None,
                                m_evac.get(("r", m - 2)) if k == 0 else None),
                    inc="ts" if k == NCH - 1 else None)
            m_evac[("r", m)] = rec.emit("scalar", lambda e, m=m, bk=bk: e.activation(
                out=R[:, m * WIN:(m + 1) * WIN], in_=bank(bk), func=AF.Sigmoid),
                waits=after(mm), inc="ac")
        m_sq_last = None
        for m in range(NCH):
            bk = 2 + m % 2
            mm = None
            for k in range(NCH):
                mm = rec.emit("tensor", lambda e, m=m, k=k, bk=bk: e.matmul(
                    bank(bk), WK[:, k * D + m * 128:k * D + (m + 1) * 128],
                    XK[:, k * WIN:(k + 1) * WIN], start=(k == 0), stop=(k == NCH - 1)),
                    waits=after(m_wk if m == 0 and k == 0 else None,
                                m_evac.get(("k", m - 2)) if k == 0 else None),
                    inc="ts" if k == NCH - 1 else None)
            mr = rec.emit("vector", lambda e, m=m, bk=bk: e.tensor_scalar_max(
                out=TMP[:, m * WIN:(m + 1) * WIN], in0=bank(bk), scalar1=0.0),
                waits=after(mm), inc="dv")
            m_evac[("k", m)] = mr
            m_sq_last = rec.emit("scalar", lambda e, m=m: e.activation(
                out=K2[:, m * WIN:(m + 1) * WIN], in_=TMP[:, m * WIN:(m + 1) * WIN],
                func=AF.Square), waits=after(mr), inc="ac")
        for m in range(NCH):
            bk = 4 + m % 2
            mm = None
            for k in range(NCH):
                mm = rec.emit("tensor", lambda e, m=m, k=k, bk=bk: e.matmul(
                    bank(bk), WV[:, k * D + m * 128:k * D + (m + 1) * 128],
                    K2[:, k * WIN:(k + 1) * WIN], start=(k == 0), stop=(k == NCH - 1)),
                    waits=after(m_sq_last if m == 0 and k == 0 else None,
                                m_wv if m == 0 and k == 0 else None,
                                m_evac.get(("v", m - 2)) if k == 0 else None),
                    inc="ts" if k == NCH - 1 else None)
            mt = rec.emit("vector", lambda e, m=m, bk=bk: e.tensor_tensor(
                out=TMPF[:, (m % 2) * WIN:(m % 2) * WIN + WIN], in0=bank(bk),
                in1=R[:, m * WIN:(m + 1) * WIN], op=ALU.mult),
                waits=after(mm), inc="dv")
            m_evac[("v", m)] = mt
            m_resid[m] = rec.emit("vector", lambda e, m=m, c0=c0: e.scalar_tensor_tensor(
                out=Xs(m, c0, c0 + WIN), in0=TMPF[:, (m % 2) * WIN:(m % 2) * WIN + WIN],
                scalar=0.5, in1=Xs(m, c0, c0 + WIN), op0=ALU.mult, op1=ALU.add),
                inc="dv")
        if xo is not None and w == NW - 1:
            xo_r = xo.rearrange("(c p) t -> p c t", p=128)
            for c in range(NCH):
                m_store = rec.dma("sync", xo_r[:, c, 0:T], Xs(c, 1, 1 + T),
                                  waits=after(*m_resid), sem="st")
    return [m_store]


def build_s4(nc):
    rec = Rec(nc)
    ctx = Ctx(nc, rec)
    TX = T + 1
    x_in = nc.dram_tensor("x", [D, TX], F32, kind="ExternalInput").ap()
    wrT = nc.dram_tensor("wrT", [D, D], F32, kind="ExternalInput").ap()
    wkT = nc.dram_tensor("wkT", [D, D], F32, kind="ExternalInput").ap()
    wvT = nc.dram_tensor("wvT", [D, D], F32, kind="ExternalInput").ap()
    mur = nc.dram_tensor("mur", [128, NCH], F32, kind="ExternalInput").ap()
    muk = nc.dram_tensor("muk", [128, NCH], F32, kind="ExternalInput").ap()
    onc = nc.dram_tensor("ones_col", [128, 1], F32, kind="ExternalInput").ap()
    onr = nc.dram_tensor("ones_row", [1, 128], F32, kind="ExternalInput").ap()
    xo = nc.dram_tensor("xo", [D, T], F32, kind="ExternalOutput").ap()
    mil = emit_s4(ctx, x_in, xo, wrT, wkT, wvT, mur, muk, onc, onr)
    rec.finalize(mil)
    return nc


NHEAD = 16
NTB = T // 128
F2 = 128


def emit_s2(ctx, x_in, xo, wq, wk, wv, wo, kqp, kkp, sel, onc, onr, tri, ident, stage=100):
    nc, rec = ctx.nc, ctx.rec

    cc_in = nc.dram_tensor("cc_in", [128, NHEAD * 65], F32)
    cc_out = nc.dram_tensor("cc_out", [NCORES * 128, NHEAD * 65], F32)
    o_dram = nc.dram_tensor("o_dram", [D, T], BF16, kind="ExternalOutput")

    XWIN = ctx.sb("XWIN", [128, NCH * WIN], F32)
    XH = ctx.sb("XH2", [128, NCH * T], BF16)
    V = ctx.sb("V", [128, NTB * D], BF16)
    WQ = ctx.sb("WQ", [128, NCH * D], BF16)
    WK = ctx.sb("WK2", [128, NCH * D], BF16)
    WX = ctx.sb("WX", [128, NCH * D], BF16)
    KQT = ctx.sb("KQT", [128, NHEAD * F2], BF16)
    KKT = ctx.sb("KKT", [128, NHEAD * F2], BF16)
    SEL = ctx.sb("SEL", [128, NCORES], F32)
    QP = ctx.sb("QP", [128, T], BF16)
    KP = ctx.sb("KP", [128, T], BF16)
    QF = ctx.sb("QF", [128, T], BF16)
    KFF = ctx.sb("KFF", [128, T], BF16)
    KFTP = ctx.sb("KFTP", [128, NTB * 2 * F2], BF16)
    VAP = ctx.sb("VAP", [128, NTB * 2 * 65], BF16)
    ZK = ctx.sb("ZK", [128, 4 * NTB], F32)
    OST = ctx.sb("OST", [128, NTB * 128], F32)
    OTB = ctx.sb("OTB", [128, NTB * 128], BF16)
    IDTF = ctx.sb("IDTF", [128, 128], F32)
    PT = ctx.sb("PT", [128, F2], BF16)
    STSB = ctx.sb("STSB", [128, 65], BF16)
    ZI = ctx.sb("ZI", [128, 4], F32)
    AGS = ctx.sb("AGS", [128, NHEAD * 65], F32)
    PRE = AGS
    PREB = ctx.sb("PREB", [128, NHEAD * 65], BF16)
    TRI = ctx.sb("TRI", [128, 128], BF16)
    IDT = ctx.sb("IDT", [128, 128], BF16)
    XB = ctx.sb("XB", [128, NCH * WIN], BF16)
    XSQ = ctx.sb("XSQ", [128, WIN], BF16)
    XAF = ctx.sb("XAF", [128, WIN], F32)
    SROW = ctx.sb("SROW", [1, 4 * WIN + 16], F32)
    SRB = ctx.sb("SRB", [1, 4 * WIN + 16], BF16)
    ONC = ctx.sb("ONC", [128, 1], BF16)
    ONR = ctx.sb("ONR", [1, 128], BF16)
    PS = ctx.ps("PS", [128, 4096], F32)
    bank = lambda b, wdt=512: PS[:, b * 512:b * 512 + wdt]
    OFP = XB
    PRT = XWIN

    m_wq = rec.dma("gpsimd", WQ[:], wq.rearrange("(k p) m -> p k m", p=128), sem="ld_wq")
    m_wk = rec.dma("gpsimd", WK[:], wk.rearrange("(k p) m -> p k m", p=128), sem="ld_wk")
    m_wv = rec.dma("gpsimd", WX[:], wv.rearrange("(k p) m -> p k m", p=128), sem="ld_wv")
    m_kq = rec.dma("gpsimd", KQT[:], kqp.rearrange("(h p) f -> p h f", p=128), sem="ld_kq")
    m_kk = rec.dma("gpsimd", KKT[:], kkp.rearrange("(h p) f -> p h f", p=128), sem="ld_kk")
    m_sel = rec.dma("sync", SEL[:], sel, sem="ld_sel")
    m_onc = rec.dma("gpsimd", ONC[:], onc, sem="ld_onc")
    m_onr = rec.dma("gpsimd", ONR[:], onr, sem="ld_onr")
    m_tri = rec.dma("gpsimd", TRI[:], tri, sem="ld_tri")
    m_idt = rec.dma("gpsimd", IDT[:], ident, sem="ld_idt")
    m_idtf = rec.dma("sync", IDTF[:], ident, sem="ld_idtf")
    m_neg = None
    for h in range(NHEAD):
        for KT in (KQT, KKT):
            m_neg = rec.emit("vector", lambda e, h=h, KT=KT: e.tensor_scalar(
                out=KT[:, h * F2 + 64:h * F2 + 128], in0=KT[:, h * F2:h * F2 + 64],
                scalar1=-1.0, scalar2=None, op0=ALU.mult),
                waits=after(m_kq, m_kk) if h == 0 else (), inc="dv")

    m_ln = None
    for w in range(NW):
        m_xw = rec.dma("sync", XWIN[:],
                       x_in.rearrange("(c p) t -> p c t", p=128)[:, :, w * WIN:(w + 1) * WIN],
                       waits=after(m_ln), sem="ld_xw")
        m_ln = emit_ln(ctx, lambda c, a, b: XWIN[:, c * WIN + a:c * WIN + b],
                       lambda c, a, b, w=w: XH[:, c * T + w * WIN + a:c * T + w * WIN + b],
                       WIN, lambda c, m_xw=m_xw: [m_xw], m_onc, m_onr)
    if stage <= 1:
        ms = rec.dma("gpsimd", xo.rearrange("(c p) t -> p c t", p=128), XH[:],
                     waits=after(m_ln), sem="st")
        return [ms]

    m_prev_evac = [None, None]
    m_v = None
    for tb in range(NTB):
        mm = [None, None]
        for k in range(NCH):
            for half in range(2):
                mm[half] = rec.emit("tensor", lambda e, tb=tb, k=k, half=half: e.matmul(
                    bank(half), XH[:, k * T + tb * 128:k * T + (tb + 1) * 128],
                    WX[:, k * D + half * 512:k * D + (half + 1) * 512],
                    start=(k == 0), stop=(k == NCH - 1)),
                    waits=after(m_ln if tb == 0 and k == 0 and half == 0 else None,
                                m_wv if tb == 0 and k == 0 and half == 0 else None,
                                m_prev_evac[half] if k == 0 else None),
                    inc="ts" if k == NCH - 1 else None)
        for half in range(2):
            m_v = rec.emit("vector", lambda e, tb=tb, half=half: e.tensor_copy(
                out=V[:, tb * D + half * 512:tb * D + (half + 1) * 512], in_=bank(half)),
                waits=after(mm[half]), inc="dv")
            m_prev_evac[half] = m_v
    m_v_done = m_v
    if stage <= 2:
        ms = rec.dma("gpsimd", xo.rearrange("(c p) t -> p c t", p=128)[:, 0:8, :],
                     V[:, 0:8 * 2048], waits=after(m_v_done), sem="st")
        return [ms]

    def proj_pair(p, Wsb, m_w, OUT, m_prev_use):
        m_ev = None
        for w in range(NW):
            mm = None
            for k in range(NCH):
                mm = rec.emit("tensor", lambda e, p=p, w=w, k=k, Wsb=Wsb: e.matmul(
                    bank(2), Wsb[:, k * D + p * 128:k * D + (p + 1) * 128],
                    XH[:, k * T + w * WIN:k * T + (w + 1) * WIN],
                    start=(k == 0), stop=(k == NCH - 1)),
                    waits=after(m_w if w == 0 and k == 0 else None,
                                m_prev_use if w == 0 and k == 0 else None,
                                m_ev if k == 0 else None),
                    inc="ts" if k == NCH - 1 else None)
            m_ev = rec.emit("vector", lambda e, w=w, OUT=OUT: e.tensor_copy(
                out=OUT[:, w * WIN:(w + 1) * WIN], in_=bank(2)),
                waits=after(mm), inc="dv")
        return m_ev

    def pair_features(p, m_kp, m_prev_done):
        m_e = None
        m_prev_exp = m_prev_done
        for tb in range(NTB):
            mm = None
            for half, hd in ((0, p * 2), (1, p * 2 + 1)):
                mm = rec.emit("tensor", lambda e, tb=tb, half=half, hd=hd: e.matmul(
                    PS[:, 3 * 512 + half * 128:3 * 512 + half * 128 + 128],
                    KP[:, tb * 128:(tb + 1) * 128], KKT[:, hd * F2:(hd + 1) * F2],
                    start=True, stop=True),
                    waits=after(m_kp if tb == 0 and half == 0 else None,
                                m_neg if tb == 0 and half == 0 else None,
                                m_prev_exp if half == 0 else None),
                    inc="ts" if half == 1 else None)
            m_exp = rec.emit("scalar", lambda e, tb=tb: e.activation(
                out=KFTP[:, tb * 2 * F2:(tb + 1) * 2 * F2], in_=PS[:, 3 * 512:3 * 512 + 256],
                func=AF.Exp), waits=after(mm), inc="ac")
            m_prev_exp = m_exp
            m_zk = rec.emit("vector", lambda e, tb=tb: e.reduce_sum(
                out=ZK[:, 2 * tb:2 * tb + 2],
                in_=KFTP.ap().rearrange("p (tb s f) -> p tb s f", tb=NTB, s=2)[:, tb],
                axis=AX.X), waits=after(m_exp), inc="dv")
            m_zki = rec.emit("vector", lambda e, tb=tb: e.reciprocal(
                out=ZK[:, 2 * NTB + 2 * tb:2 * NTB + 2 * tb + 2], in_=ZK[:, 2 * tb:2 * tb + 2]),
                waits=after(m_zk), inc="dv")
            for half, hd in ((0, p * 2), (1, p * 2 + 1)):
                m1 = rec.emit("vector", lambda e, tb=tb, half=half, hd=hd: e.tensor_scalar(
                    out=VAP[:, tb * 130 + half * 65:tb * 130 + half * 65 + 64],
                    in0=V[:, tb * D + hd * 64:tb * D + (hd + 1) * 64],
                    scalar1=ZK[:, 2 * NTB + 2 * tb + half:2 * NTB + 2 * tb + half + 1],
                    scalar2=None, op0=ALU.mult),
                    waits=after(m_zki, m_v_done if tb == 0 and half == 0 else None), inc="dv")
                m_e = rec.emit("vector", lambda e, tb=tb, half=half: e.tensor_copy(
                    out=VAP[:, tb * 130 + half * 65 + 64:tb * 130 + half * 65 + 65],
                    in_=ZK[:, 2 * NTB + 2 * tb + half:2 * NTB + 2 * tb + half + 1]),
                    waits=after(m1), inc="dv")
        return m_e

    def head_fm_feats(h, SRC, msrc, KT, OUT, bk, m_prev):
        m_q = m_prev
        for w in range(NW):
            mq = rec.emit("tensor", lambda e, h=h, w=w, bk=bk, KT=KT, SRC=SRC: e.matmul(
                bank(bk, 512), KT[:, h * F2:(h + 1) * F2], SRC[:, w * WIN:(w + 1) * WIN],
                start=True, stop=True),
                waits=after(msrc if w == 0 else None, m_neg if w == 0 else None, m_q),
                inc="ts")
            m_q = rec.emit("scalar", lambda e, w=w, bk=bk, OUT=OUT: e.activation(
                out=OUT[:, w * WIN:(w + 1) * WIN], in_=bank(bk, 512), func=AF.Exp),
                waits=after(mq), inc="ac")
        return m_q

    m_pair_use = m_v_done
    m_tot_evac = None
    for p in range(NCH):
        m_kp = proj_pair(p, WK, m_wk, KP, m_pair_use)
        m_va = pair_features(p, m_kp, m_pair_use)
        for hh in range(2):
            h = 2 * p + hh
            m_upd = None
            for tb in range(NTB):
                m_upd = rec.emit("tensor", lambda e, tb=tb, hh=hh: e.matmul(
                    PS[:, 0:65], KFTP[:, tb * 2 * F2 + hh * F2:tb * 2 * F2 + (hh + 1) * F2],
                    VAP[:, tb * 130 + hh * 65:tb * 130 + (hh + 1) * 65],
                    start=(tb == 0), stop=(tb == NTB - 1)),
                    waits=after(m_va if tb == 0 else None,
                                m_tot_evac if tb == 0 else None),
                    inc="ts" if tb == NTB - 1 else None)
            m_tot_evac = rec.emit("vector", lambda e, h=h: e.tensor_copy(
                out=AGS[:, h * 65:(h + 1) * 65], in_=PS[:, 0:65]),
                waits=after(m_upd), inc="dv")
        m_pair_use = m_tot_evac
    m_agin = rec.dma("gpsimd", cc_in.ap(), AGS[:], waits=after(m_tot_evac), sem="cc_in")
    m_ag = rec.emit("gpsimd", lambda e: e.collective_compute(
        "AllGather", ALU.bypass, replica_groups=[list(range(NCORES))],
        ins=[cc_in.ap()], outs=[cc_out.ap()]),
        waits=after(m_agin), inc="cc")
    m_pre = rec.emit("vector", lambda e: e.memset(PRE[:], 0.0), waits=after(m_agin), inc="dv")
    m_pre_hist = [None, None]
    for r in range(NCORES):
        m_prt = rec.dma("sync", PRT[:, (r % 2) * NHEAD * 65:(r % 2 + 1) * NHEAD * 65],
                        cc_out.ap()[r * 128:(r + 1) * 128, :],
                        waits=after(m_ag, m_pre_hist[r % 2]), sem="ld_prt%d" % (r % 2))
        m_pre = rec.emit("vector", lambda e, r=r: e.scalar_tensor_tensor(
            out=PRE[:], in0=PRT[:, (r % 2) * NHEAD * 65:(r % 2 + 1) * NHEAD * 65],
            scalar=SEL[:, r:r + 1], in1=PRE[:], op0=ALU.mult, op1=ALU.add),
            waits=after(m_prt, m_pre, m_sel), inc="dv")
        m_pre_hist[r % 2] = m_pre
    m_preb = rec.emit("vector", lambda e: e.tensor_copy(out=PREB[:], in_=PRE[:]),
                      waits=after(m_pre), inc="dv")
    if stage <= 3:
        ms = rec.dma("sync", xo.rearrange("(c p) t -> p c t", p=128)[:, 0, 0:NHEAD * 65],
                     PRE[:], waits=after(m_pre), sem="st")
        return [ms]

    m_ostore = []
    m_store_prev = None
    for p in range(NCH):
        m_kp = proj_pair(p, WK, m_wk, KP, m_pair_use)
        m_qp = proj_pair(p, WQ, m_wq, QP, m_kp)
        m_va = pair_features(p, m_kp, m_pair_use)
        if stage == 33:
            m_ostore.append(m_va)
            m_pair_use = m_va
            continue
        for hh in range(2):
            h = 2 * p + hh
            m_qf = head_fm_feats(h, QP, m_qp, KQT, QF, 5,
                                 m_ostore[-1] if m_ostore else None)
            m_kff = head_fm_feats(h, KP, m_kp, KKT, KFF, 6,
                                  m_ostore[-1] if m_ostore else None)
            if stage == 34:
                m_ostore.append(m_kff)
                continue
            m_init = rec.emit("tensor", lambda e, h=h: e.matmul(
                PS[:, 0:65], IDT[:], PREB[:, h * 65:(h + 1) * 65],
                start=True, stop=False),
                waits=after(m_preb, m_idt, m_va, m_qf, m_kff,
                            m_ostore[-1] if m_ostore else None), inc="ts")
            m_state = m_init
            m_sev = None
            m_pt = None
            m_ost = None
            for tb in range(NTB):
                if stage >= 37:
                    m_sev = rec.emit("scalar", lambda e: e.activation(
                        out=STSB[:], in_=PS[:, 0:65], func=AF.Copy),
                        waits=after(m_state, m_ost), inc="ac")
                m_smm = rec.emit("tensor", lambda e, tb=tb: e.matmul(
                    PS[:, 2 * 512:2 * 512 + 128], KFF[:, tb * 128:(tb + 1) * 128],
                    QF[:, tb * 128:(tb + 1) * 128], start=True, stop=True),
                    waits=after(m_pt), inc="ts")
                m_pt = rec.emit("vector", lambda e: e.tensor_tensor(
                    out=PT[:], in0=PS[:, 2 * 512:2 * 512 + 128], in1=TRI[:], op=ALU.mult),
                    waits=after(m_smm, m_tri), inc="dv")
                if stage >= 38:
                    m_o1 = rec.emit("tensor", lambda e, tb=tb, hh=hh: e.matmul(
                        PS[:, 1 * 512:1 * 512 + 65], PT[:],
                        VAP[:, tb * 130 + hh * 65:tb * 130 + hh * 65 + 65],
                        start=True, stop=False),
                        waits=after(m_pt, m_ost), inc=None)
                    m_o2 = rec.emit("tensor", lambda e, tb=tb: e.matmul(
                        PS[:, 1 * 512:1 * 512 + 65], QF[:, tb * 128:(tb + 1) * 128], STSB[:],
                        start=False, stop=True),
                        waits=after(m_sev), inc="ts")
                m_state = rec.emit("tensor", lambda e, tb=tb, hh=hh: e.matmul(
                    PS[:, 0:65], KFTP[:, tb * 2 * F2 + hh * F2:tb * 2 * F2 + (hh + 1) * F2],
                    VAP[:, tb * 130 + hh * 65:tb * 130 + hh * 65 + 65],
                    start=False, stop=(tb == NTB - 1)),
                    waits=after(m_sev), inc="ts")
                if stage >= 38:
                    m_z1 = rec.emit("vector", lambda e: e.tensor_scalar(
                        out=ZI[:, 0:1], in0=PS[:, 1 * 512 + 64:1 * 512 + 65], scalar1=1e-30,
                        scalar2=None, op0=ALU.add), waits=after(m_o2), inc="dv")
                    m_z2 = rec.emit("vector", lambda e: e.reciprocal(
                        out=ZI[:, 1:2], in_=ZI[:, 0:1]), waits=after(m_z1), inc="dv")
                    m_ost = rec.emit("vector", lambda e, tb=tb, hh=hh: e.tensor_scalar(
                        out=OST[:, tb * 128 + hh * 64:tb * 128 + hh * 64 + 64],
                        in0=PS[:, 1 * 512:1 * 512 + 64],
                        scalar1=ZI[:, 1:2], scalar2=None, op0=ALU.mult),
                        waits=after(m_z2), inc="dv")
            m_ostore.append(m_ost if m_ost is not None else m_state)
        if stage not in (35, 36, 37):
            m_tc = None
            for tb in range(NTB):
                mtr = rec.emit("tensor", lambda e, tb=tb: e.transpose(
                    PS[:, 2 * 512:2 * 512 + 128], OST[:, tb * 128:(tb + 1) * 128],
                    IDTF[:]),
                    waits=after(m_ostore[-1] if tb == 0 else None,
                                m_idtf if tb == 0 else None, m_tc), inc="ts")
                m_tc = rec.emit("vector", lambda e, tb=tb: e.tensor_copy(
                    out=OTB[:, tb * 128:(tb + 1) * 128], in_=PS[:, 2 * 512:2 * 512 + 128]),
                    waits=after(mtr, m_store_prev if tb == 0 else None), inc="dv")
            if stage == 39:
                m_ostore.append(m_tc)
            else:
                m_store_prev = rec.dma("gpsimd", o_dram.ap()[p * 128:(p + 1) * 128, :], OTB[:],
                                       waits=after(m_tc), sem="st_o%d" % p)
                m_ostore.append(m_tc)
        m_pair_use = m_ostore[-1]
    stws = [("st_o%d" % p, 16) for p in range(NCH) if ("st_o%d" % p) in rec.counts]
    if stage <= 4 or stage in (33, 34, 35, 36, 37, 39):
        ms = rec.dma("gpsimd", xo.rearrange("(c p) t -> p c t", p=128)[:, 0:1, 0:NTB * 128],
                     OST[:, 0:NTB * 128], waits=after(m_ostore[-1], *stws), sem="st")
        return [ms]

    if stage == 48:
        ms = rec.dma("gpsimd", xo, o_dram.ap(), waits=after(m_ostore[-1], *stws), sem="st")
        return [ms]
    m_wo = rec.dma("gpsimd", WX[:], wo.rearrange("(k p) m -> p k m", p=128),
                   waits=after(m_v_done, m_pair_use), sem="ld_wo")
    m_store = None
    m_resid_prev = [None] * NCH
    for w in range(NW if stage != 52 else 1):
        m_xw2 = rec.dma("sync", XWIN[:],
                        x_in.rearrange("(c p) t -> p c t", p=128)[:, :, w * WIN:(w + 1) * WIN],
                        waits=after(m_ln if w == 0 else None,
                                    m_pre if w == 0 else None,
                                    ("st%d" % ((w - 1) % 2), rec.counts.get("st%d" % ((w - 1) % 2), 0)) if w > 0 else None,
                                    *(m_resid_prev if w > 0 else [])),
                        sem="ld_xw2")
        m_pan = []
        for k in range(NCH):
            mp = rec.dma("sync", OFP[:, k * WIN:(k + 1) * WIN],
                         o_dram.ap()[k * 128:(k + 1) * 128, w * WIN:(w + 1) * WIN],
                         waits=after(*(stws if w == 0 and k == 0 else []),
                                     m_ln if w == 0 and k == 0 else None,
                                     *(m_resid_prev if w > 0 and k == 0 else [])),
                         sem="ld_ofp%d" % (w % 2))
            m_pan.append(mp)
        if stage == 49 and w == 0:
            ms1 = rec.dma("gpsimd", xo.rearrange("(c p) t -> p c t", p=128)[:, :, 0:WIN],
                          OFP[:], waits=after(m_pan[-1]), sem="st0")
            ms2 = rec.dma("sync", xo.rearrange("(c p) t -> p c t", p=128)[:, :, WIN:2 * WIN],
                          XWIN[:], waits=after(m_xw2), sem="st0")
            return [ms1, ms2]
        for m in range(NCH):
            bk = m % 2
            mm = None
            for k in range(NCH):
                mm = rec.emit("tensor", lambda e, m=m, k=k, bk=bk: e.matmul(
                    bank(bk), WX[:, k * D + m * 128:k * D + (m + 1) * 128],
                    OFP[:, k * WIN:(k + 1) * WIN], start=(k == 0), stop=(k == NCH - 1)),
                    waits=after(m_pan[-1] if m == 0 and k == 0 else None,
                                m_wo if m == 0 and k == 0 else None,
                                m_resid_prev[m - 2] if k == 0 and m >= 2 else None),
                    inc="ts" if k == NCH - 1 else None)
            if stage == 50:
                m_res = rec.emit("vector", lambda e, m=m, bk=bk: e.tensor_copy(
                    out=XWIN[:, m * WIN:(m + 1) * WIN], in_=bank(bk)),
                    waits=after(mm, m_xw2), inc="dv")
            else:
                m_res = rec.emit("vector", lambda e, m=m, bk=bk: e.scalar_tensor_tensor(
                    out=XWIN[:, m * WIN:(m + 1) * WIN], in0=bank(bk), scalar=1.0,
                    in1=XWIN[:, m * WIN:(m + 1) * WIN], op0=ALU.mult, op1=ALU.add),
                    waits=after(mm, m_xw2), inc="dv")
            m_resid_prev[m] = m_res
            m_store = rec.dma("sync", xo.rearrange("(c p) t -> p c t", p=128)[:, m, w * WIN:(w + 1) * WIN],
                              XWIN[:, m * WIN:(m + 1) * WIN], waits=after(m_res), sem="st%d" % (w % 2))
    return [(s, rec.counts[s]) for s in ("st0", "st1") if s in rec.counts]


def build_s2(nc, stage=100):
    rec = Rec(nc)
    ctx = Ctx(nc, rec)
    x_in = nc.dram_tensor("x", [D, T], F32, kind="ExternalInput").ap()
    wq = nc.dram_tensor("wq", [D, D], F32, kind="ExternalInput").ap()
    wk = nc.dram_tensor("wk", [D, D], F32, kind="ExternalInput").ap()
    wv = nc.dram_tensor("wv", [D, D], F32, kind="ExternalInput").ap()
    wo = nc.dram_tensor("wo", [D, D], F32, kind="ExternalInput").ap()
    kqp = nc.dram_tensor("kqp", [NHEAD * 128, 128], F32, kind="ExternalInput").ap()
    kkp = nc.dram_tensor("kkp", [NHEAD * 128, 128], F32, kind="ExternalInput").ap()
    sel = nc.dram_tensor("sel", [128, NCORES], F32, kind="ExternalInput").ap()
    onc = nc.dram_tensor("ones_col", [128, 1], F32, kind="ExternalInput").ap()
    onr = nc.dram_tensor("ones_row", [1, 128], F32, kind="ExternalInput").ap()
    tri = nc.dram_tensor("tri", [128, 128], F32, kind="ExternalInput").ap()
    ident = nc.dram_tensor("ident", [128, 128], F32, kind="ExternalInput").ap()
    xo = nc.dram_tensor("xo", [D, T], F32, kind="ExternalOutput").ap()
    mil = emit_s2(ctx, x_in, xo, wq, wk, wv, wo, kqp, kkp, sel, onc, onr, tri, ident, stage=stage)
    rec.finalize(mil)
    return nc


def s2_host_inputs(Kq, Kk):
    kqp = np.zeros((NHEAD * 128, 128), np.float32)
    kkp = np.zeros((NHEAD * 128, 128), np.float32)
    for h in range(NHEAD):
        r0 = h * 128 + (h % 2) * 64
        kqp[r0:r0 + 64, 0:64] = Kq[h]
        kkp[r0:r0 + 64, 0:64] = Kk[h]
    tri = np.triu(np.ones((128, 128), np.float32))
    return kqp, kkp, tri


def emit_s5(ctx, x_in, xo, pewT, qkvwT, owT, taps, onc, onr, sel2, oacc, ozacc):
    nc, rec = ctx.nc, ctx.rec
    TXX = T + 2
    NG = 8
    cc_k = nc.dram_tensor("cc_k", [D, T], BF16)
    cc_v = nc.dram_tensor("cc_v", [T, D], BF16)
    ko = nc.dram_tensor("ko", [4 * D, T], BF16)
    vo = nc.dram_tensor("vo", [4 * T, D], BF16)

    XWIN = ctx.sb("XWIN", [128, NCH * WIN], F32)
    BIG = ctx.sb("BIG", [128, 33000], BF16)
    XHo = 0
    XS2o = 16400
    KBo = 0
    VBo = 8192
    PTo = 16400
    OFMo = 0
    OWo = 16400
    XSW = ctx.sb("XSW", [128, NCH * WIN], BF16)
    Q = ctx.sb("Q", [128, NCH * T], BF16)
    PEW = ctx.sb("PEW", [128, NCH * D], BF16)
    WSL = ctx.sb("WSL", [128, 2 * NCH * 128], BF16)
    KEV = ctx.sb("KEV", [128, 2 * WIN], BF16)
    VEV = ctx.sb("VEV", [128, 2 * WIN], BF16)
    OLOAD = ctx.sb("OLOAD", [128, T], F32)
    ZROW = ctx.sb("ZROW", [2, 2 * T], F32)
    ZRB = ctx.sb("ZRB", [2, 2 * T], BF16)
    OEV = ctx.sb("OEV", [128, WIN], F32)
    ZEV = ctx.sb("ZEV", [1, WIN], F32)
    TAPS = ctx.sb("TAPS", [128, 3 * NCH], F32)
    SEL2 = ctx.sb("SEL2", [2, 128], BF16)
    CTMP = ctx.sb("CTMP", [128, WIN], F32)
    XB = ctx.sb("XB", [128, NCH * WIN], BF16)
    XSQ = ctx.sb("XSQ", [128, WIN], BF16)
    XAF = ctx.sb("XAF", [128, WIN], F32)
    SROW = ctx.sb("SROW", [1, 4 * WIN + 16], F32)
    SRB = ctx.sb("SRB", [1, 4 * WIN + 16], BF16)
    ONC = ctx.sb("ONC", [128, 1], BF16)
    ONR = ctx.sb("ONR", [1, 128], BF16)
    PS = ctx.ps("PS", [128, 4096], F32)
    bank = lambda b, wdt=512: PS[:, b * 512:b * 512 + wdt]

    m_pew = rec.dma("gpsimd", PEW[:], pewT.rearrange("(k p) m -> p k m", p=128), sem="ld_pe")
    m_taps = rec.dma("sync", TAPS[:], taps, sem="ld_taps")
    m_onc = rec.dma("gpsimd", ONC[:], onc, sem="ld_onc")
    m_onr = rec.dma("gpsimd", ONR[:], onr, sem="ld_onr")
    m_sel2 = rec.dma("gpsimd", SEL2[:], sel2, sem="ld_sel2")

    m_ln = None
    xr_ = x_in.rearrange("(c p) t -> p c t", p=128)
    for w in range(NW):
        m_xw = rec.dma("sync", XWIN[:], xr_[:, :, 1 + w * WIN:1 + (w + 1) * WIN],
                       waits=after(m_ln), sem="ld_xw")
        m_ln = emit_ln(ctx, lambda c, a, b: XWIN[:, c * WIN + a:c * WIN + b],
                       lambda c, a, b, w=w: BIG[:, c * TXX + 1 + w * WIN + a:c * TXX + 1 + w * WIN + b],
                       WIN, lambda c, m_xw=m_xw: [m_xw], m_onc, m_onr)
    m_xb = rec.dma("sync", XWIN[:, 0:NCH], xr_[:, :, 0:1], waits=after(m_ln), sem="ld_xb",
                   allow_slow_non_contiguous=True)
    m_ln = emit_ln(ctx, lambda c, a, b: XWIN[:, c + a:c + b],
                   lambda c, a, b: BIG[:, c * TXX + a:c * TXX + b],
                   1, lambda c, m_xb=m_xb, m_ln=m_ln: [m_xb, m_ln], m_onc, m_onr)
    m_xb2 = rec.dma("sync", XWIN[:, NCH:2 * NCH], xr_[:, :, T + 1:T + 2],
                    waits=after(m_ln), sem="ld_xb", allow_slow_non_contiguous=True)
    m_ln = emit_ln(ctx, lambda c, a, b: XWIN[:, NCH + c + a:NCH + c + b],
                   lambda c, a, b: BIG[:, c * TXX + T + 1 + a:c * TXX + T + 1 + b],
                   1, lambda c, m_xb2=m_xb2, m_ln=m_ln: [m_xb2, m_ln], m_onc, m_onr)

    m_xs2 = None
    m_ev = {}
    m_xs = None
    for w in range(NW):
        for c in range(NCH):
            c0 = c * TXX + 1 + w * WIN
            m1 = rec.emit("vector", lambda e, c=c, c0=c0: e.tensor_scalar(
                out=CTMP[:], in0=BIG[:, c0 - 1:c0 - 1 + WIN],
                scalar1=TAPS[:, 3 * c:3 * c + 1], scalar2=None, op0=ALU.mult),
                waits=after(m_ln if w == 0 and c == 0 else None,
                            m_taps if w == 0 and c == 0 else None,
                            m_xs,
                            m_ev.get((w - 1, NCH - 1)) if c == 0 else None), inc="dv")
            m2 = rec.emit("vector", lambda e, c=c, c0=c0: e.scalar_tensor_tensor(
                out=CTMP[:], in0=BIG[:, c0:c0 + WIN],
                scalar=TAPS[:, 3 * c + 1:3 * c + 2], in1=CTMP[:],
                op0=ALU.mult, op1=ALU.add), waits=after(m1), inc="dv")
            m_xs = rec.emit("vector", lambda e, c=c, c0=c0: e.scalar_tensor_tensor(
                out=XSW[:, c * WIN:(c + 1) * WIN], in0=BIG[:, c0 + 1:c0 + 1 + WIN],
                scalar=TAPS[:, 3 * c + 2:3 * c + 3], in1=CTMP[:],
                op0=ALU.mult, op1=ALU.add), waits=after(m2), inc="dv")
        for m in range(NCH):
            bk = m % 2
            mm = None
            for k in range(NCH):
                mm = rec.emit("tensor", lambda e, m=m, k=k, bk=bk: e.matmul(
                    bank(bk), PEW[:, k * D + m * 128:k * D + (m + 1) * 128],
                    XSW[:, k * WIN:(k + 1) * WIN],
                    start=(k == 0), stop=(k == NCH - 1)),
                    waits=after(m_xs if m == 0 and k == 0 else None,
                                m_pew if w == 0 and m == 0 and k == 0 else None,
                                (m_ev.get((w, m - 2)) or m_ev.get((w - 1, m + 6))) if k == 0 else None),
                    inc="ts" if k == NCH - 1 else None)
            m_xs2 = rec.emit("vector", lambda e, w=w, m=m, bk=bk: e.scalar_tensor_tensor(
                out=BIG[:, XS2o + m * T + w * WIN:XS2o + m * T + (w + 1) * WIN],
                in0=bank(bk), scalar=1.0, in1=XSW[:, m * WIN:(m + 1) * WIN],
                op0=ALU.mult, op1=ALU.add), waits=after(mm), inc="dv")
            m_ev[(w, m)] = m_xs2

    XS2s = lambda k, w: BIG[:, XS2o + k * T + w * WIN:XS2o + k * T + (w + 1) * WIN]
    m_wslice = [None, None]
    m_evq = {}
    m_kst = [None, None]
    for m in range(16):
        db = m % 2
        m_wslice[db] = rec.dma("gpsimd", WSL[:, db * NCH * 128:(db + 1) * NCH * 128],
                               qkvwT.rearrange("(k p) j -> p k j", p=128)[:, :, m * 128:(m + 1) * 128],
                               waits=after(m_evq.get(m - 2)), sem="ld_wsl%d" % db)
        mev = None
        for w in range(NW):
            bk = 2 + (w % 2)
            mm = None
            for k in range(NCH):
                mm = rec.emit("tensor", lambda e, m=m, w=w, k=k, bk=bk, db=db: e.matmul(
                    bank(bk), WSL[:, db * NCH * 128 + k * 128:db * NCH * 128 + (k + 1) * 128],
                    XS2s(k, w), start=(k == 0), stop=(k == NCH - 1)),
                    waits=after(m_wslice[db] if w == 0 and k == 0 else None,
                                m_xs2 if m == 0 and w == 0 and k == 0 else None,
                                (m_evq.get((m, w - 2)) if w >= 2 else m_evq.get((m - 1, w + 2))) if k == 0 else None),
                    inc="ts" if k == NCH - 1 else None)
            if m < 8:
                mev = rec.emit("vector", lambda e, m=m, w=w, bk=bk: e.tensor_copy(
                    out=Q[:, m * T + w * WIN:m * T + (w + 1) * WIN], in_=bank(bk)),
                    waits=after(mm), inc="dv")
            else:
                mev = rec.emit("vector", lambda e, w=w, bk=bk: e.tensor_copy(
                    out=KEV[:, (w % 2) * WIN:(w % 2) * WIN + WIN], in_=bank(bk)),
                    waits=after(mm, m_kst[w % 2]), inc="dv")
                m_kst[w % 2] = rec.dma("sync", cc_k.ap()[(m - 8) * 128:(m - 7) * 128, w * WIN:(w + 1) * WIN],
                                       KEV[:, (w % 2) * WIN:(w % 2) * WIN + WIN],
                                       waits=after(mev), sem="st_k%d" % (w % 2))
            m_evq[(m, w)] = mev
        m_evq[m] = mev
    WV = PEW
    m_wv = rec.dma("gpsimd", WV[:], qkvwT.rearrange("(k p) j -> p k j", p=128)[:, :, 2048:3072],
                   waits=after(m_ev[(NW - 1, NCH - 1)]), sem="ld_wv")
    m_vst = [None, None]
    m_vev = {}
    for tb in range(NTB):
        for half in range(2):
            bk = 6 + half
            mm = None
            for k in range(NCH):
                mm = rec.emit("tensor", lambda e, tb=tb, half=half, k=k, bk=bk: e.matmul(
                    bank(bk), BIG[:, XS2o + k * T + tb * 128:XS2o + k * T + (tb + 1) * 128],
                    WV[:, k * D + half * 512:k * D + (half + 1) * 512],
                    start=(k == 0), stop=(k == NCH - 1)),
                    waits=after(m_wv if tb == 0 and half == 0 and k == 0 else None,
                                m_vev.get((tb - 1, half)) if k == 0 else None),
                    inc="ts" if k == NCH - 1 else None)
            mev = rec.emit("vector", lambda e, tb=tb, half=half, bk=bk: e.tensor_copy(
                out=VEV[:, half * WIN:half * WIN + WIN], in_=bank(bk)),
                waits=after(mm, m_vst[half]), inc="dv")
            m_vev[(tb, half)] = mev
            m_vst[half] = rec.dma("sync", cc_v.ap()[tb * 128:(tb + 1) * 128, half * 512:(half + 1) * 512],
                                  VEV[:, half * WIN:half * WIN + WIN],
                                  waits=after(mev), sem="st_v%d" % half)

    rec.emit("gpsimd", lambda e: e.collective_compute(
        "AllGather", ALU.bypass, replica_groups=[[0, 1, 2, 3], [4, 5, 6, 7]],
        ins=[cc_k.ap()], outs=[ko.ap()]),
        waits=after(("st_k0", rec.counts["st_k0"]), ("st_k1", rec.counts["st_k1"]),
                    ("st_v0", rec.counts["st_v0"]), ("st_v1", rec.counts["st_v1"])),
        inc="cc")
    m_ag = rec.emit("gpsimd", lambda e: e.collective_compute(
        "AllGather", ALU.bypass, replica_groups=[[0, 1, 2, 3], [4, 5, 6, 7]],
        ins=[cc_v.ap()], outs=[vo.ap()]),
        waits=after(("cc", rec.counts["cc"] + 1)) if False else (), inc="cc")
    m_ag = ("cc", rec.counts["cc"])

    m_kb = None
    m_vb = None
    m_oacc_gate = None
    m_accum_prev = None
    m_last_exp = [None, None]
    m_oev = None
    m_zev = None
    for g in range(NG):
        r, lo = g // 2, (g % 2) * 1024
        m_kb = rec.dma("sync", BIG[:, KBo:KBo + 8192],
                       ko.ap()[r * D:(r + 1) * D, lo:lo + 1024].rearrange("(c p) t -> p c t", p=128),
                       waits=after(m_ag if g == 0 else None, m_ln if g == 0 else None,
                                   m_last_exp[0] if g > 0 else None, m_last_exp[1] if g > 0 else None,
                                   m_oev if g > 0 else None),
                       sem="ld_kb%d" % (g % 2))
        m_vb = rec.dma("sync", BIG[:, VBo:VBo + 8192],
                       vo.ap()[g * 1024:(g + 1) * 1024, :].rearrange("(b p) j -> p b j", p=128),
                       waits=after(m_ag if g == 0 else None,
                                   m_oev if g > 0 else None),
                       sem="ld_vb%d" % (g % 2))
        m_ga_prev = m_oacc_gate
        for h in range(NHEAD):
            hb = (h % 2) * 64
            ch = h // 2
            for qw in range(NW):
                m_omm = None
                m_zmm = None
                for kb in range(8):
                    sbk = kb % 2
                    m_smm = rec.emit("tensor", lambda e, h=h, qw=qw, kb=kb, sbk=sbk, hb=hb, ch=ch: e.matmul(
                        bank(sbk), BIG[hb:hb + 64, KBo + ch * 1024 + kb * 128:KBo + ch * 1024 + (kb + 1) * 128],
                        Q[hb:hb + 64, ch * T + qw * WIN:ch * T + (qw + 1) * WIN],
                        start=True, stop=True),
                        waits=after(m_kb if h == 0 and qw == 0 and kb == 0 else None,
                                    m_last_exp[sbk]), inc="ts")
                    m_exp = rec.emit("scalar", lambda e, kb=kb, sbk=sbk: e.activation(
                        out=BIG[:, PTo + sbk * 512:PTo + (sbk + 1) * 512], in_=bank(sbk),
                        func=AF.Exp, scale=0.125),
                        waits=after(m_smm, m_omm if kb >= 2 else None,
                                    m_zmm if kb >= 2 else None), inc="ac")
                    m_last_exp[sbk] = m_exp
                    m_omm = rec.emit("tensor", lambda e, h=h, kb=kb, sbk=sbk: e.matmul(
                        PS[0:64, 2 * 512:2 * 512 + 512],
                        BIG[:, VBo + kb * 1024 + h * 64:VBo + kb * 1024 + (h + 1) * 64],
                        BIG[:, PTo + sbk * 512:PTo + (sbk + 1) * 512],
                        start=(kb == 0), stop=(kb == 7)),
                        waits=after(m_exp, m_vb if h == 0 and qw == 0 and kb == 0 else None,
                                    m_oev if kb == 0 else None), inc="ts")
                    m_zmm = rec.emit("tensor", lambda e, kb=kb, sbk=sbk: e.matmul(
                        PS[0:1, 3 * 512:3 * 512 + 512], ONC[:, 0:1],
                        BIG[:, PTo + sbk * 512:PTo + (sbk + 1) * 512],
                        start=(kb == 0), stop=(kb == 7)),
                        waits=after(m_zev if kb == 0 else None,
                                    m_onc if g == 0 and h == 0 and qw == 0 and kb == 0 else None),
                        inc="ts")
                m_oev = rec.emit("vector", lambda e: e.tensor_copy(
                    out=OEV[0:64, :], in_=PS[0:64, 2 * 512:2 * 512 + 512]),
                    waits=after(m_omm, m_accum_prev), inc="dv")
                m_zev = rec.emit("vector", lambda e: e.tensor_copy(
                    out=ZEV[:], in_=PS[0:1, 3 * 512:3 * 512 + 512]),
                    waits=after(m_zmm), inc="dv")
                aop = ALU.add if g > 0 else ALU.bypass
                m_o1 = rec.dma("gpsimd", oacc.ap()[h * 64:(h + 1) * 64, qw * WIN:(qw + 1) * WIN],
                               OEV[0:64, :], waits=after(m_oev, m_ga_prev), sem="st_oa",
                               accum_op=aop)
                m_accum_prev = rec.dma("gpsimd", ozacc.ap()[h:h + 1, qw * WIN:(qw + 1) * WIN],
                                       ZEV[:], waits=after(m_zev), sem="st_oa",
                                       accum_op=aop)
        m_oacc_gate = ("st_oa", rec.counts["st_oa"])
        m_accum_prev = m_oacc_gate

    m_owt = rec.dma("gpsimd", BIG[:, OWo:OWo + 8192].tensor[:, OWo:OWo + 8192] if False else BIG[:, OWo:OWo + 8192],
                    owT.rearrange("(k p) m -> p k m", p=128),
                    waits=after(m_last_exp[0], m_last_exp[1], m_oev), sem="ld_ow")
    m_ofm = None
    m_old = None
    m_zld = None
    m_zi = None
    for pch in range(NCH):
        ha, hc = 2 * pch, 2 * pch + 1
        m_old1 = rec.dma("sync", OLOAD[0:64, :], oacc.ap()[ha * 64:(ha + 1) * 64, :],
                         waits=after(m_oacc_gate, m_ofm), sem="ld_oa%d" % (pch % 2))
        m_old = rec.dma("sync", OLOAD[64:128, :], oacc.ap()[hc * 64:(hc + 1) * 64, :],
                        sem="ld_oa%d" % (pch % 2))
        m_zld = rec.dma("sync", ZROW[:, 0:T], ozacc.ap()[ha:ha + 2, :],
                        waits=after(m_zld, m_oacc_gate if pch == 0 else None,
                                    m_zi if pch >= 1 else None), sem="ld_oz%d" % (pch % 2))
        m_zi = rec.emit("vector", lambda e: e.reciprocal(
            out=ZROW[:, T:2 * T], in_=ZROW[:, 0:T]), waits=after(m_zld), inc="dv")
        m_hl = emit_hilo(ctx, ZROW[:, T:2 * T], ZRB[:, 0:T], ZRB[:, T:2 * T], m_zi)
        for w in range(NW):
            rec.emit("tensor", lambda e, w=w: e.matmul(
                bank(4), SEL2[:], ZRB[:, w * WIN:(w + 1) * WIN], start=True, stop=False),
                waits=after(m_hl, m_sel2 if pch == 0 and w == 0 else None,
                            m_ofm), inc=None)
            m_zb = rec.emit("tensor", lambda e, w=w: e.matmul(
                bank(4), SEL2[:], ZRB[:, T + w * WIN:T + (w + 1) * WIN], start=False, stop=True),
                inc="ts")
            m_ofm = rec.emit("vector", lambda e, pch=pch, w=w: e.tensor_tensor(
                out=BIG[:, OFMo + pch * T + w * WIN:OFMo + pch * T + (w + 1) * WIN],
                in0=OLOAD[:, w * WIN:(w + 1) * WIN], in1=bank(4), op=ALU.mult),
                waits=after(m_zb, m_old), inc="dv")
    m_store = None
    m_resid_prev = [None] * NCH
    for w in range(NW):
        m_xw2 = rec.dma("sync", XWIN[:], xr_[:, :, 1 + w * WIN:1 + (w + 1) * WIN],
                        waits=after(m_ln if w == 0 else None,
                                    ("st%d" % ((w - 1) % 2), rec.counts.get("st%d" % ((w - 1) % 2), 0)) if w > 0 else None,
                                    *(m_resid_prev if w > 0 else [])),
                        sem="ld_xw2")
        for m in range(NCH):
            bk = m % 2
            mm = None
            for k in range(NCH):
                mm = rec.emit("tensor", lambda e, m=m, k=k, bk=bk, w=w: e.matmul(
                    bank(bk), BIG[:, OWo + k * D + m * 128:OWo + k * D + (m + 1) * 128],
                    BIG[:, OFMo + k * T + w * WIN:OFMo + k * T + (w + 1) * WIN],
                    start=(k == 0), stop=(k == NCH - 1)),
                    waits=after(m_ofm if w == 0 and m == 0 and k == 0 else None,
                                m_owt if w == 0 and m == 0 and k == 0 else None,
                                m_resid_prev[m - 2] if k == 0 else None),
                    inc="ts" if k == NCH - 1 else None)
            m_res = rec.emit("vector", lambda e, m=m, bk=bk: e.scalar_tensor_tensor(
                out=XWIN[:, m * WIN:(m + 1) * WIN], in0=bank(bk), scalar=1.0,
                in1=XWIN[:, m * WIN:(m + 1) * WIN], op0=ALU.mult, op1=ALU.add),
                waits=after(mm, m_xw2), inc="dv")
            m_resid_prev[m] = m_res
            m_store = rec.dma("sync", xo.rearrange("(c p) t -> p c t", p=128)[:, m, w * WIN:(w + 1) * WIN],
                              XWIN[:, m * WIN:(m + 1) * WIN], waits=after(m_res), sem="st%d" % (w % 2))
    return [(s, rec.counts[s]) for s in ("st0", "st1") if s in rec.counts]


def build_s5(nc):
    rec = Rec(nc)
    ctx = Ctx(nc, rec)
    x_in = nc.dram_tensor("x", [D, T + 2], F32, kind="ExternalInput").ap()
    pewT = nc.dram_tensor("pewT", [D, D], F32, kind="ExternalInput").ap()
    qkvwT = nc.dram_tensor("qkvwT", [D, 3 * D], F32, kind="ExternalInput").ap()
    owT = nc.dram_tensor("owT", [D, D], F32, kind="ExternalInput").ap()
    taps = nc.dram_tensor("taps", [128, 3 * NCH], F32, kind="ExternalInput").ap()
    onc = nc.dram_tensor("ones_col", [128, 1], F32, kind="ExternalInput").ap()
    onr = nc.dram_tensor("ones_row", [1, 128], F32, kind="ExternalInput").ap()
    sel2 = nc.dram_tensor("sel2", [2, 128], F32, kind="ExternalInput").ap()
    xo = nc.dram_tensor("xo", [D, T], F32, kind="ExternalOutput").ap()
    oacc = nc.dram_tensor("oacc", [NHEAD * 64, T], F32, kind="ExternalOutput")
    ozacc = nc.dram_tensor("ozacc", [NHEAD, T], F32, kind="ExternalOutput")
    mil = emit_s5(ctx, x_in, xo, pewT, qkvwT, owT, taps, onc, onr, sel2, oacc, ozacc)
    rec.finalize(mil)
    return nc


def run_s5(x_full, conv_w, pe_w, qkv_w, out_w):
    nc = build_s5(bass.Bass("TRN2", target_bir_lowering=False,
                            dynamic_dma_scratch_size=16384))
    flat = np.ascontiguousarray(np.asarray(x_full).reshape(2 * 8192, D))
    taps_h = np.ascontiguousarray(np.asarray(conv_w)[:, 0, :, 1])
    taps_t = np.zeros((128, 3 * NCH), np.float32)
    for c in range(NCH):
        taps_t[:, 3 * c:3 * (c + 1)] = taps_h[c * 128:(c + 1) * 128]
    sel2v = np.zeros((2, 128), np.float32)
    sel2v[0, 0:64] = 1.0
    sel2v[1, 64:128] = 1.0
    in_maps = []
    for c in range(NCORES):
        s, e = c * T, (c + 1) * T
        prev = flat[s - 1:s].T if s % 8192 else np.zeros((D, 1), np.float32)
        nxt = flat[e:e + 1].T if e % 8192 else np.zeros((D, 1), np.float32)
        xslab = np.concatenate([prev, flat[s:e].T, nxt], axis=1).astype(np.float32)
        in_maps.append({
            "x": np.ascontiguousarray(xslab),
            "pewT": np.ascontiguousarray(np.asarray(pe_w).T),
            "qkvwT": np.ascontiguousarray(np.asarray(qkv_w).T),
            "owT": np.ascontiguousarray(np.asarray(out_w).T),
            "taps": taps_t, "sel2": sel2v,
            "ones_col": np.ones((128, 1), np.float32),
            "ones_row": np.ones((1, 128), np.float32),
        })
    res = run_bass_kernel_spmd(nc, in_maps, core_ids=list(range(NCORES)))
    out = np.concatenate([res.results[c]["xo"].T for c in range(NCORES)], axis=0)
    return out.reshape(2, 8192, D)


NX1 = 3584
HALO1 = 1536
NW1 = 7
NPAIR = 22
NK2 = 22
TX1 = 513


def emit_s1(ctx, x_in, x1d, xo, wqkvf, w1f, w2f, woT, b1r, b2r,
            tqa, tqb, tka, tkb, qscol, kscol, gcol, bcol,
            perm, hsel, sel2, ident, mband, hv, onc, onr, onr5,
            layers=2, nwin_limit=None):
    nc, rec = ctx.nc, ctx.rec
    TX = TX1

    XW = ctx.sb("XW", [128, NCH * TX], F32)
    XSB = ctx.sb("XSB", [128, NCH * TX], BF16)
    QP = ctx.sb("QP", [128, 4 * WIN], BF16)
    KP = ctx.sb("KP", [128, 4 * 2 * WIN], BF16)
    VP = ctx.sb("VP", [128, 8 * WIN], BF16)
    SCR = ctx.sb("SCR", [128, NPAIR * WIN], BF16)
    XLN = ctx.sb("XLN", [128, NCH * WIN], BF16)
    OFMW = ctx.sb("OFMW", [128, 4 * WIN], BF16)
    SM = ctx.sb("SM", [128, 640], F32)
    PTM = ctx.sb("PTM", [128, 640], BF16)
    PTN = ctx.sb("PTN", [128, 640], BF16)
    PTT = ctx.sb("PTT", [128, 640], BF16)
    TBQA = ctx.sb("TBQA", [128, WIN], BF16)
    TBQB = ctx.sb("TBQB", [128, WIN], BF16)
    TBKA = ctx.sb("TBKA", [128, WIN], BF16)
    TBKB = ctx.sb("TBKB", [128, WIN], BF16)
    QRAW = ctx.sb("QRAW", [128, WIN], BF16)
    QT1 = ctx.sb("QT1", [128, WIN], F32)
    T1Q = ctx.sb("T1Q", [128, WIN], BF16)
    RT = ctx.sb("RT", [128, WIN], BF16)
    QA = ctx.sb("QA", [128, WIN], F32)
    XAF = ctx.sb("XAF", [128, WIN], F32)
    XAF2 = ctx.sb("XAF2", [128, WIN], F32)
    XB = ctx.sb("XB", [128, NCH * WIN], BF16)
    XSQ = ctx.sb("XSQ", [128, WIN], BF16)
    SROW = ctx.sb("SROW", [1, 4 * WIN + 16], F32)
    SRB = ctx.sb("SRB", [1, 4 * WIN + 16], BF16)
    WSLQ = ctx.sb("WSLQ", [128, 2 * 1024], BF16)
    WQV = ctx.sb("WQV", [128, NCH * WIN], BF16)
    W1SL = ctx.sb("W1SL", [128, 2 * 2048], BF16)
    W2SL = ctx.sb("W2SL", [128, 2 * 2816], BF16)
    WOS = ctx.sb("WOS", [128, 4 * D], BF16)
    B1 = ctx.sb("B1", [1, 5460], BF16)
    B2 = ctx.sb("B2", [1, D], BF16)
    RROW = ctx.sb("RROW", [2, 3 * WIN], F32)
    RRB = ctx.sb("RRB", [2, 2 * WIN], BF16)
    RM = ctx.sb("RM", [128, 4], F32)
    PERM = ctx.sb("PERM", [128, 128], BF16)
    HSEL = ctx.sb("HSEL", [128, 2], BF16)
    SEL2 = ctx.sb("SEL2", [2, 128], BF16)
    IDT = ctx.sb("IDT", [128, 128], BF16)
    MBIAS = ctx.sb("MBIAS", [128, 640], F32)
    HV = ctx.sb("HV", [128, 36], F32)
    QS = ctx.sb("QS", [128, 4], F32)
    GB = ctx.sb("GB", [128, 4 * NCH], F32)
    XMCOL = ctx.sb("XMCOL", [128, 16], F32)
    XQCOL = ctx.sb("XQCOL", [128, 8], BF16)
    XQF = ctx.sb("XQF", [128, 8], F32)
    ONC = ctx.sb("ONC", [128, 1], BF16)
    ONR = ctx.sb("ONR", [1, 128], BF16)
    ONR5 = ctx.sb("ONR5", [1, WIN], BF16)
    PS = ctx.ps("PS", [128, 4096], F32)
    bank = lambda b, wdt=512: PS[:, b * 512:b * 512 + wdt]

    x_r = x_in.rearrange("(c p) t -> p c t", p=128)
    x1_r = x1d.ap().rearrange("(c p) t -> p c t", p=128)
    xo_r = xo.rearrange("(c p) t -> p c t", p=128)
    wqkv_r = wqkvf.rearrange("(l k p) m -> p l k m", l=2, p=128)
    w1_r = w1f.rearrange("(l k p) m -> p l k m", l=2, p=128)
    XW_3 = XW.ap().rearrange("p (c t) -> p c t", c=NCH)
    XSB_3 = XSB.ap().rearrange("p (c t) -> p c t", c=NCH)

    for ap_, tgt in ((perm, PERM[:]), (hsel, HSEL[:]), (sel2, SEL2[:]),
                     (ident, IDT[:]), (onc, ONC[:]), (onr, ONR[:]),
                     (onr5, ONR5[:])):
        rec.dma("gpsimd", tgt, ap_, sem="ld_cst")
    for ap_, tgt in ((mband, MBIAS[:]), (hv, HV[:]), (qscol, QS[:, 0:2]),
                     (kscol, QS[:, 2:4]), (gcol, GB[:, 0:2 * NCH]),
                     (bcol, GB[:, 2 * NCH:4 * NCH])):
        mC = rec.dma("sync", tgt, ap_, sem="ld_cst2")
    mCg = ("ld_cst", rec.counts["ld_cst"])
    m_perm = m_hsel = m_sel2 = m_idt = m_onc = m_onr = m_onr5 = mCg
    m_mb = m_hv = m_qs = m_gb = mC
    m_const = [mCg, mC]

    m_store = None
    for w in range(NW1):
        m_xw = rec.dma("sync", XW_3[:, :, 0:WIN],
                       x_r[:, :, 1 + w * WIN:1 + (w + 1) * WIN],
                       waits=after(m_store), sem="ld_xi")
        m_ln = emit_ln(ctx, lambda c, a, b: XW[:, c * TX + a:c * TX + b],
                       lambda c, a, b: XW[:, c * TX + a:c * TX + b],
                       WIN, lambda c, m_xw=m_xw: [m_xw], m_onc, m_onr)
        m_store = rec.dma("sync", x1_r[:, :, 1 + w * WIN:1 + (w + 1) * WIN],
                          XW_3[:, :, 0:WIN], waits=after(m_ln), sem="st_x1")
    m_zc = rec.emit("vector", lambda e: e.memset(XQF[:], 0.0),
                    waits=after(m_store), inc="dv")
    m_store = rec.dma("sync", x1_r[:, :, 0:1], XQF[:], waits=after(m_zc),
                      sem="st_x1", allow_slow_non_contiguous=True)

    env = dict(XW=XW, XSB=XSB, QP=QP, KP=KP, VP=VP, SCR=SCR, XLN=XLN, OFMW=OFMW,
               SM=SM, PTM=PTM, PTN=PTN, PTT=PTT, TBQA=TBQA, TBQB=TBQB, TBKA=TBKA,
               TBKB=TBKB, QRAW=QRAW, QT1=QT1, T1Q=T1Q, RT=RT, QA=QA, XAF=XAF,
               XSQ=XSQ, WSLQ=WSLQ, WQV=WQV, W1SL=W1SL, W2SL=W2SL, WOS=WOS, B1=B1,
               B2=B2, RROW=RROW, RRB=RRB, RM=RM, PERM=PERM, HSEL=HSEL, SEL2=SEL2,
               IDT=IDT, MBIAS=MBIAS, HV=HV, QS=QS, GB=GB, XMCOL=XMCOL,
               XQCOL=XQCOL, ONC=ONC, ONR=ONR, ONR5=ONR5, PS=PS, bank=bank,
               x1_r=x1_r, wqkv_r=wqkv_r, w1_r=w1_r, w2f=w2f, tqa=tqa, tqb=tqb,
               tka=tka, tkb=tkb, XW_3=XW_3, XSB_3=XSB_3, m_onc=m_onc,
               m_onr=m_onr)

    st = {"store": m_store, "attn": None, "tbl": None, "p1dst": None,
          "wqv_use": None, "kvp_done": None, "xmprev": None}

    for l in range(layers):
        t0 = 0 if l == 0 else 512
        nwin = NW1 if l == 0 else NW1 - 1
        if nwin_limit is not None:
            nwin = min(nwin, nwin_limit)
        m_wo = rec.dma("gpsimd", WOS[:],
                       woT.rearrange("(l k p) m -> p l k m", l=2, p=128)[:, l],
                       waits=after(st["store"]), sem="ld_wo")
        m_b1 = rec.dma("gpsimd", B1[:], b1r[l:l + 1, :], waits=after(st["store"]),
                       sem="ld_b")
        m_b2 = rec.dma("gpsimd", B2[:], b2r[l:l + 1, :], sem="ld_b")
        m_xqf = rec.dma("sync", XQF[:], x1_r[:, :, 0:1],
                        waits=after(st["store"]), sem="ld_xq",
                        allow_slow_non_contiguous=True)
        m_xqc = rec.emit("vector", lambda e: e.tensor_copy(out=XQCOL[:], in_=XQF[:]),
                         waits=after(m_xqf), inc="dv")
        m_xm0 = rec.emit("vector", lambda e: e.memset(XMCOL[:, 0:8], 0.0),
                         waits=after(st["store"]), inc="dv")
        m_kv0 = rec.emit("vector", lambda e: e.memset(KP[:], 0.0),
                         waits=after(st["store"]), inc="dv")
        m_kv1 = rec.emit("vector", lambda e: e.memset(VP[:], 0.0), inc="dv")
        st["entry"] = [m_wo, m_b2, m_xqc, m_xm0, m_kv0, m_kv1] + m_const
        st["xmprev"] = m_xm0
        if l == 1:
            emit_s1_win(ctx, l, -1, 0, True, st, env)
        for w in range(nwin):
            emit_s1_win(ctx, l, w, t0, False, st, env)

    X1F = SCR.ap()[:, 0:8192].bitcast(F32)
    m_res_prev = None
    m_st = None
    for w in range(NW):
        m_l1 = rec.dma("sync", XW_3[:, :, 0:WIN],
                       x_r[:, :, 1537 + w * WIN:1537 + (w + 1) * WIN],
                       waits=after(st["store"] if w == 0 else None, m_res_prev, m_st),
                       sem="ld_f%d" % (w % 2))
        m_l2 = rec.dma("sync", X1F.rearrange("p (c t) -> p c t", c=NCH)[:, :, 0:WIN],
                       x1_r[:, :, 1537 + w * WIN:1537 + (w + 1) * WIN],
                       waits=after(st["store"] if w == 0 else None, m_res_prev),
                       sem="ld_f%d" % (w % 2))
        m_res = None
        for c in range(NCH):
            m_res = rec.emit("vector", lambda e, c=c: e.scalar_tensor_tensor(
                out=XW[:, c * TX:c * TX + WIN], in0=X1F[:, c * WIN:(c + 1) * WIN],
                scalar=0.5, in1=XW[:, c * TX:c * TX + WIN], op0=ALU.mult, op1=ALU.add),
                waits=after(m_l2 if c == 0 else None),
                inc="dv")
        m_res_prev = m_res
        m_st = rec.dma("sync", xo_r[:, :, w * WIN:(w + 1) * WIN], XW_3[:, :, 0:WIN],
                       waits=after(m_res), sem="st_o%d" % (w % 2))
    return [("st_o0", rec.counts["st_o0"]), ("st_o1", rec.counts["st_o1"])]


def emit_s1_win(ctx, l, w, t0, kv_only, st, env):
    nc, rec = ctx.nc, ctx.rec
    (XW, XSB, QP, KP, VP, SCR, XLN, OFMW, SM, PTM, PTN, PTT, TBQA, TBQB, TBKA,
     TBKB, QRAW, QT1, T1Q, RT, QA, XAF, XSQ, WSLQ, WQV, W1SL, W2SL, WOS, B1, B2,
     RROW, RRB, RM, PERM, HSEL, SEL2, IDT, MBIAS, HV, QS, GB, XMCOL, XQCOL, ONC,
     ONR, ONR5, PS, bank, x1_r, wqkv_r, w1_r, w2f, tqa, tqb, tka, tkb, XW_3,
     XSB_3, m_onc, m_onr) = (
        env["XW"], env["XSB"], env["QP"], env["KP"], env["VP"], env["SCR"],
        env["XLN"], env["OFMW"], env["SM"], env["PTM"], env["PTN"], env["PTT"],
        env["TBQA"], env["TBQB"], env["TBKA"], env["TBKB"], env["QRAW"],
        env["QT1"], env["T1Q"], env["RT"], env["QA"], env["XAF"], env["XSQ"],
        env["WSLQ"], env["WQV"], env["W1SL"], env["W2SL"], env["WOS"], env["B1"],
        env["B2"], env["RROW"], env["RRB"], env["RM"], env["PERM"], env["HSEL"],
        env["SEL2"], env["IDT"], env["MBIAS"], env["HV"], env["QS"], env["GB"],
        env["XMCOL"], env["XQCOL"], env["ONC"], env["ONR"], env["ONR5"],
        env["PS"], env["bank"], env["x1_r"], env["wqkv_r"], env["w1_r"],
        env["w2f"], env["tqa"], env["tqb"], env["tka"], env["tkb"], env["XW_3"],
        env["XSB_3"], env["m_onc"], env["m_onr"])
    TX = TX1
    XWs = lambda c, a, b: XW[:, c * TX + a:c * TX + b]
    XSBs = lambda c, a, b: XSB[:, c * TX + a:c * TX + b]
    W0 = 0 if kv_only else t0 + w * WIN
    entry = st.pop("entry", None) or []

    m_kvcp = None
    if not (l == 0 and w == 0) and not kv_only:
        kvp = st.pop("kvp_done", None) or []
        mk = None
        for c in range(4):
            mk = rec.emit("vector", lambda e, c=c: e.tensor_copy(
                out=KP[:, c * 1024:c * 1024 + 512],
                in_=KP[:, c * 1024 + 512:c * 1024 + 1024]),
                waits=after(st["attn"] if c == 0 else None,
                            st["p1dst"] if c == 0 else None,
                            *(kvp if c == 0 else [])), inc="dv")
        m_kvcp = rec.emit("vector", lambda e: e.tensor_copy(
            out=VP[:, 0:4 * WIN], in_=VP[:, 4 * WIN:8 * WIN]),
            waits=after(mk), inc="dv")

    m_xw = rec.dma("sync", XW_3[:, :, 0:TX], x1_r[:, :, W0:W0 + TX],
                   waits=after(st["store"], m_kvcp, *entry), sem="ld_w%d" % (w % 2))
    m_cast = []
    for c in range(NCH):
        m_cast.append(rec.emit("vector", lambda e, c=c: e.tensor_copy(
            out=XSBs(c, 0, TX), in_=XWs(c, 0, TX)),
            waits=after(m_xw if c == 0 else None), inc="dv"))
    m_fix = None
    for c in range(4, NCH):
        m_fix = rec.emit("vector", lambda e, c=c: e.tensor_copy(
            out=XSBs(c, 0, 1), in_=XQCOL[:, c:c + 1]),
            waits=after(m_cast[c]), inc="dv")
    m_stash = rec.emit("vector", lambda e: e.tensor_copy(
        out=XQCOL[:], in_=XSB_3[:, :, TX - 1]),
        waits=after(m_cast[-1], m_fix), inc="dv")
    m_p0 = m_stash

    m_tb = {}
    tbl_loads = ([] if kv_only else [("qa", TBQA, tqa), ("qb", TBQB, tqb)]) + \
        [("ka", TBKA, tka), ("kb", TBKB, tkb)]
    for nm, buf, srcm in tbl_loads:
        m_tb[nm] = rec.dma("gpsimd", buf[:], srcm[:, W0:W0 + WIN],
                           waits=after(st["tbl"]), sem="ld_tb" + nm)
    ocs = list(range(4, 8) if kv_only else range(0, 8))
    m_xsq = {}
    m_qraw = {}
    m_tseps = {}
    m_t1 = {}
    m_t1q = {}
    m_rt = {}
    m_qbd = {}
    m_dst = {}
    for oc in ocs:
        bk = oc % 2
        buf = oc % 2
        m_wsl = rec.dma("gpsimd", WSLQ[:, buf * 1024:(buf + 1) * 1024],
                        wqkv_r[:, l, :, oc * 128:(oc + 1) * 128],
                        waits=after(m_qraw.get(oc - 2), m_xsq.get(oc - 2)),
                        sem="ld_qk%d" % buf)
        mm = None
        for kc in range(NCH):
            sh = 1 if kc < 4 else 0
            mm = rec.emit("tensor", lambda e, kc=kc, bk=bk, buf=buf, sh=sh: e.matmul(
                bank(bk), WSLQ[:, buf * 1024 + kc * 128:buf * 1024 + (kc + 1) * 128],
                XSBs(kc, sh, sh + WIN), start=(kc == 0), stop=(kc == NCH - 1)),
                waits=after(m_wsl if kc == 0 else None,
                            m_p0 if kc == 0 else None,
                            m_qraw.get(oc - 2) if kc == 0 else None,
                            m_xsq.get(oc - 2) if kc == 0 else None),
                inc="ts" if kc == NCH - 1 else None)
        m_xsq[oc] = rec.emit("scalar", lambda e, bk=bk: e.activation(
            out=XSQ[:], in_=bank(bk), func=AF.Square), waits=after(mm), inc="ac")
        m_qraw[oc] = rec.emit("scalar", lambda e, bk=bk: e.copy(
            out=QRAW[:], in_=bank(bk)),
            waits=after(mm, m_xsq[oc], m_t1.get(oc - 1)), inc="ac")
        m_ss = rec.emit("tensor", lambda e: e.matmul(
            PS[0:2, 6 * 512:6 * 512 + 512], HSEL[:], XSQ[:], start=True, stop=True),
            waits=after(m_xsq[oc], m_tseps.get(oc - 1)), inc="ts")
        m_eps = rec.emit("vector", lambda e: e.tensor_scalar(
            out=RROW[:, 0:512], in0=PS[0:2, 6 * 512:6 * 512 + 512], scalar1=1e-24,
            scalar2=None, op0=ALU.add), waits=after(m_ss), inc="dv")
        m_tseps[oc] = m_eps
        m_lnr = rec.emit("scalar", lambda e: e.activation(
            out=RROW[:, 512:1024], in_=RROW[:, 0:512], func=AF.Ln),
            waits=after(m_eps), inc="ac")
        m_rsq = rec.emit("scalar", lambda e: e.activation(
            out=RROW[:, 1024:1536], in_=RROW[:, 512:1024], func=AF.Exp, scale=-0.5),
            waits=after(m_lnr), inc="ac")
        m_hl = emit_hilo(ctx, RROW[:, 1024:1536], RRB[:, 0:512], RRB[:, 512:1024],
                         m_rsq)
        rec.emit("tensor", lambda e: e.matmul(
            bank(3), SEL2[:], RRB[:, 0:512], start=True, stop=False),
            waits=after(m_hl, m_t1.get(oc - 1)), inc=None)
        m_bc = rec.emit("tensor", lambda e: e.matmul(
            bank(3), SEL2[:], RRB[:, 512:1024], start=False, stop=True), inc="ts")
        m_t1[oc] = rec.emit("vector", lambda e: e.tensor_tensor(
            out=QT1[:], in0=QRAW[:], in1=bank(3), op=ALU.mult),
            waits=after(m_bc, m_qraw[oc]), inc="dv")
        sccol = l if oc < 4 else (2 + l)
        m_t1q[oc] = rec.emit("vector", lambda e, sccol=sccol: e.tensor_scalar(
            out=T1Q[:], in0=QT1[:], scalar1=QS[:, sccol:sccol + 1], scalar2=None,
            op0=ALU.mult), waits=after(m_t1[oc]), inc="dv")
        m_pmm = rec.emit("tensor", lambda e: e.matmul(
            bank(2), PERM[:], T1Q[:], start=True, stop=True),
            waits=after(m_t1q[oc], m_rt.get(oc - 1)), inc="ts")
        m_rt[oc] = rec.emit("scalar", lambda e: e.copy(
            out=RT[:], in_=bank(2)),
            waits=after(m_pmm, m_qbd.get(oc - 1)), inc="ac")
        TA, TB = (TBQA, TBQB) if oc < 4 else (TBKA, TBKB)
        tka_m = m_tb["qa" if oc < 4 else "ka"]
        tkb_m = m_tb["qb" if oc < 4 else "kb"]
        m_qa = rec.emit("vector", lambda e, TA=TA: e.tensor_tensor(
            out=QA[:], in0=T1Q[:], in1=TA[:], op=ALU.mult),
            waits=after(m_t1q[oc], tka_m), inc="dv")
        m_qbd[oc] = rec.emit("vector", lambda e, TB=TB: e.tensor_tensor(
            out=XAF[:], in0=RT[:], in1=TB[:], op=ALU.mult),
            waits=after(m_rt[oc], tkb_m), inc="dv")
        if oc < 4:
            dstf = lambda oc=oc: QP[:, (oc % 4) * WIN:((oc % 4) + 1) * WIN]
            extra = st["attn"]
        else:
            dstf = lambda oc=oc: KP[:, (oc - 4) * 1024 + 512:(oc - 4) * 1024 + 1024]
            extra = m_kvcp
        m_dst[oc] = rec.emit("vector", lambda e, dstf=dstf: e.tensor_tensor(
            out=dstf(), in0=QA[:], in1=XAF[:], op=ALU.add),
            waits=after(m_qa, m_qbd[oc], extra), inc="dv")
    m_wqv = rec.dma("gpsimd", WQV[:], wqkv_r[:, l, :, 1024:1536],
                    waits=after(st["wqv_use"]), sem="ld_qv")
    m_vev = {}
    for tb in range(4):
        bk = tb % 2
        mm = None
        for kc in range(NCH):
            sh = 1 if kc < 4 else 0
            mm = rec.emit("tensor", lambda e, tb=tb, kc=kc, bk=bk, sh=sh: e.matmul(
                bank(bk), XSBs(kc, sh + tb * 128, sh + (tb + 1) * 128),
                WQV[:, kc * WIN:(kc + 1) * WIN], start=(kc == 0),
                stop=(kc == NCH - 1)),
                waits=after(m_wqv if tb == 0 and kc == 0 else None,
                            (m_qraw[ocs[-1]] if tb == 0 and kc == 0 else None),
                            (m_xsq[ocs[-1]] if tb == 0 and kc == 0 else None),
                            (m_vev.get(tb - 2) if kc == 0 else None)),
                inc="ts" if kc == NCH - 1 else None)
        m_vev[tb] = rec.emit("scalar", lambda e, tb=tb, bk=bk: e.copy(
            out=VP[:, (4 + tb) * WIN:(5 + tb) * WIN], in_=bank(bk)),
            waits=after(mm, m_kvcp if tb == 0 else None), inc="ac")
    st["tbl"] = m_dst[ocs[-1]]
    st["wqv_use"] = ("ts", rec.counts["ts"])
    st["p1dst"] = m_dst[ocs[-1]]
    m_p1 = [m_dst[oc] for oc in ocs] + [m_vev[tb] for tb in range(4)]
    if kv_only:
        st["attn"] = None
        st["kvp_done"] = m_p1
        return

    qb_glob0 = W0 // 128
    m_mask_prev = None
    m_exp_prev = None
    m_tsp_prev = None
    m_trA = {}
    m_tev_prev = {}
    m_oev = None
    m_tr_last = None
    for h in range(8):
        hb = (h % 2) * 64
        c = h // 2
        for qb in range(4):
            first = (h == 0 and qb == 0)
            m_s1 = rec.emit("tensor", lambda e, hb=hb, c=c, qb=qb: e.matmul(
                bank(0), QP[hb:hb + 64, c * WIN + qb * 128:c * WIN + (qb + 1) * 128],
                KP[hb:hb + 64, c * 1024 + qb * 128:c * 1024 + qb * 128 + 512],
                start=True, stop=True),
                waits=after(*(m_p1 if first else [m_mask_prev])), inc="ts")
            m_s2 = rec.emit("tensor", lambda e, hb=hb, c=c, qb=qb: e.matmul(
                PS[:, 512:640],
                QP[hb:hb + 64, c * WIN + qb * 128:c * WIN + (qb + 1) * 128],
                KP[hb:hb + 64, c * 1024 + qb * 128 + 512:c * 1024 + qb * 128 + 640],
                start=True, stop=True), inc="ts")
            m_mask = rec.emit("vector", lambda e: e.scalar_tensor_tensor(
                out=SM[:], in0=PS[:, 0:640], scalar=1.0, in1=MBIAS[:],
                op0=ALU.mult, op1=ALU.add),
                waits=after(m_s1, m_s2, m_exp_prev), inc="dv")
            m_mask_prev = m_mask
            m_rmx = rec.emit("vector", lambda e: e.reduce_max(
                out=RM[:, 0:1], in_=SM[:], axis=AX.X), waits=after(m_mask), inc="dv")
            m_nrm = rec.emit("vector", lambda e: e.tensor_scalar(
                out=RM[:, 1:2], in0=RM[:, 0:1], scalar1=-1.0, scalar2=None,
                op0=ALU.mult), waits=after(m_rmx), inc="dv")
            m_exp = rec.emit("scalar", lambda e: e.activation(
                out=PTM[:], in_=SM[:], func=AF.Exp, bias=RM[:, 1:2], scale=1.0),
                waits=after(m_nrm, m_tr_last), inc="ac")
            m_exp_prev = m_exp
            m_hvl = None
            for sb in range(5):
                kcol = min(qb_glob0 + qb + sb, 35)
                m_hvl = rec.emit("vector", lambda e, sb=sb, kcol=kcol: e.tensor_scalar(
                    out=PTN[:, sb * 128:(sb + 1) * 128],
                    in0=PTM[:, sb * 128:(sb + 1) * 128],
                    scalar1=HV[:, kcol:kcol + 1], scalar2=None, op0=ALU.mult),
                    waits=after(m_exp if sb == 0 else None,
                                m_tsp_prev if sb == 0 else None), inc="dv")
            m_z = rec.emit("vector", lambda e: e.reduce_sum(
                out=RM[:, 2:3], in_=PTN[:], axis=AX.X), waits=after(m_hvl), inc="dv")
            m_ze = rec.emit("vector", lambda e: e.tensor_scalar(
                out=RM[:, 3:4], in0=RM[:, 2:3], scalar1=1e-30, scalar2=None,
                op0=ALU.add), waits=after(m_z), inc="dv")
            m_zi = rec.emit("vector", lambda e: e.reciprocal(
                out=RM[:, 0:1], in_=RM[:, 3:4]), waits=after(m_ze), inc="dv")
            m_tsp = rec.emit("vector", lambda e: e.tensor_scalar(
                out=PTM[:], in0=PTN[:], scalar1=RM[:, 0:1], scalar2=None,
                op0=ALU.mult), waits=after(m_zi), inc="dv")
            m_tsp_prev = m_tsp
            m_omms = []
            for sb in range(5):
                bkt = 2 + sb % 2
                m_tr = rec.emit("tensor", lambda e, sb=sb, bkt=bkt: e.matmul(
                    PS[:, bkt * 512:bkt * 512 + 128], PTM[:, sb * 128:(sb + 1) * 128],
                    IDT[:], start=True, stop=True),
                    waits=after(m_tsp if sb == 0 else None, m_trA.get(bkt)),
                    inc="ts")
                if sb == 4:
                    m_tr_last = m_tr
                m_tev = rec.emit("scalar", lambda e, sb=sb, bkt=bkt: e.copy(
                    out=PTT[:, sb * 128:(sb + 1) * 128],
                    in_=PS[:, bkt * 512:bkt * 512 + 128]),
                    waits=after(m_tr, m_tev_prev.get(sb)), inc="ac")
                m_trA[bkt] = m_tev
                m_omm = rec.emit("tensor", lambda e, sb=sb, h=h, qb=qb: e.matmul(
                    PS[0:64, 4 * 512:4 * 512 + 128],
                    VP[:, (qb + sb) * WIN + h * 64:(qb + sb) * WIN + h * 64 + 64],
                    PTT[:, sb * 128:(sb + 1) * 128], start=(sb == 0), stop=(sb == 4)),
                    waits=after(m_tev, m_oev if sb == 0 else None), inc="ts")
                m_omms.append(m_omm)
            m_tev_prev = {sb: m_omms[sb] for sb in range(5)}
            m_oev = rec.emit("scalar", lambda e, hb=hb, c=c, qb=qb: e.copy(
                out=OFMW[hb:hb + 64, c * WIN + qb * 128:c * WIN + (qb + 1) * 128],
                in_=PS[0:64, 4 * 512:4 * 512 + 128]),
                waits=after(m_omms[-1]), inc="ac")
    st["attn"] = m_oev
    m_p2 = m_oev

    m_stt3 = {}
    for m in range(NCH):
        bk = m % 2
        mm = None
        for kc in range(4):
            mm = rec.emit("tensor", lambda e, m=m, kc=kc, bk=bk: e.matmul(
                bank(bk), WOS[:, kc * D + m * 128:kc * D + (m + 1) * 128],
                OFMW[:, kc * WIN:(kc + 1) * WIN], start=(kc == 0), stop=(kc == 3)),
                waits=after(m_p2 if m == 0 and kc == 0 else None,
                            m_stt3.get(m - 2) if kc == 0 else None),
                inc="ts" if kc == 3 else None)
        m_stt3[m] = rec.emit("vector", lambda e, m=m, bk=bk: e.scalar_tensor_tensor(
            out=XWs(m, 1, 1 + WIN), in0=bank(bk), scalar=1.0, in1=XWs(m, 1, 1 + WIN),
            op0=ALU.mult, op1=ALU.add), waits=after(mm), inc="dv")
    m_p3 = [m_stt3[m] for m in range(NCH)]

    nb, cb = (w + 1) % 2, w % 2
    m_xm = rec.emit("vector", lambda e, nb=nb: e.tensor_copy(
        out=XMCOL[:, nb * 8:nb * 8 + 8], in_=XW_3[:, :, TX - 1]),
        waits=after(*m_p3), inc="dv")
    XSH = SCR.ap()[:, 0:8192].bitcast(F32)
    m_sh = []
    for c in range(NCH):
        sh = 1 if c < 4 else 0
        m1 = rec.emit("vector", lambda e, c=c, sh=sh: e.tensor_copy(
            out=XSH[:, c * WIN:(c + 1) * WIN], in_=XWs(c, sh, sh + WIN)),
            waits=after(*(m_p3 if c == 0 else [])), inc="dv")
        if c >= 4:
            m1 = rec.emit("vector", lambda e, c=c, cb=cb: e.tensor_copy(
                out=XSH[:, c * WIN:c * WIN + 1],
                in_=XMCOL[:, cb * 8 + c:cb * 8 + c + 1]),
                waits=after(m1, st["xmprev"]), inc="dv")
        m_sh.append(m1)
    st["xmprev"] = m_xm
    m_ln = emit_ln(ctx, lambda c, a, b: XSH[:, c * WIN + a:c * WIN + b],
                   lambda c, a, b: XLN[:, c * WIN + a:c * WIN + b],
                   WIN, lambda c: [m_sh[c], m_sh[-1]], m_onc, m_onr,
                   g_ap=lambda c, l=l: GB[:, l * NCH + c:l * NCH + c + 1],
                   b_ap=lambda c, l=l: GB[:, 2 * NCH + l * NCH + c:2 * NCH + l * NCH + c + 1])

    m_ff = {}
    m_gev_prev = None
    for i in range(NPAIR):
        Ma = 128 if i < NPAIR - 1 else 42
        buf = i % 2
        m_w1a = rec.dma("gpsimd", W1SL[:, buf * 2048:buf * 2048 + 8 * Ma],
                        w1_r[:, l, :, i * 128:i * 128 + Ma],
                        waits=after(m_ff.get(i - 2)), sem="ld_w1%d" % buf)
        m_w1g = rec.dma("gpsimd", W1SL[:, buf * 2048 + 1024:buf * 2048 + 1024 + 8 * Ma],
                        w1_r[:, l, :, 2730 + i * 128:2730 + i * 128 + Ma],
                        sem="ld_w1%d" % buf)
        ab = (i % 2) * 2 * 512
        gb = 512 + (i % 2) * 2 * 512
        mma = rec.emit("tensor", lambda e, i=i, Ma=Ma, ab=ab: e.matmul(
            PS[0:Ma, ab:ab + 512], B1[0:1, i * 128:i * 128 + Ma], ONR5[:],
            start=True, stop=False),
            waits=after(m_w1g, m_ln if i == 0 else None, m_ff.get(i - 2)), inc=None)
        for kc in range(NCH):
            mma = rec.emit("tensor", lambda e, kc=kc, Ma=Ma, buf=buf, ab=ab: e.matmul(
                PS[0:Ma, ab:ab + 512],
                W1SL[:, buf * 2048 + kc * Ma:buf * 2048 + (kc + 1) * Ma],
                XLN[:, kc * WIN:(kc + 1) * WIN], start=False, stop=(kc == NCH - 1)),
                inc="ts" if kc == NCH - 1 else None)
        mmg = rec.emit("tensor", lambda e, i=i, Ma=Ma, gb=gb: e.matmul(
            PS[0:Ma, gb:gb + 512], B1[0:1, 2730 + i * 128:2730 + i * 128 + Ma], ONR5[:],
            start=True, stop=False), waits=after(m_w1g, m_gev_prev), inc=None)
        for kc in range(NCH):
            mmg = rec.emit("tensor", lambda e, kc=kc, Ma=Ma, buf=buf, gb=gb: e.matmul(
                PS[0:Ma, gb:gb + 512],
                W1SL[:, buf * 2048 + 1024 + kc * Ma:buf * 2048 + 1024 + (kc + 1) * Ma],
                XLN[:, kc * WIN:(kc + 1) * WIN], start=False, stop=(kc == NCH - 1)),
                inc="ts" if kc == NCH - 1 else None)
        m_gev = rec.emit("scalar", lambda e, Ma=Ma, gb=gb: e.activation(
            out=QRAW[0:Ma, :], in_=PS[0:Ma, gb:gb + 512], func=AF.Gelu),
            waits=after(mmg, m_ff.get(i - 1)), inc="ac")
        m_gev_prev = m_gev
        m_ff[i] = rec.emit("vector", lambda e, i=i, Ma=Ma, ab=ab: e.tensor_tensor(
            out=SCR[0:Ma, i * WIN:(i + 1) * WIN], in0=PS[0:Ma, ab:ab + 512],
            in1=QRAW[0:Ma, :], op=ALU.mult), waits=after(mma, m_gev), inc="dv")
    m_stt5 = {}
    for m in range(NCH):
        bk = 2 + m % 2
        buf = m % 2
        m_w2a = rec.dma("gpsimd", W2SL[:, buf * 2816:buf * 2816 + 21 * 128],
                        w2f[l * 2730:l * 2730 + 2688, m * 128:(m + 1) * 128]
                        .rearrange("(k p) m -> p k m", p=128),
                        waits=after(m_stt5.get(m - 2)), sem="ld_w2%d" % buf)
        m_w2b = rec.dma("gpsimd", W2SL[0:42, buf * 2816 + 21 * 128:buf * 2816 + 22 * 128],
                        w2f[l * 2730 + 2688:l * 2730 + 2730, m * 128:(m + 1) * 128],
                        sem="ld_w2%d" % buf)
        mm = rec.emit("tensor", lambda e, m=m, bk=bk: e.matmul(
            bank(bk), B2[0:1, m * 128:(m + 1) * 128], ONR5[:], start=True, stop=False),
            waits=after(m_w2b, m_ff[NPAIR - 1] if m <= 1 else None,
                        m_ff[NPAIR - 2] if m <= 1 else None,
                        m_stt5.get(m - 2)), inc=None)
        for k2 in range(NK2):
            Ka = 128 if k2 < NK2 - 1 else 42
            mm = rec.emit("tensor", lambda e, k2=k2, Ka=Ka, bk=bk, buf=buf: e.matmul(
                bank(bk), W2SL[0:Ka, buf * 2816 + k2 * 128:buf * 2816 + k2 * 128 + 128],
                SCR[0:Ka, k2 * WIN:(k2 + 1) * WIN], start=False, stop=(k2 == NK2 - 1)),
                waits=after(m_ff[k2] if m == 0 else None),
                inc="ts" if k2 == NK2 - 1 else None)
        m_stt5[m] = rec.emit("vector", lambda e, m=m, bk=bk: e.scalar_tensor_tensor(
            out=XWs(m, 1, 1 + WIN), in0=bank(bk), scalar=1.0, in1=XWs(m, 1, 1 + WIN),
            op0=ALU.mult, op1=ALU.add), waits=after(mm), inc="dv")

    st["store"] = rec.dma("sync", x1_r[:, :, W0 + 1:W0 + 1 + WIN],
                          XW_3[:, :, 1:1 + WIN],
                          waits=after(*[m_stt5[m] for m in range(NCH)]),
                          sem="st_x1")


def build_s1(nc, layers=2, nwin_limit=None):
    rec = Rec(nc)
    ctx = Ctx(nc, rec)
    x_in = nc.dram_tensor("x", [D, 1 + NX1], F32, kind="ExternalInput").ap()
    wqkvf = nc.dram_tensor("wqkvf", [2 * D, 1536], F32, kind="ExternalInput").ap()
    w1f = nc.dram_tensor("w1f", [2 * D, 5460], F32, kind="ExternalInput").ap()
    w2f = nc.dram_tensor("w2f", [2 * 2730, 1024], F32, kind="ExternalInput").ap()
    woT = nc.dram_tensor("woT", [2 * 512, 1024], F32, kind="ExternalInput").ap()
    b1r = nc.dram_tensor("b1r", [2, 5460], F32, kind="ExternalInput").ap()
    b2r = nc.dram_tensor("b2r", [2, 1024], F32, kind="ExternalInput").ap()
    tqa = nc.dram_tensor("tqa", [128, NX1], F32, kind="ExternalInput").ap()
    tqb = nc.dram_tensor("tqb", [128, NX1], F32, kind="ExternalInput").ap()
    tka = nc.dram_tensor("tka", [128, NX1], F32, kind="ExternalInput").ap()
    tkb = nc.dram_tensor("tkb", [128, NX1], F32, kind="ExternalInput").ap()
    qscol = nc.dram_tensor("qscol", [128, 2], F32, kind="ExternalInput").ap()
    kscol = nc.dram_tensor("kscol", [128, 2], F32, kind="ExternalInput").ap()
    gcol = nc.dram_tensor("gcol", [128, 16], F32, kind="ExternalInput").ap()
    bcol = nc.dram_tensor("bcol", [128, 16], F32, kind="ExternalInput").ap()
    perm = nc.dram_tensor("perm", [128, 128], F32, kind="ExternalInput").ap()
    hsel = nc.dram_tensor("hsel", [128, 2], F32, kind="ExternalInput").ap()
    sel2 = nc.dram_tensor("sel2", [2, 128], F32, kind="ExternalInput").ap()
    ident = nc.dram_tensor("ident", [128, 128], F32, kind="ExternalInput").ap()
    mband = nc.dram_tensor("mband", [128, 640], F32, kind="ExternalInput").ap()
    hv = nc.dram_tensor("hv", [128, 36], F32, kind="ExternalInput").ap()
    onc = nc.dram_tensor("ones_col", [128, 1], F32, kind="ExternalInput").ap()
    onr = nc.dram_tensor("ones_row", [1, 128], F32, kind="ExternalInput").ap()
    onr5 = nc.dram_tensor("ones_row5", [1, WIN], F32, kind="ExternalInput").ap()
    x1d = nc.dram_tensor("x1d", [D, 1 + NX1], F32, kind="ExternalOutput")
    xo = nc.dram_tensor("xo", [D, T], F32, kind="ExternalOutput").ap()
    mil = emit_s1(ctx, x_in, x1d, xo, wqkvf, w1f, w2f, woT, b1r, b2r,
                  tqa, tqb, tka, tkb, qscol, kscol, gcol, bcol,
                  perm, hsel, sel2, ident, mband, hv, onc, onr, onr5,
                  layers=layers, nwin_limit=nwin_limit)
    rec.finalize(mil)
    return nc


def s1_host_tables(sb):
    half = np.arange(0, 64, 2, dtype=np.float64)
    inv_freq = 1.0 / 10000.0 ** (half / 64.0)
    base = (half + 0.4 * 64) / (1.4 * 64)
    p = np.arange(sb - HALO1, sb + T, dtype=np.float64)
    fr = p[:, None] * inv_freq[None, :]
    power = (p - 4096.0) / 256.0
    sc = base[None, :] ** power[:, None]
    cos, sin = np.cos(fr), np.sin(fr)

    def pack(t):
        t64 = np.concatenate([t, t], axis=1)
        return np.ascontiguousarray(
            np.concatenate([t64, t64], axis=1).T.astype(np.float32))
    return (pack(8.0 * cos * sc), pack(8.0 * sin * sc),
            pack(cos / sc), pack(sin / sc))


def s1_host_consts():
    permv = np.zeros((128, 128), np.float32)
    for b in (0, 64):
        for m in range(32):
            permv[b + m + 32, b + m] = -1.0
            permv[b + m, b + m + 32] = 1.0
    hselv = np.zeros((128, 2), np.float32)
    hselv[0:64, 0] = 1.0
    hselv[64:128, 1] = 1.0
    sel2v = np.zeros((2, 128), np.float32)
    sel2v[0, 0:64] = 1.0
    sel2v[1, 64:128] = 1.0
    r = np.arange(128)[:, None]
    cc = np.arange(640)[None, :]
    mbandv = np.where((cc >= r) & (cc <= r + 512), 0.0, -1e9).astype(np.float32)
    return permv, hselv, sel2v, mbandv


def s1_host_hv(batch_start):
    hvv = np.ones((128, 36), np.float32)
    hvv[:, 0:4] = 0.0
    if batch_start:
        hvv[:, 4:16] = 0.0
    return hvv


def run_s1(x_full, wqkv, q_scale, k_scale, wo, ln_g, ln_b, w1, b1, w2, b2,
           layers=2, nwin_limit=None, want_x1=False):
    nc = build_s1(bass.Bass("TRN2", target_bir_lowering=False,
                            dynamic_dma_scratch_size=16384),
                  layers=layers, nwin_limit=nwin_limit)
    flat = np.ascontiguousarray(np.asarray(x_full).reshape(2 * 8192, D))
    permv, hselv, sel2v, mbandv = s1_host_consts()
    wqkv_h = np.ascontiguousarray(np.asarray(wqkv).reshape(2 * D, 1536))
    w1_h = np.ascontiguousarray(np.asarray(w1).reshape(2 * D, 5460))
    w2_h = np.ascontiguousarray(np.asarray(w2).reshape(2 * 2730, 1024))
    wo_h = np.ascontiguousarray(np.asarray(wo).reshape(2 * 512, 1024))
    qsc = np.zeros((128, 2), np.float32)
    ksc = np.zeros((128, 2), np.float32)
    gcv = np.zeros((128, 16), np.float32)
    bcv = np.zeros((128, 16), np.float32)
    for l in range(2):
        qsc[:, l] = np.tile(np.asarray(q_scale)[l], 2)
        ksc[:, l] = np.tile(np.asarray(k_scale)[l], 2)
        for c in range(NCH):
            gcv[:, l * 8 + c] = np.asarray(ln_g)[l, c * 128:(c + 1) * 128]
            bcv[:, l * 8 + c] = np.asarray(ln_b)[l, c * 128:(c + 1) * 128]
    in_maps = []
    for c in range(NCORES):
        s = c * T
        sb = s % 8192
        xs = np.zeros((1 + NX1, D), np.float32)
        if sb >= HALO1:
            xs[1:1 + HALO1] = flat[s - HALO1:s]
        xs[1 + HALO1:] = flat[s:s + T]
        tqa_, tqb_, tka_, tkb_ = s1_host_tables(sb)
        in_maps.append({
            "x": np.ascontiguousarray(xs.T),
            "wqkvf": wqkv_h, "w1f": w1_h, "w2f": w2_h, "woT": wo_h,
            "b1r": np.ascontiguousarray(np.asarray(b1, np.float32)),
            "b2r": np.ascontiguousarray(np.asarray(b2, np.float32)),
            "tqa": tqa_, "tqb": tqb_, "tka": tka_, "tkb": tkb_,
            "qscol": qsc, "kscol": ksc, "gcol": gcv, "bcol": bcv,
            "perm": permv, "hsel": hselv, "sel2": sel2v,
            "ident": np.eye(128, dtype=np.float32),
            "mband": mbandv, "hv": s1_host_hv(sb == 0),
            "ones_col": np.ones((128, 1), np.float32),
            "ones_row": np.ones((1, 128), np.float32),
            "ones_row5": np.ones((1, WIN), np.float32),
        })
    res = run_bass_kernel_spmd(nc, in_maps, core_ids=list(range(NCORES)))
    out = np.concatenate([res.results[c]["xo"].T for c in range(NCORES)], axis=0)
    if want_x1:
        x1s = [res.results[c]["x1d"].T for c in range(NCORES)]
        return out.reshape(2, 8192, D), x1s
    return out.reshape(2, 8192, D)



def shard_tokens(x_full):
    flat = np.ascontiguousarray(x_full.reshape(2 * 8192, D))
    return [np.ascontiguousarray(flat[c * T:(c + 1) * T].T) for c in range(NCORES)]


def boundary_cols(x_full):
    flat = x_full.reshape(2 * 8192, D)
    outs = []
    for c in range(NCORES):
        s = c * T
        if s % 8192 == 0:
            outs.append(np.zeros((D, 1), np.float32))
        else:
            outs.append(np.ascontiguousarray(flat[s - 1:s].T))
    return outs


def run_s4(x_full, W_r, W_k, W_v, mu_r, mu_k):
    nc = build_s4(bass.Bass("TRN2", target_bir_lowering=False,
                            dynamic_dma_scratch_size=16384))
    shards = shard_tokens(x_full)
    bnds = boundary_cols(x_full)
    wrT = np.ascontiguousarray(W_r.T)
    wkT = np.ascontiguousarray(W_k.T)
    wvT = np.ascontiguousarray(W_v.T)
    in_maps = []
    for c in range(NCORES):
        in_maps.append({
            "x": np.concatenate([bnds[c], shards[c]], axis=1),
            "wrT": wrT, "wkT": wkT, "wvT": wvT,
            "mur": np.ascontiguousarray(mu_r.reshape(NCH, 128).T),
            "muk": np.ascontiguousarray(mu_k.reshape(NCH, 128).T),
            "ones_col": np.ones((128, 1), np.float32),
            "ones_row": np.ones((1, 128), np.float32),
        })
    res = run_bass_kernel_spmd(nc, in_maps, core_ids=list(range(NCORES)))
    out = np.concatenate([res.results[c]["xo"].T for c in range(NCORES)], axis=0)
    return out.reshape(2, 8192, D)


def run_s2(x_full, Wq, Wk, Wv, Wo, Kq, Kk, stage=100):
    nc = build_s2(bass.Bass("TRN2", target_bir_lowering=False,
                            dynamic_dma_scratch_size=16384), stage=stage)
    shards = shard_tokens(x_full)
    kqp, kkp, tri = s2_host_inputs(np.asarray(Kq), np.asarray(Kk))
    in_maps = []
    for c in range(NCORES):
        selv = np.zeros((128, NCORES), np.float32)
        b0 = (c // 4) * 4
        for r in range(b0, c):
            selv[:, r] = 1.0
        in_maps.append({
            "x": shards[c], "wq": np.ascontiguousarray(Wq), "wk": np.ascontiguousarray(Wk),
            "wv": np.ascontiguousarray(Wv), "wo": np.ascontiguousarray(Wo),
            "kqp": kqp, "kkp": kkp, "sel": selv,
            "ones_col": np.ones((128, 1), np.float32),
            "ones_row": np.ones((1, 128), np.float32),
            "tri": tri, "ident": np.eye(128, dtype=np.float32),
        })
    res = run_bass_kernel_spmd(nc, in_maps, core_ids=list(range(NCORES)))
    out = np.concatenate([res.results[c]["xo"].T for c in range(NCORES)], axis=0)
    return out.reshape(2, 8192, D)




def emit_xhalo(ctx, pfx, src_body, dstL, dstR, L, R, selL, selR):
    nc, rec = ctx.nc, ctx.rec
    ncols = L + R
    cc_in = nc.dram_tensor(pfx + "cci", [D, ncols], F32)
    cc_out = nc.dram_tensor(pfx + "cco", [4 * D, ncols], F32)
    SEL = ctx.sb("SEL", [128, 8], F32)
    HB = ctx.sb("HB", [128, NCH * 512], F32)
    RB = ctx.sb("RBUF", [128, 2 * NCH * 512], F32)
    HB3 = HB.ap().rearrange("p (c t) -> p c t", c=NCH)
    RB3 = RB.ap().rearrange("p (b c t) -> p b c t", b=2, c=NCH)
    m_sl = rec.dma("sync", SEL[:, 0:4], selL, sem="hx_sl")
    m_sr = rec.dma("sync", SEL[:, 4:8], selR, sem="hx_sr")
    ms = []
    if L:
        ms.append(rec.dma("sync", cc_in.ap()[:, 0:L], src_body[:, T - L:T],
                          sem="hx_cil",
                          allow_slow_non_contiguous=(L < 8)))
    if R:
        ms.append(rec.dma("sync", cc_in.ap()[:, L:L + R], src_body[:, 0:R],
                          sem="hx_cir",
                          allow_slow_non_contiguous=True))
    m_ag = rec.emit("gpsimd", lambda e: e.collective_compute(
        "AllGather", ALU.bypass, replica_groups=[[0, 1, 2, 3], [4, 5, 6, 7]],
        ins=[cc_in.ap()], outs=[cc_out.ap()]), waits=after(*ms), inc="cc")
    outms = []
    m_add = {}
    for (side, base, ncol, dst, selo) in (("L", 0, L, dstL, 0),
                                          ("R", L, R, dstR, 4)):
        if ncol == 0:
            continue
        for c0 in range(0, ncol, 512):
            ck = min(512, ncol - c0)
            mz = rec.emit("vector", lambda e, ck=ck: e.memset(
                HB[:, 0:NCH * ck], 0.0), waits=after(outms[-1] if outms else None),
                inc="dv")
            ma = mz
            for r in range(4):
                m_ld = rec.dma("sync", RB3[:, r % 2, :, 0:ck],
                               cc_out.ap()[r * D:(r + 1) * D, base + c0:base + c0 + ck]
                               .rearrange("(c p) t -> p c t", p=128),
                               waits=after(m_ag if r == 0 else None,
                                           m_add.get(r - 2)),
                               sem="hx_" + side + "r%d" % (r % 2),
                               allow_slow_non_contiguous=(ck < 8))
                ma = rec.emit("vector", lambda e, r=r, ck=ck, selo=selo: e.scalar_tensor_tensor(
                    out=HB[:, 0:NCH * ck], in0=RB[:, (r % 2) * NCH * 512:(r % 2) * NCH * 512 + NCH * ck],
                    scalar=SEL[:, selo + r:selo + r + 1], in1=HB[:, 0:NCH * ck],
                    op0=ALU.mult, op1=ALU.add),
                    waits=after(m_ld, ma, m_sl if r == 0 else None,
                                m_sr if r == 0 else None), inc="dv")
                m_add[r] = ma
            m_st = rec.dma("sync", dst[:, c0:c0 + ck]
                           .rearrange("(c p) t -> p c t", p=128),
                           HB3[:, :, 0:ck], waits=after(ma), sem="hx_" + side + "st",
                           allow_slow_non_contiguous=(ck < 8))
            outms.append(m_st)
    return outms


def build_fused(nc):
    rec = Rec(nc)
    ctx = Ctx(nc, rec, arena=(17024, 229376))
    EI = lambda n, sh: nc.dram_tensor(n, sh, F32, kind="ExternalInput").ap()
    x_in = EI("x", [D, 1 + NX1])
    wqkvf = EI("wqkvf", [2 * D, 1536]); w1f = EI("w1f", [2 * D, 5460])
    w2f = EI("w2f", [2 * 2730, 1024]); woT = EI("woT", [2 * 512, 1024])
    b1r = EI("b1r", [2, 5460]); b2r = EI("b2r", [2, 1024])
    tqa = EI("tqa", [128, NX1]); tqb = EI("tqb", [128, NX1])
    tka = EI("tka", [128, NX1]); tkb = EI("tkb", [128, NX1])
    qscol = EI("qscol", [128, 2]); kscol = EI("kscol", [128, 2])
    gcol = EI("gcol", [128, 16]); bcol = EI("bcol", [128, 16])
    perm = EI("perm", [128, 128]); hsel = EI("hsel", [128, 2])
    sel2 = EI("sel2", [2, 128]); ident = EI("ident", [128, 128])
    mband = EI("mband", [128, 640]); hv = EI("hv", [128, 36])
    onc = EI("ones_col", [128, 1]); onr = EI("ones_row", [1, 128])
    onr5 = EI("ones_row5", [1, WIN])
    wq = EI("wq", [D, D]); wk = EI("wk", [D, D]); wv = EI("wv", [D, D])
    wo2 = EI("wo", [D, D])
    kqp = EI("kqp", [NHEAD * 128, 128]); kkp = EI("kkp", [NHEAD * 128, 128])
    sel = EI("sel", [128, NCORES]); tri = EI("tri", [128, 128])
    wrT = EI("wrT", [D, D]); wkT = EI("wkT", [D, D]); wvT = EI("wvT", [D, D])
    mur = EI("mur", [128, NCH]); muk = EI("muk", [128, NCH])
    pewT = EI("pewT", [D, D]); qkvwT = EI("qkvwT", [D, 3 * D])
    owT = EI("owT", [D, D]); taps = EI("taps", [128, 3 * NCH])
    selL = EI("selL", [128, 4]); selR = EI("selR", [128, 4])
    x1d = nc.dram_tensor("x1d", [D, 1 + NX1], F32, kind="ExternalOutput")
    xs2 = nc.dram_tensor("xs2", [D, T], F32, kind="ExternalOutput")
    xs3 = nc.dram_tensor("xs3", [D, 1 + NX1], F32, kind="ExternalOutput")
    xs4 = nc.dram_tensor("xs4", [D, T + 1], F32, kind="ExternalOutput")
    xs5 = nc.dram_tensor("xs5", [D, T + 2], F32, kind="ExternalOutput")
    xs6 = nc.dram_tensor("xs6", [D, T + 1], F32, kind="ExternalOutput")
    oacc = nc.dram_tensor("oacc", [NHEAD * 64, T], F32, kind="ExternalOutput")
    ozacc = nc.dram_tensor("ozacc", [NHEAD, T], F32, kind="ExternalOutput")
    xo = nc.dram_tensor("xo", [D, T], F32, kind="ExternalOutput").ap()

    s1_args = (wqkvf, w1f, w2f, woT, b1r, b2r, tqa, tqb, tka, tkb, qscol,
               kscol, gcol, bcol, perm, hsel, sel2, ident, mband, hv, onc,
               onr, onr5)
    ctx.reset("s1_")
    m1 = emit_s1(ctx, x_in, x1d, xs2.ap(), *s1_args)
    rec.gate = tuple(m1)
    ctx.reset("s2_")
    m2 = emit_s2(ctx, xs2.ap(), xs3.ap()[:, 1 + HALO1:1 + NX1], wq, wk, wv,
                 wo2, kqp, kkp, sel, onc, onr, tri, ident)
    rec.gate = tuple(m2)
    ctx.reset("h3_")
    mh3 = emit_xhalo(ctx, "h3_", xs3.ap()[:, 1 + HALO1:1 + NX1],
                     xs3.ap()[:, 1:1 + HALO1], None, HALO1, 0, selL, selR)
    rec.gate = tuple(m2) + tuple(mh3)
    ctx.reset("s3_")
    m3 = emit_s1(ctx, xs3.ap(), x1d, xs4.ap()[:, 1:1 + T], *s1_args)
    rec.gate = tuple(m3)
    ctx.reset("h4_")
    mh4 = emit_xhalo(ctx, "h4_", xs4.ap()[:, 1:1 + T], xs4.ap()[:, 0:1],
                     None, 1, 0, selL, selR)
    rec.gate = tuple(m3) + tuple(mh4)
    ctx.reset("s4_")
    m4 = emit_s4(ctx, xs4.ap(), xs5.ap()[:, 1:1 + T], wrT, wkT, wvT, mur,
                 muk, onc, onr)
    rec.gate = tuple(m4)
    ctx.reset("h5_")
    mh5 = emit_xhalo(ctx, "h5_", xs5.ap()[:, 1:1 + T], xs5.ap()[:, 0:1],
                     xs5.ap()[:, 1 + T:2 + T], 1, 1, selL, selR)
    rec.gate = tuple(m4) + tuple(mh5)
    ctx.reset("s5_")
    m5 = emit_s5(ctx, xs5.ap(), xs6.ap()[:, 1:1 + T], pewT, qkvwT, owT, taps,
                 onc, onr, sel2, oacc, ozacc)
    rec.gate = tuple(m5)
    ctx.reset("h6_")
    mh6 = emit_xhalo(ctx, "h6_", xs6.ap()[:, 1:1 + T], xs6.ap()[:, 0:1],
                     None, 1, 0, selL, selR)
    rec.gate = tuple(m5) + tuple(mh6)
    ctx.reset("s6_")
    m6 = emit_s4(ctx, xs6.ap(), xo, wrT, wkT, wvT, mur, muk, onc, onr)
    rec.finalize(m6)
    return nc


def kernel_fused(inputs):
    nc = build_fused(bass.Bass("TRN2", target_bir_lowering=False,
                               dynamic_dma_scratch_size=16384))
    flat = np.ascontiguousarray(
        np.asarray(inputs["L0_attn_x"], np.float32).reshape(2 * 8192, D))
    permv, hselv, sel2v, mbandv = s1_host_consts()
    kqp, kkp, tri = s2_host_inputs(np.asarray(inputs["L0_attn_Kq"]),
                                   np.asarray(inputs["L0_attn_Kk"]))
    wqkv_h = np.ascontiguousarray(np.asarray(inputs["L0_ffn_wqkv"]).reshape(2 * D, 1536))
    w1_h = np.ascontiguousarray(np.asarray(inputs["L0_ffn_w1"]).reshape(2 * D, 5460))
    w2_h = np.ascontiguousarray(np.asarray(inputs["L0_ffn_w2"]).reshape(2 * 2730, 1024))
    wo_h = np.ascontiguousarray(np.asarray(inputs["L0_ffn_wo"]).reshape(2 * 512, 1024))
    qsc = np.zeros((128, 2), np.float32); ksc = np.zeros((128, 2), np.float32)
    gcv = np.zeros((128, 16), np.float32); bcv = np.zeros((128, 16), np.float32)
    for l in range(2):
        qsc[:, l] = np.tile(np.asarray(inputs["L0_ffn_q_scale"])[l], 2)
        ksc[:, l] = np.tile(np.asarray(inputs["L0_ffn_k_scale"])[l], 2)
        for c in range(NCH):
            gcv[:, l * 8 + c] = np.asarray(inputs["L0_ffn_ln_g"])[l, c * 128:(c + 1) * 128]
            bcv[:, l * 8 + c] = np.asarray(inputs["L0_ffn_ln_b"])[l, c * 128:(c + 1) * 128]
    taps_h = np.ascontiguousarray(np.asarray(inputs["L1_attn_conv_w"], np.float32)[:, 0, :, 1])
    taps_t = np.zeros((128, 3 * NCH), np.float32)
    for c in range(NCH):
        taps_t[:, 3 * c:3 * (c + 1)] = taps_h[c * 128:(c + 1) * 128]
    mu_r = np.asarray(inputs["L1_ffn_mu_r"], np.float32)
    mu_k = np.asarray(inputs["L1_ffn_mu_k"], np.float32)
    in_maps = []
    for c in range(NCORES):
        s = c * T
        sb = s % 8192
        xs = np.zeros((1 + NX1, D), np.float32)
        if sb >= HALO1:
            xs[1:1 + HALO1] = flat[s - HALO1:s]
        xs[1 + HALO1:] = flat[s:s + T]
        tqa_, tqb_, tka_, tkb_ = s1_host_tables(sb)
        selv = np.zeros((128, NCORES), np.float32)
        for r in range((c // 4) * 4, c):
            selv[:, r] = 1.0
        slv = np.zeros((128, 4), np.float32)
        srv = np.zeros((128, 4), np.float32)
        if c % 4 > 0:
            slv[:, c % 4 - 1] = 1.0
        if c % 4 < 3:
            srv[:, c % 4 + 1] = 1.0
        in_maps.append({
            "x": np.ascontiguousarray(xs.T),
            "wqkvf": wqkv_h, "w1f": w1_h, "w2f": w2_h, "woT": wo_h,
            "b1r": np.ascontiguousarray(np.asarray(inputs["L0_ffn_b1"], np.float32)),
            "b2r": np.ascontiguousarray(np.asarray(inputs["L0_ffn_b2"], np.float32)),
            "tqa": tqa_, "tqb": tqb_, "tka": tka_, "tkb": tkb_,
            "qscol": qsc, "kscol": ksc, "gcol": gcv, "bcol": bcv,
            "perm": permv, "hsel": hselv, "sel2": sel2v,
            "ident": np.eye(128, dtype=np.float32),
            "mband": mbandv, "hv": s1_host_hv(sb == 0),
            "ones_col": np.ones((128, 1), np.float32),
            "ones_row": np.ones((1, 128), np.float32),
            "ones_row5": np.ones((1, WIN), np.float32),
            "wq": np.ascontiguousarray(np.asarray(inputs["L0_attn_Wq"], np.float32)),
            "wk": np.ascontiguousarray(np.asarray(inputs["L0_attn_Wk"], np.float32)),
            "wv": np.ascontiguousarray(np.asarray(inputs["L0_attn_Wv"], np.float32)),
            "wo": np.ascontiguousarray(np.asarray(inputs["L0_attn_Wo"], np.float32)),
            "kqp": kqp, "kkp": kkp, "sel": selv, "tri": tri,
            "wrT": np.ascontiguousarray(np.asarray(inputs["L1_ffn_W_r"], np.float32).T),
            "wkT": np.ascontiguousarray(np.asarray(inputs["L1_ffn_W_k"], np.float32).T),
            "wvT": np.ascontiguousarray(np.asarray(inputs["L1_ffn_W_v"], np.float32).T),
            "mur": np.ascontiguousarray(mu_r.reshape(NCH, 128).T),
            "muk": np.ascontiguousarray(mu_k.reshape(NCH, 128).T),
            "pewT": np.ascontiguousarray(np.asarray(inputs["L1_attn_pe_w"], np.float32).T),
            "qkvwT": np.ascontiguousarray(np.asarray(inputs["L1_attn_qkv_w"], np.float32).T),
            "owT": np.ascontiguousarray(np.asarray(inputs["L1_attn_out_w"], np.float32).T),
            "taps": taps_t,
            "selL": slv, "selR": srv,
        })
    res = run_bass_kernel_spmd(nc, in_maps, core_ids=list(range(NCORES)))
    out = np.concatenate([res.results[c]["xo"].T for c in range(NCORES)], axis=0)
    return np.ascontiguousarray(out.reshape(2, 8192, D).astype(np.float32))


def kernel(**inputs):
    x = np.asarray(inputs["L0_attn_x"], np.float32)
    s1_args = (inputs["L0_ffn_wqkv"], inputs["L0_ffn_q_scale"],
               inputs["L0_ffn_k_scale"], inputs["L0_ffn_wo"],
               inputs["L0_ffn_ln_g"], inputs["L0_ffn_ln_b"],
               inputs["L0_ffn_w1"], inputs["L0_ffn_b1"],
               inputs["L0_ffn_w2"], inputs["L0_ffn_b2"])
    s4_args = (np.asarray(inputs["L1_ffn_W_r"], np.float32),
               np.asarray(inputs["L1_ffn_W_k"], np.float32),
               np.asarray(inputs["L1_ffn_W_v"], np.float32),
               np.asarray(inputs["L1_ffn_mu_r"], np.float32),
               np.asarray(inputs["L1_ffn_mu_k"], np.float32))
    del x, s1_args, s4_args
    return kernel_fused(inputs)


# version 03
# speedup vs baseline: 1.0017x; 1.0017x over previous
import numpy as np
import ml_dtypes
import concourse.bass as bass
import concourse.mybir as mybir
from concourse.bass_utils import run_bass_kernel_spmd

F32 = mybir.dt.float32
BF16 = mybir.dt.bfloat16
AF = mybir.ActivationFunctionType
ALU = mybir.AluOpType
AX = mybir.AxisListType
BF = ml_dtypes.bfloat16

NCORES = 8
T = 2048
D = 1024
NCH = 8
WIN = 512
NW = T // WIN

ENGINES = ("tensor", "scalar", "vector", "gpsimd", "sync")


class Rec:

    def __init__(self, nc):
        self.nc = nc
        self.steps = {e: [] for e in ENGINES}
        self.counts = {}
        self.gate = ()

    def emit(self, eng, fn, waits=(), inc=None, inc_n=1):
        ws = tuple((m[0], m[1]) for m in tuple(self.gate) + tuple(waits)
                   if m is not None)
        self.steps[eng].append((fn, ws, inc, inc_n))
        if inc is None:
            return None
        self.counts[inc] = self.counts.get(inc, 0) + inc_n
        return (inc, self.counts[inc])

    def dma(self, eng, out, in_, waits=(), sem="ld", **kw):
        return self.emit(eng, lambda e, o=out, i=in_, kw=kw: e.dma_start(out=o, in_=i, **kw),
                         waits=waits, inc=sem, inc_n=16)

    def finalize(self, final_waits):
        nc = self.nc
        self.emit("vector", lambda e: e.engine_nop(), waits=list(final_waits))
        handles = {n: nc.alloc_semaphore(n) for n in self.counts}
        for n, c in self.counts.items():
            assert c < 60000, f"semaphore {n} count {c} too large"
        with nc.Block() as block:
            for eng in ENGINES:
                steps = self.steps[eng]

                def body(e, steps=steps):
                    last = {}
                    for fn, waits, inc, inc_n in steps:
                        for (s, v) in waits:
                            if last.get(s, -1) < v:
                                e.wait_ge(handles[s], v)
                                last[s] = v
                        ins = fn(e)
                        if inc is not None:
                            ins.then_inc(handles[inc], inc_n)
                getattr(block, eng)(body)


def after(*ms):
    return [m for m in ms if m is not None]


class Ctx:
    def __init__(self, nc, rec, arena=None):
        self.nc = nc
        self.rec = rec
        self.buf = {}
        self.arena = arena
        self.ptr = arena[0] if arena else None
        self.pfx = ""
        self.ps_shared = None

    def reset(self, pfx):
        self.ptr = self.arena[0]
        self.pfx = pfx
        self.buf = {}

    def sb(self, name, shape, dtype):
        if self.arena is None:
            t = self.nc.alloc_sbuf_tensor(name, shape, dtype)
        else:
            nb = shape[1] * (4 if dtype == F32 else 2)
            off = (self.ptr + 31) & ~31
            assert off + nb <= self.arena[1], (self.pfx, name, off, nb)
            t = self.nc.alloc_sbuf_tensor_at(self.pfx + name, shape, dtype,
                                             offset=off)
            self.ptr = off + nb
        self.buf[name] = t
        return t

    def ps(self, name, shape, dtype=F32):
        if self.ps_shared is not None:
            self.buf[name] = self.ps_shared
            return self.ps_shared
        t = self.nc.alloc_psum_tensor(name, shape, dtype)
        self.buf[name] = t
        if self.arena is not None:
            self.ps_shared = t
        return t



def emit_ln(ctx, src, dst, width, cast_waits, ones_col_m, ones_row_m,
            g_ap=None, b_ap=None):
    nc, rec = ctx.nc, ctx.rec
    w = width
    XB, XSQ, SROW, PS = ctx.buf["XB"], ctx.buf["XSQ"], ctx.buf["SROW"], ctx.buf["PS"]
    SRB, XAF = ctx.buf["SRB"], ctx.buf["XAF"]
    ONC = ctx.buf["ONC"]
    sx = PS[0:1, 7 * 512:7 * 512 + w]
    sxx = PS[0:1, 6 * 512:6 * 512 + w]
    bA = PS[:, 4 * 512:4 * 512 + w]
    bB = PS[:, 5 * 512:5 * 512 + w]

    m_cast = []
    for c in range(NCH):
        m = rec.emit("vector", lambda e, c=c: e.tensor_copy(
            out=XB[:, c * w:(c + 1) * w], in_=src(c, 0, w)),
            waits=after(*(cast_waits(c) or [])), inc="dv")
        m_cast.append(m)
    m_sx = None
    for c in range(NCH):
        m_sx = rec.emit("tensor", lambda e, c=c: e.matmul(
            sx, ONC[:, 0:1], XB[:, c * w:(c + 1) * w],
            start=(c == 0), stop=(c == NCH - 1)),
            waits=after(m_cast[-1], ones_col_m) if c == 0 else (),
            inc="ts" if c == NCH - 1 else None)
    m_sxx = None
    for c in range(NCH):
        sqm = rec.emit("scalar", lambda e, c=c: e.activation(
            out=XSQ[:, 0:w], in_=XB[:, c * w:(c + 1) * w],
            func=AF.Square),
            waits=after(m_cast[c], m_sxx if c >= 1 else None), inc="ac")
        m_sxx = rec.emit("tensor", lambda e, c=c: e.matmul(
            sxx, ONC[:, 0:1], XSQ[:, 0:w],
            start=(c == 0), stop=(c == NCH - 1)),
            waits=after(sqm), inc="ts")
    m1 = rec.emit("vector", lambda e: e.tensor_scalar(
        out=SROW[0:1, 0:w], in0=sx, scalar1=1.0 / D, scalar2=None, op0=ALU.mult),
        waits=after(m_sx, m_sxx), inc="dv")
    m2 = rec.emit("vector", lambda e: e.tensor_scalar(
        out=SROW[0:1, w:2 * w], in0=sxx, scalar1=1.0 / D, scalar2=None, op0=ALU.mult),
        inc="dv")
    m3 = rec.emit("vector", lambda e: e.tensor_tensor(
        out=SROW[0:1, 3 * w:4 * w], in0=SROW[0:1, 0:w], in1=SROW[0:1, 0:w], op=ALU.mult),
        waits=after(m1), inc="dv")
    m4 = rec.emit("vector", lambda e: e.tensor_tensor(
        out=SROW[0:1, 2 * w:3 * w], in0=SROW[0:1, w:2 * w], in1=SROW[0:1, 3 * w:4 * w],
        op=ALU.subtract), waits=after(m2, m3), inc="dv")
    m_var = rec.emit("vector", lambda e: e.tensor_scalar(
        out=SROW[0:1, 2 * w:3 * w], in0=SROW[0:1, 2 * w:3 * w], scalar1=1e-5,
        scalar2=None, op0=ALU.add), waits=after(m4), inc="dv")
    m_ln2 = rec.emit("scalar", lambda e: e.activation(
        out=SROW[0:1, 2 * w:3 * w], in_=SROW[0:1, 2 * w:3 * w], func=AF.Ln),
        waits=after(m_var), inc="ac")
    m_rstd = rec.emit("scalar", lambda e: e.activation(
        out=SROW[0:1, 2 * w:3 * w], in_=SROW[0:1, 2 * w:3 * w], func=AF.Exp, scale=-0.5),
        waits=after(m_ln2), inc="ac")
    m_B = rec.emit("vector", lambda e: e.scalar_tensor_tensor(
        out=SROW[0:1, 3 * w:4 * w], in0=SROW[0:1, 0:w], scalar=-1.0,
        in1=SROW[0:1, 2 * w:3 * w], op0=ALU.mult, op1=ALU.mult),
        waits=after(m_rstd), inc="dv")
    m_hl = emit_hilo(ctx, SROW[0:1, 2 * w:3 * w], SRB[0:1, 0:w], SRB[0:1, w:2 * w], m_rstd)
    m_hlB = emit_hilo(ctx, SROW[0:1, 3 * w:4 * w], SRB[0:1, 2 * w:3 * w], SRB[0:1, 3 * w:4 * w], m_B)
    m_Ab = emit_bcast(ctx, bA, SRB[0:1, 0:w], SRB[0:1, w:2 * w], after(m_hl, ones_row_m))
    m_Bb = emit_bcast(ctx, bB, SRB[0:1, 2 * w:3 * w], SRB[0:1, 3 * w:4 * w], after(m_hlB))
    m_last = None
    for c in range(NCH):
        mm1 = rec.emit("vector", lambda e, c=c: e.tensor_tensor(
            out=XAF[:, 0:w], in0=src(c, 0, w), in1=bA, op=ALU.mult),
            waits=after(m_Ab if c == 0 else None, m_last), inc="dv")
        if g_ap is None:
            m_last = rec.emit("vector", lambda e, c=c: e.tensor_tensor(
                out=dst(c, 0, w), in0=XAF[:, 0:w], in1=bB, op=ALU.add),
                waits=after(m_Bb if c == 0 else None, mm1), inc="dv")
        else:
            XAF2 = ctx.buf["XAF2"]
            mmid = rec.emit("vector", lambda e: e.tensor_tensor(
                out=XAF2[:, 0:w], in0=XAF[:, 0:w], in1=bB, op=ALU.add),
                waits=after(m_Bb if c == 0 else None, mm1), inc="dv")
            m_last = rec.emit("vector", lambda e, c=c, s1=g_ap(c), s2=b_ap(c): e.tensor_scalar(
                out=dst(c, 0, w), in0=XAF2[:, 0:w], scalar1=s1, scalar2=s2,
                op0=ALU.mult, op1=ALU.add), waits=after(mmid), inc="dv")
    return m_last


def emit_hilo(ctx, row_f32, hi_bf, lo_bf, dep):
    rec = ctx.rec
    mh = rec.emit("vector", lambda e: e.tensor_copy(out=hi_bf, in_=row_f32),
                  waits=after(dep), inc="dv")
    return rec.emit("vector", lambda e: e.tensor_tensor(
        out=lo_bf, in0=row_f32, in1=hi_bf, op=ALU.subtract), waits=after(mh), inc="dv")


def emit_bcast(ctx, psum_dst, hi_bf, lo_bf, waits):
    rec = ctx.rec
    ONR = ctx.buf["ONR"]
    rec.emit("tensor", lambda e: e.matmul(
        psum_dst, ONR[0:1, 0:128], hi_bf, start=True, stop=False),
        waits=waits, inc=None)
    return rec.emit("tensor", lambda e: e.matmul(
        psum_dst, ONR[0:1, 0:128], lo_bf, start=False, stop=True), inc="ts")



def emit_s4(ctx, x_in, xo, wrT, wkT, wvT, mur, muk, onc, onr):
    nc, rec = ctx.nc, ctx.rec
    TX = T + 1
    X = ctx.sb("X", [128, NCH * TX], F32)
    XH = ctx.sb("XH", [128, NCH * (WIN + 1)], BF16)
    XB = ctx.sb("XB", [128, NCH * WIN], BF16)
    XSQ = ctx.sb("XSQ", [128, WIN], BF16)
    XR = ctx.sb("XR", [128, NCH * WIN], BF16)
    XK = ctx.sb("XK", [128, NCH * WIN], BF16)
    K2 = ctx.sb("K2", [128, NCH * WIN], BF16)
    R = ctx.sb("R", [128, NCH * WIN], BF16)
    TMP = ctx.sb("TMP", [128, NCH * WIN], BF16)
    TMPF = ctx.sb("TMPF", [128, 2 * WIN], F32)
    WR = ctx.sb("WR", [128, NCH * D], BF16)
    WK = ctx.sb("WK", [128, NCH * D], BF16)
    WV = ctx.sb("WV", [128, NCH * D], BF16)
    MU = ctx.sb("MU", [128, 4 * NCH], F32)
    ONC = ctx.sb("ONC", [128, 1], BF16)
    ONR = ctx.sb("ONR", [1, 128], BF16)
    SROW = ctx.sb("SROW", [1, 4 * WIN + 16], F32)
    SRB = ctx.sb("SRB", [1, 4 * WIN + 16], BF16)
    XAF = ctx.sb("XAF", [128, WIN], F32)
    PS = ctx.ps("PS", [128, 4096], F32)
    bank = lambda b: PS[:, b * 512:b * 512 + WIN]

    m_x = rec.dma("sync", X[:], x_in.rearrange("(c p) t -> p c t", p=128), sem="ld_x")
    m_wr = rec.dma("gpsimd", WR[:], wrT.rearrange("(k p) m -> p k m", p=128), sem="ld_wr")
    m_wk = rec.dma("gpsimd", WK[:], wkT.rearrange("(k p) m -> p k m", p=128), sem="ld_wk")
    m_wv = rec.dma("gpsimd", WV[:], wvT.rearrange("(k p) m -> p k m", p=128), sem="ld_wv")
    rec.dma("sync", MU[:, 0:NCH], mur, sem="ld_mu")
    m_mu = rec.dma("sync", MU[:, NCH:2 * NCH], muk, sem="ld_mu")
    m_onc = rec.dma("gpsimd", ONC[:], onc, sem="ld_onc")
    m_onr = rec.dma("gpsimd", ONR[:], onr, sem="ld_onr")
    m_mu1m = rec.emit("vector", lambda e: e.tensor_scalar(
        out=MU[:, 2 * NCH:4 * NCH], in0=MU[:, 0:2 * NCH],
        scalar1=-1.0, scalar2=1.0, op0=ALU.mult, op1=ALU.add),
        waits=after(m_mu), inc="dv")

    Xs = lambda c, a, b: X[:, c * TX + a:c * TX + b]
    XHs = lambda c, a, b: XH[:, c * (WIN + 1) + a:c * (WIN + 1) + b]

    m_resid = [None] * NCH
    m_xh_last = None
    m_store = None
    for w in range(NW):
        c0 = 1 + w * WIN
        carry = []
        if w > 0:
            for c in range(NCH):
                mcp = rec.emit("vector", lambda e, c=c: e.tensor_copy(
                    out=XHs(c, 0, 1), in_=XHs(c, WIN, WIN + 1)),
                    waits=after(m_xh_last), inc="dv")
                carry.append(mcp)
        if w == 0:
            cw = lambda c: [m_x, m_resid[c]]
        else:
            cw = lambda c, carry=carry: [carry[c], m_resid[c]]
        m_xh_last = emit_ln(ctx, lambda c, a, b, c0=c0: Xs(c, c0 + a, c0 + b),
                            lambda c, a, b: XHs(c, 1 + a, 1 + b),
                            WIN, cw, m_onc, m_onr)
        if w == 0:
            m_xh_last = emit_ln(ctx, lambda c, a, b: Xs(c, a, b),
                                lambda c, a, b: XHs(c, a, b),
                                1, lambda c, m=m_xh_last: [m], m_onc, m_onr)
        m_mix_last = None
        for c in range(NCH):
            for (dst, mu_i) in ((XR, 0), (XK, 1)):
                rec.emit("vector", lambda e, c=c, mu_i=mu_i: e.tensor_scalar(
                    out=TMP[:, c * WIN:(c + 1) * WIN], in0=XHs(c, 0, WIN),
                    scalar1=MU[:, (2 + mu_i) * NCH + c:(2 + mu_i) * NCH + c + 1],
                    scalar2=None, op0=ALU.mult),
                    waits=after(m_xh_last, m_mu1m) if c == 0 and mu_i == 0 else (),
                    inc="dv")
                m_mix_last = rec.emit("vector", lambda e, c=c, mu_i=mu_i, dst=dst: e.scalar_tensor_tensor(
                    out=dst[:, c * WIN:(c + 1) * WIN], in0=XHs(c, 1, WIN + 1),
                    scalar=MU[:, mu_i * NCH + c:mu_i * NCH + c + 1],
                    in1=TMP[:, c * WIN:(c + 1) * WIN], op0=ALU.mult, op1=ALU.add),
                    inc="dv")
        m_evac = {}
        for m in range(NCH):
            bk = m % 2
            mm = None
            for k in range(NCH):
                mm = rec.emit("tensor", lambda e, m=m, k=k, bk=bk: e.matmul(
                    bank(bk), WR[:, k * D + m * 128:k * D + (m + 1) * 128],
                    XR[:, k * WIN:(k + 1) * WIN], start=(k == 0), stop=(k == NCH - 1)),
                    waits=after(m_mix_last if m == 0 and k == 0 else None,
                                m_wr if m == 0 and k == 0 else None,
                                m_evac.get(("r", m - 2)) if k == 0 else None),
                    inc="ts" if k == NCH - 1 else None)
            m_evac[("r", m)] = rec.emit("scalar", lambda e, m=m, bk=bk: e.activation(
                out=R[:, m * WIN:(m + 1) * WIN], in_=bank(bk), func=AF.Sigmoid),
                waits=after(mm), inc="ac")
        m_sq_last = None
        for m in range(NCH):
            bk = 2 + m % 2
            mm = None
            for k in range(NCH):
                mm = rec.emit("tensor", lambda e, m=m, k=k, bk=bk: e.matmul(
                    bank(bk), WK[:, k * D + m * 128:k * D + (m + 1) * 128],
                    XK[:, k * WIN:(k + 1) * WIN], start=(k == 0), stop=(k == NCH - 1)),
                    waits=after(m_wk if m == 0 and k == 0 else None,
                                m_evac.get(("k", m - 2)) if k == 0 else None),
                    inc="ts" if k == NCH - 1 else None)
            mr = rec.emit("vector", lambda e, m=m, bk=bk: e.tensor_scalar_max(
                out=TMP[:, m * WIN:(m + 1) * WIN], in0=bank(bk), scalar1=0.0),
                waits=after(mm), inc="dv")
            m_evac[("k", m)] = mr
            m_sq_last = rec.emit("scalar", lambda e, m=m: e.activation(
                out=K2[:, m * WIN:(m + 1) * WIN], in_=TMP[:, m * WIN:(m + 1) * WIN],
                func=AF.Square), waits=after(mr), inc="ac")
        for m in range(NCH):
            bk = 4 + m % 2
            mm = None
            for k in range(NCH):
                mm = rec.emit("tensor", lambda e, m=m, k=k, bk=bk: e.matmul(
                    bank(bk), WV[:, k * D + m * 128:k * D + (m + 1) * 128],
                    K2[:, k * WIN:(k + 1) * WIN], start=(k == 0), stop=(k == NCH - 1)),
                    waits=after(m_sq_last if m == 0 and k == 0 else None,
                                m_wv if m == 0 and k == 0 else None,
                                m_evac.get(("v", m - 2)) if k == 0 else None),
                    inc="ts" if k == NCH - 1 else None)
            mt = rec.emit("vector", lambda e, m=m, bk=bk: e.tensor_tensor(
                out=TMPF[:, (m % 2) * WIN:(m % 2) * WIN + WIN], in0=bank(bk),
                in1=R[:, m * WIN:(m + 1) * WIN], op=ALU.mult),
                waits=after(mm), inc="dv")
            m_evac[("v", m)] = mt
            m_resid[m] = rec.emit("vector", lambda e, m=m, c0=c0: e.scalar_tensor_tensor(
                out=Xs(m, c0, c0 + WIN), in0=TMPF[:, (m % 2) * WIN:(m % 2) * WIN + WIN],
                scalar=0.5, in1=Xs(m, c0, c0 + WIN), op0=ALU.mult, op1=ALU.add),
                inc="dv")
        if xo is not None and w == NW - 1:
            xo_r = xo.rearrange("(c p) t -> p c t", p=128)
            for c in range(NCH):
                m_store = rec.dma("sync", xo_r[:, c, 0:T], Xs(c, 1, 1 + T),
                                  waits=after(*m_resid), sem="st")
    return [m_store]


def build_s4(nc):
    rec = Rec(nc)
    ctx = Ctx(nc, rec)
    TX = T + 1
    x_in = nc.dram_tensor("x", [D, TX], F32, kind="ExternalInput").ap()
    wrT = nc.dram_tensor("wrT", [D, D], F32, kind="ExternalInput").ap()
    wkT = nc.dram_tensor("wkT", [D, D], F32, kind="ExternalInput").ap()
    wvT = nc.dram_tensor("wvT", [D, D], F32, kind="ExternalInput").ap()
    mur = nc.dram_tensor("mur", [128, NCH], F32, kind="ExternalInput").ap()
    muk = nc.dram_tensor("muk", [128, NCH], F32, kind="ExternalInput").ap()
    onc = nc.dram_tensor("ones_col", [128, 1], F32, kind="ExternalInput").ap()
    onr = nc.dram_tensor("ones_row", [1, 128], F32, kind="ExternalInput").ap()
    xo = nc.dram_tensor("xo", [D, T], F32, kind="ExternalOutput").ap()
    mil = emit_s4(ctx, x_in, xo, wrT, wkT, wvT, mur, muk, onc, onr)
    rec.finalize(mil)
    return nc


NHEAD = 16
NTB = T // 128
F2 = 128


def emit_s2(ctx, x_in, xo, wq, wk, wv, wo, kqp, kkp, sel, onc, onr, tri, ident, stage=100):
    nc, rec = ctx.nc, ctx.rec

    cc_in = nc.dram_tensor("cc_in", [128, NHEAD * 65], F32)
    cc_out = nc.dram_tensor("cc_out", [NCORES * 128, NHEAD * 65], F32)
    o_dram = nc.dram_tensor("o_dram", [D, T], BF16, kind="ExternalOutput")

    XWIN = ctx.sb("XWIN", [128, NCH * WIN], F32)
    XH = ctx.sb("XH2", [128, NCH * T], BF16)
    V = ctx.sb("V", [128, NTB * D], BF16)
    WQ = ctx.sb("WQ", [128, NCH * D], BF16)
    WK = ctx.sb("WK2", [128, NCH * D], BF16)
    WX = ctx.sb("WX", [128, NCH * D], BF16)
    KQT = ctx.sb("KQT", [128, NHEAD * F2], BF16)
    KKT = ctx.sb("KKT", [128, NHEAD * F2], BF16)
    SEL = ctx.sb("SEL", [128, NCORES], F32)
    QP = ctx.sb("QP", [128, T], BF16)
    KP = ctx.sb("KP", [128, T], BF16)
    QF = ctx.sb("QF", [128, T], BF16)
    KFF = ctx.sb("KFF", [128, T], BF16)
    KFTP = ctx.sb("KFTP", [128, NTB * 2 * F2], BF16)
    VAP = ctx.sb("VAP", [128, NTB * 2 * 65], BF16)
    ZK = ctx.sb("ZK", [128, 4 * NTB], F32)
    OST = ctx.sb("OST", [128, NTB * 128], F32)
    OTB = ctx.sb("OTB", [128, NTB * 128], BF16)
    IDTF = ctx.sb("IDTF", [128, 128], F32)
    PT = ctx.sb("PT", [128, F2], BF16)
    STSB = ctx.sb("STSB", [128, 65], BF16)
    ZI = ctx.sb("ZI", [128, 4], F32)
    AGS = ctx.sb("AGS", [128, NHEAD * 65], F32)
    PRE = AGS
    PREB = ctx.sb("PREB", [128, NHEAD * 65], BF16)
    TRI = ctx.sb("TRI", [128, 128], BF16)
    IDT = ctx.sb("IDT", [128, 128], BF16)
    XB = ctx.sb("XB", [128, NCH * WIN], BF16)
    XSQ = ctx.sb("XSQ", [128, WIN], BF16)
    XAF = ctx.sb("XAF", [128, WIN], F32)
    SROW = ctx.sb("SROW", [1, 4 * WIN + 16], F32)
    SRB = ctx.sb("SRB", [1, 4 * WIN + 16], BF16)
    ONC = ctx.sb("ONC", [128, 1], BF16)
    ONR = ctx.sb("ONR", [1, 128], BF16)
    PS = ctx.ps("PS", [128, 4096], F32)
    bank = lambda b, wdt=512: PS[:, b * 512:b * 512 + wdt]
    OFP = XB
    PRT = XWIN

    m_wq = rec.dma("gpsimd", WQ[:], wq.rearrange("(k p) m -> p k m", p=128), sem="ld_wq")
    m_wk = rec.dma("gpsimd", WK[:], wk.rearrange("(k p) m -> p k m", p=128), sem="ld_wk")
    m_wv = rec.dma("gpsimd", WX[:], wv.rearrange("(k p) m -> p k m", p=128), sem="ld_wv")
    m_kq = rec.dma("gpsimd", KQT[:], kqp.rearrange("(h p) f -> p h f", p=128), sem="ld_kq")
    m_kk = rec.dma("gpsimd", KKT[:], kkp.rearrange("(h p) f -> p h f", p=128), sem="ld_kk")
    m_sel = rec.dma("sync", SEL[:], sel, sem="ld_sel")
    m_onc = rec.dma("gpsimd", ONC[:], onc, sem="ld_onc")
    m_onr = rec.dma("gpsimd", ONR[:], onr, sem="ld_onr")
    m_tri = rec.dma("gpsimd", TRI[:], tri, sem="ld_tri")
    m_idt = rec.dma("gpsimd", IDT[:], ident, sem="ld_idt")
    m_idtf = rec.dma("sync", IDTF[:], ident, sem="ld_idtf")
    m_neg = None
    for h in range(NHEAD):
        for KT in (KQT, KKT):
            m_neg = rec.emit("vector", lambda e, h=h, KT=KT: e.tensor_scalar(
                out=KT[:, h * F2 + 64:h * F2 + 128], in0=KT[:, h * F2:h * F2 + 64],
                scalar1=-1.0, scalar2=None, op0=ALU.mult),
                waits=after(m_kq, m_kk) if h == 0 else (), inc="dv")

    m_ln = None
    for w in range(NW):
        m_xw = rec.dma("sync", XWIN[:],
                       x_in.rearrange("(c p) t -> p c t", p=128)[:, :, w * WIN:(w + 1) * WIN],
                       waits=after(m_ln), sem="ld_xw")
        m_ln = emit_ln(ctx, lambda c, a, b: XWIN[:, c * WIN + a:c * WIN + b],
                       lambda c, a, b, w=w: XH[:, c * T + w * WIN + a:c * T + w * WIN + b],
                       WIN, lambda c, m_xw=m_xw: [m_xw], m_onc, m_onr)
    if stage <= 1:
        ms = rec.dma("gpsimd", xo.rearrange("(c p) t -> p c t", p=128), XH[:],
                     waits=after(m_ln), sem="st")
        return [ms]

    m_prev_evac = [None, None]
    m_v = None
    for tb in range(NTB):
        mm = [None, None]
        for k in range(NCH):
            for half in range(2):
                mm[half] = rec.emit("tensor", lambda e, tb=tb, k=k, half=half: e.matmul(
                    bank(half), XH[:, k * T + tb * 128:k * T + (tb + 1) * 128],
                    WX[:, k * D + half * 512:k * D + (half + 1) * 512],
                    start=(k == 0), stop=(k == NCH - 1)),
                    waits=after(m_ln if tb == 0 and k == 0 and half == 0 else None,
                                m_wv if tb == 0 and k == 0 and half == 0 else None,
                                m_prev_evac[half] if k == 0 else None),
                    inc="ts" if k == NCH - 1 else None)
        for half in range(2):
            m_v = rec.emit("vector", lambda e, tb=tb, half=half: e.tensor_copy(
                out=V[:, tb * D + half * 512:tb * D + (half + 1) * 512], in_=bank(half)),
                waits=after(mm[half]), inc="dv")
            m_prev_evac[half] = m_v
    m_v_done = m_v
    if stage <= 2:
        ms = rec.dma("gpsimd", xo.rearrange("(c p) t -> p c t", p=128)[:, 0:8, :],
                     V[:, 0:8 * 2048], waits=after(m_v_done), sem="st")
        return [ms]

    def proj_pair(p, Wsb, m_w, OUT, m_prev_use):
        m_ev = None
        for w in range(NW):
            mm = None
            for k in range(NCH):
                mm = rec.emit("tensor", lambda e, p=p, w=w, k=k, Wsb=Wsb: e.matmul(
                    bank(2), Wsb[:, k * D + p * 128:k * D + (p + 1) * 128],
                    XH[:, k * T + w * WIN:k * T + (w + 1) * WIN],
                    start=(k == 0), stop=(k == NCH - 1)),
                    waits=after(m_w if w == 0 and k == 0 else None,
                                m_prev_use if w == 0 and k == 0 else None,
                                m_ev if k == 0 else None),
                    inc="ts" if k == NCH - 1 else None)
            m_ev = rec.emit("vector", lambda e, w=w, OUT=OUT: e.tensor_copy(
                out=OUT[:, w * WIN:(w + 1) * WIN], in_=bank(2)),
                waits=after(mm), inc="dv")
        return m_ev

    def pair_features(p, m_kp, m_prev_done):
        m_e = None
        m_prev_exp = m_prev_done
        for tb in range(NTB):
            mm = None
            for half, hd in ((0, p * 2), (1, p * 2 + 1)):
                mm = rec.emit("tensor", lambda e, tb=tb, half=half, hd=hd: e.matmul(
                    PS[:, 3 * 512 + half * 128:3 * 512 + half * 128 + 128],
                    KP[:, tb * 128:(tb + 1) * 128], KKT[:, hd * F2:(hd + 1) * F2],
                    start=True, stop=True),
                    waits=after(m_kp if tb == 0 and half == 0 else None,
                                m_neg if tb == 0 and half == 0 else None,
                                m_prev_exp if half == 0 else None),
                    inc="ts" if half == 1 else None)
            m_exp = rec.emit("scalar", lambda e, tb=tb: e.activation(
                out=KFTP[:, tb * 2 * F2:(tb + 1) * 2 * F2], in_=PS[:, 3 * 512:3 * 512 + 256],
                func=AF.Exp), waits=after(mm), inc="ac")
            m_prev_exp = m_exp
            m_zk = rec.emit("vector", lambda e, tb=tb: e.reduce_sum(
                out=ZK[:, 2 * tb:2 * tb + 2],
                in_=KFTP.ap().rearrange("p (tb s f) -> p tb s f", tb=NTB, s=2)[:, tb],
                axis=AX.X), waits=after(m_exp), inc="dv")
            m_zki = rec.emit("vector", lambda e, tb=tb: e.reciprocal(
                out=ZK[:, 2 * NTB + 2 * tb:2 * NTB + 2 * tb + 2], in_=ZK[:, 2 * tb:2 * tb + 2]),
                waits=after(m_zk), inc="dv")
            for half, hd in ((0, p * 2), (1, p * 2 + 1)):
                m1 = rec.emit("vector", lambda e, tb=tb, half=half, hd=hd: e.tensor_scalar(
                    out=VAP[:, tb * 130 + half * 65:tb * 130 + half * 65 + 64],
                    in0=V[:, tb * D + hd * 64:tb * D + (hd + 1) * 64],
                    scalar1=ZK[:, 2 * NTB + 2 * tb + half:2 * NTB + 2 * tb + half + 1],
                    scalar2=None, op0=ALU.mult),
                    waits=after(m_zki, m_v_done if tb == 0 and half == 0 else None), inc="dv")
                m_e = rec.emit("vector", lambda e, tb=tb, half=half: e.tensor_copy(
                    out=VAP[:, tb * 130 + half * 65 + 64:tb * 130 + half * 65 + 65],
                    in_=ZK[:, 2 * NTB + 2 * tb + half:2 * NTB + 2 * tb + half + 1]),
                    waits=after(m1), inc="dv")
        return m_e

    def head_fm_feats(h, SRC, msrc, KT, OUT, bk, m_prev):
        m_q = m_prev
        for w in range(NW):
            mq = rec.emit("tensor", lambda e, h=h, w=w, bk=bk, KT=KT, SRC=SRC: e.matmul(
                bank(bk, 512), KT[:, h * F2:(h + 1) * F2], SRC[:, w * WIN:(w + 1) * WIN],
                start=True, stop=True),
                waits=after(msrc if w == 0 else None, m_neg if w == 0 else None, m_q),
                inc="ts")
            m_q = rec.emit("scalar", lambda e, w=w, bk=bk, OUT=OUT: e.activation(
                out=OUT[:, w * WIN:(w + 1) * WIN], in_=bank(bk, 512), func=AF.Exp),
                waits=after(mq), inc="ac")
        return m_q

    m_pair_use = m_v_done
    m_tot_evac = None
    for p in range(NCH):
        m_kp = proj_pair(p, WK, m_wk, KP, m_pair_use)
        m_va = pair_features(p, m_kp, m_pair_use)
        for hh in range(2):
            h = 2 * p + hh
            m_upd = None
            for tb in range(NTB):
                m_upd = rec.emit("tensor", lambda e, tb=tb, hh=hh: e.matmul(
                    PS[:, 0:65], KFTP[:, tb * 2 * F2 + hh * F2:tb * 2 * F2 + (hh + 1) * F2],
                    VAP[:, tb * 130 + hh * 65:tb * 130 + (hh + 1) * 65],
                    start=(tb == 0), stop=(tb == NTB - 1)),
                    waits=after(m_va if tb == 0 else None,
                                m_tot_evac if tb == 0 else None),
                    inc="ts" if tb == NTB - 1 else None)
            m_tot_evac = rec.emit("vector", lambda e, h=h: e.tensor_copy(
                out=AGS[:, h * 65:(h + 1) * 65], in_=PS[:, 0:65]),
                waits=after(m_upd), inc="dv")
        m_pair_use = m_tot_evac
    m_agin = rec.dma("gpsimd", cc_in.ap(), AGS[:], waits=after(m_tot_evac), sem="cc_in")
    m_ag = rec.emit("gpsimd", lambda e: e.collective_compute(
        "AllGather", ALU.bypass, replica_groups=[list(range(NCORES))],
        ins=[cc_in.ap()], outs=[cc_out.ap()]),
        waits=after(m_agin), inc="cc")
    m_pre = rec.emit("vector", lambda e: e.memset(PRE[:], 0.0), waits=after(m_agin), inc="dv")
    m_pre_hist = [None, None]
    for r in range(NCORES):
        m_prt = rec.dma("sync", PRT[:, (r % 2) * NHEAD * 65:(r % 2 + 1) * NHEAD * 65],
                        cc_out.ap()[r * 128:(r + 1) * 128, :],
                        waits=after(m_ag, m_pre_hist[r % 2]), sem="ld_prt%d" % (r % 2))
        m_pre = rec.emit("vector", lambda e, r=r: e.scalar_tensor_tensor(
            out=PRE[:], in0=PRT[:, (r % 2) * NHEAD * 65:(r % 2 + 1) * NHEAD * 65],
            scalar=SEL[:, r:r + 1], in1=PRE[:], op0=ALU.mult, op1=ALU.add),
            waits=after(m_prt, m_pre, m_sel), inc="dv")
        m_pre_hist[r % 2] = m_pre
    m_preb = rec.emit("vector", lambda e: e.tensor_copy(out=PREB[:], in_=PRE[:]),
                      waits=after(m_pre), inc="dv")
    if stage <= 3:
        ms = rec.dma("sync", xo.rearrange("(c p) t -> p c t", p=128)[:, 0, 0:NHEAD * 65],
                     PRE[:], waits=after(m_pre), sem="st")
        return [ms]

    m_ostore = []
    m_store_prev = None
    for p in range(NCH):
        m_kp = proj_pair(p, WK, m_wk, KP, m_pair_use)
        m_qp = proj_pair(p, WQ, m_wq, QP, m_kp)
        m_va = pair_features(p, m_kp, m_pair_use)
        if stage == 33:
            m_ostore.append(m_va)
            m_pair_use = m_va
            continue
        for hh in range(2):
            h = 2 * p + hh
            m_qf = head_fm_feats(h, QP, m_qp, KQT, QF, 5,
                                 m_ostore[-1] if m_ostore else None)
            m_kff = head_fm_feats(h, KP, m_kp, KKT, KFF, 6,
                                  m_ostore[-1] if m_ostore else None)
            if stage == 34:
                m_ostore.append(m_kff)
                continue
            m_init = rec.emit("tensor", lambda e, h=h: e.matmul(
                PS[:, 0:65], IDT[:], PREB[:, h * 65:(h + 1) * 65],
                start=True, stop=False),
                waits=after(m_preb, m_idt, m_va, m_qf, m_kff,
                            m_ostore[-1] if m_ostore else None), inc="ts")
            m_state = m_init
            m_sev = None
            m_pt = None
            m_ost = None
            for tb in range(NTB):
                if stage >= 37:
                    m_sev = rec.emit("scalar", lambda e: e.activation(
                        out=STSB[:], in_=PS[:, 0:65], func=AF.Copy),
                        waits=after(m_state, m_ost), inc="ac")
                m_smm = rec.emit("tensor", lambda e, tb=tb: e.matmul(
                    PS[:, 2 * 512:2 * 512 + 128], KFF[:, tb * 128:(tb + 1) * 128],
                    QF[:, tb * 128:(tb + 1) * 128], start=True, stop=True),
                    waits=after(m_pt), inc="ts")
                m_pt = rec.emit("vector", lambda e: e.tensor_tensor(
                    out=PT[:], in0=PS[:, 2 * 512:2 * 512 + 128], in1=TRI[:], op=ALU.mult),
                    waits=after(m_smm, m_tri), inc="dv")
                if stage >= 38:
                    m_o1 = rec.emit("tensor", lambda e, tb=tb, hh=hh: e.matmul(
                        PS[:, 1 * 512:1 * 512 + 65], PT[:],
                        VAP[:, tb * 130 + hh * 65:tb * 130 + hh * 65 + 65],
                        start=True, stop=False),
                        waits=after(m_pt, m_ost), inc=None)
                    m_o2 = rec.emit("tensor", lambda e, tb=tb: e.matmul(
                        PS[:, 1 * 512:1 * 512 + 65], QF[:, tb * 128:(tb + 1) * 128], STSB[:],
                        start=False, stop=True),
                        waits=after(m_sev), inc="ts")
                m_state = rec.emit("tensor", lambda e, tb=tb, hh=hh: e.matmul(
                    PS[:, 0:65], KFTP[:, tb * 2 * F2 + hh * F2:tb * 2 * F2 + (hh + 1) * F2],
                    VAP[:, tb * 130 + hh * 65:tb * 130 + hh * 65 + 65],
                    start=False, stop=(tb == NTB - 1)),
                    waits=after(m_sev), inc="ts")
                if stage >= 38:
                    m_z1 = rec.emit("vector", lambda e: e.tensor_scalar(
                        out=ZI[:, 0:1], in0=PS[:, 1 * 512 + 64:1 * 512 + 65], scalar1=1e-30,
                        scalar2=None, op0=ALU.add), waits=after(m_o2), inc="dv")
                    m_z2 = rec.emit("vector", lambda e: e.reciprocal(
                        out=ZI[:, 1:2], in_=ZI[:, 0:1]), waits=after(m_z1), inc="dv")
                    m_ost = rec.emit("vector", lambda e, tb=tb, hh=hh: e.tensor_scalar(
                        out=OST[:, tb * 128 + hh * 64:tb * 128 + hh * 64 + 64],
                        in0=PS[:, 1 * 512:1 * 512 + 64],
                        scalar1=ZI[:, 1:2], scalar2=None, op0=ALU.mult),
                        waits=after(m_z2), inc="dv")
            m_ostore.append(m_ost if m_ost is not None else m_state)
        if stage not in (35, 36, 37):
            m_tc = None
            for tb in range(NTB):
                mtr = rec.emit("tensor", lambda e, tb=tb: e.transpose(
                    PS[:, 2 * 512:2 * 512 + 128], OST[:, tb * 128:(tb + 1) * 128],
                    IDTF[:]),
                    waits=after(m_ostore[-1] if tb == 0 else None,
                                m_idtf if tb == 0 else None, m_tc), inc="ts")
                m_tc = rec.emit("vector", lambda e, tb=tb: e.tensor_copy(
                    out=OTB[:, tb * 128:(tb + 1) * 128], in_=PS[:, 2 * 512:2 * 512 + 128]),
                    waits=after(mtr, m_store_prev if tb == 0 else None), inc="dv")
            if stage == 39:
                m_ostore.append(m_tc)
            else:
                m_store_prev = rec.dma("gpsimd", o_dram.ap()[p * 128:(p + 1) * 128, :], OTB[:],
                                       waits=after(m_tc), sem="st_o%d" % p)
                m_ostore.append(m_tc)
        m_pair_use = m_ostore[-1]
    stws = [("st_o%d" % p, 16) for p in range(NCH) if ("st_o%d" % p) in rec.counts]
    if stage <= 4 or stage in (33, 34, 35, 36, 37, 39):
        ms = rec.dma("gpsimd", xo.rearrange("(c p) t -> p c t", p=128)[:, 0:1, 0:NTB * 128],
                     OST[:, 0:NTB * 128], waits=after(m_ostore[-1], *stws), sem="st")
        return [ms]

    if stage == 48:
        ms = rec.dma("gpsimd", xo, o_dram.ap(), waits=after(m_ostore[-1], *stws), sem="st")
        return [ms]
    m_wo = rec.dma("gpsimd", WX[:], wo.rearrange("(k p) m -> p k m", p=128),
                   waits=after(m_v_done, m_pair_use), sem="ld_wo")
    m_store = None
    m_resid_prev = [None] * NCH
    for w in range(NW if stage != 52 else 1):
        m_xw2 = rec.dma("sync", XWIN[:],
                        x_in.rearrange("(c p) t -> p c t", p=128)[:, :, w * WIN:(w + 1) * WIN],
                        waits=after(m_ln if w == 0 else None,
                                    m_pre if w == 0 else None,
                                    ("st%d" % ((w - 1) % 2), rec.counts.get("st%d" % ((w - 1) % 2), 0)) if w > 0 else None,
                                    *(m_resid_prev if w > 0 else [])),
                        sem="ld_xw2")
        m_pan = []
        for k in range(NCH):
            mp = rec.dma("sync", OFP[:, k * WIN:(k + 1) * WIN],
                         o_dram.ap()[k * 128:(k + 1) * 128, w * WIN:(w + 1) * WIN],
                         waits=after(*(stws if w == 0 and k == 0 else []),
                                     m_ln if w == 0 and k == 0 else None,
                                     *(m_resid_prev if w > 0 and k == 0 else [])),
                         sem="ld_ofp%d" % (w % 2))
            m_pan.append(mp)
        if stage == 49 and w == 0:
            ms1 = rec.dma("gpsimd", xo.rearrange("(c p) t -> p c t", p=128)[:, :, 0:WIN],
                          OFP[:], waits=after(m_pan[-1]), sem="st0")
            ms2 = rec.dma("sync", xo.rearrange("(c p) t -> p c t", p=128)[:, :, WIN:2 * WIN],
                          XWIN[:], waits=after(m_xw2), sem="st0")
            return [ms1, ms2]
        for m in range(NCH):
            bk = m % 2
            mm = None
            for k in range(NCH):
                mm = rec.emit("tensor", lambda e, m=m, k=k, bk=bk: e.matmul(
                    bank(bk), WX[:, k * D + m * 128:k * D + (m + 1) * 128],
                    OFP[:, k * WIN:(k + 1) * WIN], start=(k == 0), stop=(k == NCH - 1)),
                    waits=after(m_pan[-1] if m == 0 and k == 0 else None,
                                m_wo if m == 0 and k == 0 else None,
                                m_resid_prev[m - 2] if k == 0 and m >= 2 else None),
                    inc="ts" if k == NCH - 1 else None)
            if stage == 50:
                m_res = rec.emit("vector", lambda e, m=m, bk=bk: e.tensor_copy(
                    out=XWIN[:, m * WIN:(m + 1) * WIN], in_=bank(bk)),
                    waits=after(mm, m_xw2), inc="dv")
            else:
                m_res = rec.emit("vector", lambda e, m=m, bk=bk: e.scalar_tensor_tensor(
                    out=XWIN[:, m * WIN:(m + 1) * WIN], in0=bank(bk), scalar=1.0,
                    in1=XWIN[:, m * WIN:(m + 1) * WIN], op0=ALU.mult, op1=ALU.add),
                    waits=after(mm, m_xw2), inc="dv")
            m_resid_prev[m] = m_res
            m_store = rec.dma("sync", xo.rearrange("(c p) t -> p c t", p=128)[:, m, w * WIN:(w + 1) * WIN],
                              XWIN[:, m * WIN:(m + 1) * WIN], waits=after(m_res), sem="st%d" % (w % 2))
    return [(s, rec.counts[s]) for s in ("st0", "st1") if s in rec.counts]


def build_s2(nc, stage=100):
    rec = Rec(nc)
    ctx = Ctx(nc, rec)
    x_in = nc.dram_tensor("x", [D, T], F32, kind="ExternalInput").ap()
    wq = nc.dram_tensor("wq", [D, D], F32, kind="ExternalInput").ap()
    wk = nc.dram_tensor("wk", [D, D], F32, kind="ExternalInput").ap()
    wv = nc.dram_tensor("wv", [D, D], F32, kind="ExternalInput").ap()
    wo = nc.dram_tensor("wo", [D, D], F32, kind="ExternalInput").ap()
    kqp = nc.dram_tensor("kqp", [NHEAD * 128, 128], F32, kind="ExternalInput").ap()
    kkp = nc.dram_tensor("kkp", [NHEAD * 128, 128], F32, kind="ExternalInput").ap()
    sel = nc.dram_tensor("sel", [128, NCORES], F32, kind="ExternalInput").ap()
    onc = nc.dram_tensor("ones_col", [128, 1], F32, kind="ExternalInput").ap()
    onr = nc.dram_tensor("ones_row", [1, 128], F32, kind="ExternalInput").ap()
    tri = nc.dram_tensor("tri", [128, 128], F32, kind="ExternalInput").ap()
    ident = nc.dram_tensor("ident", [128, 128], F32, kind="ExternalInput").ap()
    xo = nc.dram_tensor("xo", [D, T], F32, kind="ExternalOutput").ap()
    mil = emit_s2(ctx, x_in, xo, wq, wk, wv, wo, kqp, kkp, sel, onc, onr, tri, ident, stage=stage)
    rec.finalize(mil)
    return nc


def s2_host_inputs(Kq, Kk):
    kqp = np.zeros((NHEAD * 128, 128), np.float32)
    kkp = np.zeros((NHEAD * 128, 128), np.float32)
    for h in range(NHEAD):
        r0 = h * 128 + (h % 2) * 64
        kqp[r0:r0 + 64, 0:64] = Kq[h]
        kkp[r0:r0 + 64, 0:64] = Kk[h]
    tri = np.triu(np.ones((128, 128), np.float32))
    return kqp, kkp, tri


def emit_s5(ctx, x_in, xo, pewT, qkvwT, owT, taps, onc, onr, sel2, oacc, ozacc):
    nc, rec = ctx.nc, ctx.rec
    TXX = T + 2
    NG = 8
    cc_k = nc.dram_tensor("cc_k", [D, T], BF16)
    cc_v = nc.dram_tensor("cc_v", [T, D], BF16)
    ko = nc.dram_tensor("ko", [4 * D, T], BF16)
    vo = nc.dram_tensor("vo", [4 * T, D], BF16)

    XWIN = ctx.sb("XWIN", [128, NCH * WIN], F32)
    BIG = ctx.sb("BIG", [128, 33000], BF16)
    XHo = 0
    XS2o = 16400
    KBo = 0
    VBo = 8192
    PTo = 16400
    OFMo = 0
    OWo = 16400
    XSW = ctx.sb("XSW", [128, NCH * WIN], BF16)
    Q = ctx.sb("Q", [128, NCH * T], BF16)
    PEW = ctx.sb("PEW", [128, NCH * D], BF16)
    WSL = ctx.sb("WSL", [128, 2 * NCH * 128], BF16)
    KEV = ctx.sb("KEV", [128, 2 * WIN], BF16)
    VEV = ctx.sb("VEV", [128, 2 * WIN], BF16)
    OLOAD = ctx.sb("OLOAD", [128, T], F32)
    ZROW = ctx.sb("ZROW", [2, 2 * T], F32)
    ZRB = ctx.sb("ZRB", [2, 2 * T], BF16)
    OEV = ctx.sb("OEV", [128, WIN], F32)
    ZEV = ctx.sb("ZEV", [1, WIN], F32)
    TAPS = ctx.sb("TAPS", [128, 3 * NCH], F32)
    SEL2 = ctx.sb("SEL2", [2, 128], BF16)
    CTMP = ctx.sb("CTMP", [128, WIN], F32)
    XB = ctx.sb("XB", [128, NCH * WIN], BF16)
    XSQ = ctx.sb("XSQ", [128, WIN], BF16)
    XAF = ctx.sb("XAF", [128, WIN], F32)
    SROW = ctx.sb("SROW", [1, 4 * WIN + 16], F32)
    SRB = ctx.sb("SRB", [1, 4 * WIN + 16], BF16)
    ONC = ctx.sb("ONC", [128, 1], BF16)
    ONR = ctx.sb("ONR", [1, 128], BF16)
    PS = ctx.ps("PS", [128, 4096], F32)
    bank = lambda b, wdt=512: PS[:, b * 512:b * 512 + wdt]

    m_pew = rec.dma("gpsimd", PEW[:], pewT.rearrange("(k p) m -> p k m", p=128), sem="ld_pe")
    m_taps = rec.dma("sync", TAPS[:], taps, sem="ld_taps")
    m_onc = rec.dma("gpsimd", ONC[:], onc, sem="ld_onc")
    m_onr = rec.dma("gpsimd", ONR[:], onr, sem="ld_onr")
    m_sel2 = rec.dma("gpsimd", SEL2[:], sel2, sem="ld_sel2")

    m_ln = None
    xr_ = x_in.rearrange("(c p) t -> p c t", p=128)
    for w in range(NW):
        m_xw = rec.dma("sync", XWIN[:], xr_[:, :, 1 + w * WIN:1 + (w + 1) * WIN],
                       waits=after(m_ln), sem="ld_xw")
        m_ln = emit_ln(ctx, lambda c, a, b: XWIN[:, c * WIN + a:c * WIN + b],
                       lambda c, a, b, w=w: BIG[:, c * TXX + 1 + w * WIN + a:c * TXX + 1 + w * WIN + b],
                       WIN, lambda c, m_xw=m_xw: [m_xw], m_onc, m_onr)
    m_xb = rec.dma("sync", XWIN[:, 0:NCH], xr_[:, :, 0:1], waits=after(m_ln), sem="ld_xb",
                   allow_slow_non_contiguous=True)
    m_ln = emit_ln(ctx, lambda c, a, b: XWIN[:, c + a:c + b],
                   lambda c, a, b: BIG[:, c * TXX + a:c * TXX + b],
                   1, lambda c, m_xb=m_xb, m_ln=m_ln: [m_xb, m_ln], m_onc, m_onr)
    m_xb2 = rec.dma("sync", XWIN[:, NCH:2 * NCH], xr_[:, :, T + 1:T + 2],
                    waits=after(m_ln), sem="ld_xb", allow_slow_non_contiguous=True)
    m_ln = emit_ln(ctx, lambda c, a, b: XWIN[:, NCH + c + a:NCH + c + b],
                   lambda c, a, b: BIG[:, c * TXX + T + 1 + a:c * TXX + T + 1 + b],
                   1, lambda c, m_xb2=m_xb2, m_ln=m_ln: [m_xb2, m_ln], m_onc, m_onr)

    m_xs2 = None
    m_ev = {}
    m_xs = None
    for w in range(NW):
        for c in range(NCH):
            c0 = c * TXX + 1 + w * WIN
            m1 = rec.emit("vector", lambda e, c=c, c0=c0: e.tensor_scalar(
                out=CTMP[:], in0=BIG[:, c0 - 1:c0 - 1 + WIN],
                scalar1=TAPS[:, 3 * c:3 * c + 1], scalar2=None, op0=ALU.mult),
                waits=after(m_ln if w == 0 and c == 0 else None,
                            m_taps if w == 0 and c == 0 else None,
                            m_xs,
                            m_ev.get((w - 1, NCH - 1)) if c == 0 else None), inc="dv")
            m2 = rec.emit("vector", lambda e, c=c, c0=c0: e.scalar_tensor_tensor(
                out=CTMP[:], in0=BIG[:, c0:c0 + WIN],
                scalar=TAPS[:, 3 * c + 1:3 * c + 2], in1=CTMP[:],
                op0=ALU.mult, op1=ALU.add), waits=after(m1), inc="dv")
            m_xs = rec.emit("vector", lambda e, c=c, c0=c0: e.scalar_tensor_tensor(
                out=XSW[:, c * WIN:(c + 1) * WIN], in0=BIG[:, c0 + 1:c0 + 1 + WIN],
                scalar=TAPS[:, 3 * c + 2:3 * c + 3], in1=CTMP[:],
                op0=ALU.mult, op1=ALU.add), waits=after(m2), inc="dv")
        for m in range(NCH):
            bk = m % 2
            mm = None
            for k in range(NCH):
                mm = rec.emit("tensor", lambda e, m=m, k=k, bk=bk: e.matmul(
                    bank(bk), PEW[:, k * D + m * 128:k * D + (m + 1) * 128],
                    XSW[:, k * WIN:(k + 1) * WIN],
                    start=(k == 0), stop=(k == NCH - 1)),
                    waits=after(m_xs if m == 0 and k == 0 else None,
                                m_pew if w == 0 and m == 0 and k == 0 else None,
                                (m_ev.get((w, m - 2)) or m_ev.get((w - 1, m + 6))) if k == 0 else None),
                    inc="ts" if k == NCH - 1 else None)
            m_xs2 = rec.emit("vector", lambda e, w=w, m=m, bk=bk: e.scalar_tensor_tensor(
                out=BIG[:, XS2o + m * T + w * WIN:XS2o + m * T + (w + 1) * WIN],
                in0=bank(bk), scalar=1.0, in1=XSW[:, m * WIN:(m + 1) * WIN],
                op0=ALU.mult, op1=ALU.add), waits=after(mm), inc="dv")
            m_ev[(w, m)] = m_xs2

    XS2s = lambda k, w: BIG[:, XS2o + k * T + w * WIN:XS2o + k * T + (w + 1) * WIN]
    m_wslice = [None, None]
    m_evq = {}
    m_kst = [None, None]
    for m in range(16):
        db = m % 2
        m_wslice[db] = rec.dma("gpsimd", WSL[:, db * NCH * 128:(db + 1) * NCH * 128],
                               qkvwT.rearrange("(k p) j -> p k j", p=128)[:, :, m * 128:(m + 1) * 128],
                               waits=after(m_evq.get(m - 2)), sem="ld_wsl%d" % db)
        mev = None
        for w in range(NW):
            bk = 2 + (w % 2)
            mm = None
            for k in range(NCH):
                mm = rec.emit("tensor", lambda e, m=m, w=w, k=k, bk=bk, db=db: e.matmul(
                    bank(bk), WSL[:, db * NCH * 128 + k * 128:db * NCH * 128 + (k + 1) * 128],
                    XS2s(k, w), start=(k == 0), stop=(k == NCH - 1)),
                    waits=after(m_wslice[db] if w == 0 and k == 0 else None,
                                m_xs2 if m == 0 and w == 0 and k == 0 else None,
                                (m_evq.get((m, w - 2)) if w >= 2 else m_evq.get((m - 1, w + 2))) if k == 0 else None),
                    inc="ts" if k == NCH - 1 else None)
            if m < 8:
                mev = rec.emit("vector", lambda e, m=m, w=w, bk=bk: e.tensor_copy(
                    out=Q[:, m * T + w * WIN:m * T + (w + 1) * WIN], in_=bank(bk)),
                    waits=after(mm), inc="dv")
            else:
                mev = rec.emit("vector", lambda e, w=w, bk=bk: e.tensor_copy(
                    out=KEV[:, (w % 2) * WIN:(w % 2) * WIN + WIN], in_=bank(bk)),
                    waits=after(mm, m_kst[w % 2]), inc="dv")
                m_kst[w % 2] = rec.dma("sync", cc_k.ap()[(m - 8) * 128:(m - 7) * 128, w * WIN:(w + 1) * WIN],
                                       KEV[:, (w % 2) * WIN:(w % 2) * WIN + WIN],
                                       waits=after(mev), sem="st_k%d" % (w % 2))
            m_evq[(m, w)] = mev
        m_evq[m] = mev
    WV = PEW
    m_wv = rec.dma("gpsimd", WV[:], qkvwT.rearrange("(k p) j -> p k j", p=128)[:, :, 2048:3072],
                   waits=after(m_ev[(NW - 1, NCH - 1)]), sem="ld_wv")
    m_vst = [None, None]
    m_vev = {}
    for tb in range(NTB):
        for half in range(2):
            bk = 6 + half
            mm = None
            for k in range(NCH):
                mm = rec.emit("tensor", lambda e, tb=tb, half=half, k=k, bk=bk: e.matmul(
                    bank(bk), BIG[:, XS2o + k * T + tb * 128:XS2o + k * T + (tb + 1) * 128],
                    WV[:, k * D + half * 512:k * D + (half + 1) * 512],
                    start=(k == 0), stop=(k == NCH - 1)),
                    waits=after(m_wv if tb == 0 and half == 0 and k == 0 else None,
                                m_vev.get((tb - 1, half)) if k == 0 else None),
                    inc="ts" if k == NCH - 1 else None)
            mev = rec.emit("vector", lambda e, tb=tb, half=half, bk=bk: e.tensor_copy(
                out=VEV[:, half * WIN:half * WIN + WIN], in_=bank(bk)),
                waits=after(mm, m_vst[half]), inc="dv")
            m_vev[(tb, half)] = mev
            m_vst[half] = rec.dma("sync", cc_v.ap()[tb * 128:(tb + 1) * 128, half * 512:(half + 1) * 512],
                                  VEV[:, half * WIN:half * WIN + WIN],
                                  waits=after(mev), sem="st_v%d" % half)

    rec.emit("gpsimd", lambda e: e.collective_compute(
        "AllGather", ALU.bypass, replica_groups=[[0, 1, 2, 3], [4, 5, 6, 7]],
        ins=[cc_k.ap()], outs=[ko.ap()]),
        waits=after(("st_k0", rec.counts["st_k0"]), ("st_k1", rec.counts["st_k1"]),
                    ("st_v0", rec.counts["st_v0"]), ("st_v1", rec.counts["st_v1"])),
        inc="cc")
    m_ag = rec.emit("gpsimd", lambda e: e.collective_compute(
        "AllGather", ALU.bypass, replica_groups=[[0, 1, 2, 3], [4, 5, 6, 7]],
        ins=[cc_v.ap()], outs=[vo.ap()]),
        waits=after(("cc", rec.counts["cc"] + 1)) if False else (), inc="cc")
    m_ag = ("cc", rec.counts["cc"])

    m_kb = None
    m_vb = None
    m_oacc_gate = None
    m_accum_prev = None
    m_last_exp = [None, None]
    m_oev = None
    m_zev = None
    for g in range(NG):
        r, lo = g // 2, (g % 2) * 1024
        m_kb = rec.dma("sync", BIG[:, KBo:KBo + 8192],
                       ko.ap()[r * D:(r + 1) * D, lo:lo + 1024].rearrange("(c p) t -> p c t", p=128),
                       waits=after(m_ag if g == 0 else None, m_ln if g == 0 else None,
                                   m_last_exp[0] if g > 0 else None, m_last_exp[1] if g > 0 else None,
                                   m_oev if g > 0 else None),
                       sem="ld_kb%d" % (g % 2))
        m_vb = rec.dma("sync", BIG[:, VBo:VBo + 8192],
                       vo.ap()[g * 1024:(g + 1) * 1024, :].rearrange("(b p) j -> p b j", p=128),
                       waits=after(m_ag if g == 0 else None,
                                   m_oev if g > 0 else None),
                       sem="ld_vb%d" % (g % 2))
        m_ga_prev = m_oacc_gate
        for h in range(NHEAD):
            hb = (h % 2) * 64
            ch = h // 2
            for qw in range(NW):
                m_omm = None
                m_zmm = None
                for kb in range(8):
                    sbk = kb % 2
                    m_smm = rec.emit("tensor", lambda e, h=h, qw=qw, kb=kb, sbk=sbk, hb=hb, ch=ch: e.matmul(
                        bank(sbk), BIG[hb:hb + 64, KBo + ch * 1024 + kb * 128:KBo + ch * 1024 + (kb + 1) * 128],
                        Q[hb:hb + 64, ch * T + qw * WIN:ch * T + (qw + 1) * WIN],
                        start=True, stop=True),
                        waits=after(m_kb if h == 0 and qw == 0 and kb == 0 else None,
                                    m_last_exp[sbk]), inc="ts")
                    m_exp = rec.emit("scalar", lambda e, kb=kb, sbk=sbk: e.activation(
                        out=BIG[:, PTo + sbk * 512:PTo + (sbk + 1) * 512], in_=bank(sbk),
                        func=AF.Exp, scale=0.125),
                        waits=after(m_smm, m_omm if kb >= 2 else None,
                                    m_zmm if kb >= 2 else None), inc="ac")
                    m_last_exp[sbk] = m_exp
                    m_omm = rec.emit("tensor", lambda e, h=h, kb=kb, sbk=sbk: e.matmul(
                        PS[0:64, 2 * 512:2 * 512 + 512],
                        BIG[:, VBo + kb * 1024 + h * 64:VBo + kb * 1024 + (h + 1) * 64],
                        BIG[:, PTo + sbk * 512:PTo + (sbk + 1) * 512],
                        start=(kb == 0), stop=(kb == 7)),
                        waits=after(m_exp, m_vb if h == 0 and qw == 0 and kb == 0 else None,
                                    m_oev if kb == 0 else None), inc="ts")
                    m_zmm = rec.emit("tensor", lambda e, kb=kb, sbk=sbk: e.matmul(
                        PS[0:1, 3 * 512:3 * 512 + 512], ONC[:, 0:1],
                        BIG[:, PTo + sbk * 512:PTo + (sbk + 1) * 512],
                        start=(kb == 0), stop=(kb == 7)),
                        waits=after(m_zev if kb == 0 else None,
                                    m_onc if g == 0 and h == 0 and qw == 0 and kb == 0 else None),
                        inc="ts")
                m_oev = rec.emit("vector", lambda e: e.tensor_copy(
                    out=OEV[0:64, :], in_=PS[0:64, 2 * 512:2 * 512 + 512]),
                    waits=after(m_omm, m_accum_prev), inc="dv")
                m_zev = rec.emit("vector", lambda e: e.tensor_copy(
                    out=ZEV[:], in_=PS[0:1, 3 * 512:3 * 512 + 512]),
                    waits=after(m_zmm), inc="dv")
                aop = ALU.add if g > 0 else ALU.bypass
                m_o1 = rec.dma("gpsimd", oacc.ap()[h * 64:(h + 1) * 64, qw * WIN:(qw + 1) * WIN],
                               OEV[0:64, :], waits=after(m_oev, m_ga_prev), sem="st_oa",
                               accum_op=aop)
                m_accum_prev = rec.dma("gpsimd", ozacc.ap()[h:h + 1, qw * WIN:(qw + 1) * WIN],
                                       ZEV[:], waits=after(m_zev), sem="st_oa",
                                       accum_op=aop)
        m_oacc_gate = ("st_oa", rec.counts["st_oa"])
        m_accum_prev = m_oacc_gate

    m_owt = rec.dma("gpsimd", BIG[:, OWo:OWo + 8192].tensor[:, OWo:OWo + 8192] if False else BIG[:, OWo:OWo + 8192],
                    owT.rearrange("(k p) m -> p k m", p=128),
                    waits=after(m_last_exp[0], m_last_exp[1], m_oev), sem="ld_ow")
    m_ofm = None
    m_old = None
    m_zld = None
    m_zi = None
    for pch in range(NCH):
        ha, hc = 2 * pch, 2 * pch + 1
        m_old1 = rec.dma("sync", OLOAD[0:64, :], oacc.ap()[ha * 64:(ha + 1) * 64, :],
                         waits=after(m_oacc_gate, m_ofm), sem="ld_oa%d" % (pch % 2))
        m_old = rec.dma("sync", OLOAD[64:128, :], oacc.ap()[hc * 64:(hc + 1) * 64, :],
                        sem="ld_oa%d" % (pch % 2))
        m_zld = rec.dma("sync", ZROW[:, 0:T], ozacc.ap()[ha:ha + 2, :],
                        waits=after(m_zld, m_oacc_gate if pch == 0 else None,
                                    m_zi if pch >= 1 else None), sem="ld_oz%d" % (pch % 2))
        m_zi = rec.emit("vector", lambda e: e.reciprocal(
            out=ZROW[:, T:2 * T], in_=ZROW[:, 0:T]), waits=after(m_zld), inc="dv")
        m_hl = emit_hilo(ctx, ZROW[:, T:2 * T], ZRB[:, 0:T], ZRB[:, T:2 * T], m_zi)
        for w in range(NW):
            rec.emit("tensor", lambda e, w=w: e.matmul(
                bank(4), SEL2[:], ZRB[:, w * WIN:(w + 1) * WIN], start=True, stop=False),
                waits=after(m_hl, m_sel2 if pch == 0 and w == 0 else None,
                            m_ofm), inc=None)
            m_zb = rec.emit("tensor", lambda e, w=w: e.matmul(
                bank(4), SEL2[:], ZRB[:, T + w * WIN:T + (w + 1) * WIN], start=False, stop=True),
                inc="ts")
            m_ofm = rec.emit("vector", lambda e, pch=pch, w=w: e.tensor_tensor(
                out=BIG[:, OFMo + pch * T + w * WIN:OFMo + pch * T + (w + 1) * WIN],
                in0=OLOAD[:, w * WIN:(w + 1) * WIN], in1=bank(4), op=ALU.mult),
                waits=after(m_zb, m_old), inc="dv")
    m_store = None
    m_resid_prev = [None] * NCH
    for w in range(NW):
        m_xw2 = rec.dma("sync", XWIN[:], xr_[:, :, 1 + w * WIN:1 + (w + 1) * WIN],
                        waits=after(m_ln if w == 0 else None,
                                    ("st%d" % ((w - 1) % 2), rec.counts.get("st%d" % ((w - 1) % 2), 0)) if w > 0 else None,
                                    *(m_resid_prev if w > 0 else [])),
                        sem="ld_xw2")
        for m in range(NCH):
            bk = m % 2
            mm = None
            for k in range(NCH):
                mm = rec.emit("tensor", lambda e, m=m, k=k, bk=bk, w=w: e.matmul(
                    bank(bk), BIG[:, OWo + k * D + m * 128:OWo + k * D + (m + 1) * 128],
                    BIG[:, OFMo + k * T + w * WIN:OFMo + k * T + (w + 1) * WIN],
                    start=(k == 0), stop=(k == NCH - 1)),
                    waits=after(m_ofm if w == 0 and m == 0 and k == 0 else None,
                                m_owt if w == 0 and m == 0 and k == 0 else None,
                                m_resid_prev[m - 2] if k == 0 else None),
                    inc="ts" if k == NCH - 1 else None)
            m_res = rec.emit("vector", lambda e, m=m, bk=bk: e.scalar_tensor_tensor(
                out=XWIN[:, m * WIN:(m + 1) * WIN], in0=bank(bk), scalar=1.0,
                in1=XWIN[:, m * WIN:(m + 1) * WIN], op0=ALU.mult, op1=ALU.add),
                waits=after(mm, m_xw2), inc="dv")
            m_resid_prev[m] = m_res
            m_store = rec.dma("sync", xo.rearrange("(c p) t -> p c t", p=128)[:, m, w * WIN:(w + 1) * WIN],
                              XWIN[:, m * WIN:(m + 1) * WIN], waits=after(m_res), sem="st%d" % (w % 2))
    return [(s, rec.counts[s]) for s in ("st0", "st1") if s in rec.counts]


def build_s5(nc):
    rec = Rec(nc)
    ctx = Ctx(nc, rec)
    x_in = nc.dram_tensor("x", [D, T + 2], F32, kind="ExternalInput").ap()
    pewT = nc.dram_tensor("pewT", [D, D], F32, kind="ExternalInput").ap()
    qkvwT = nc.dram_tensor("qkvwT", [D, 3 * D], F32, kind="ExternalInput").ap()
    owT = nc.dram_tensor("owT", [D, D], F32, kind="ExternalInput").ap()
    taps = nc.dram_tensor("taps", [128, 3 * NCH], F32, kind="ExternalInput").ap()
    onc = nc.dram_tensor("ones_col", [128, 1], F32, kind="ExternalInput").ap()
    onr = nc.dram_tensor("ones_row", [1, 128], F32, kind="ExternalInput").ap()
    sel2 = nc.dram_tensor("sel2", [2, 128], F32, kind="ExternalInput").ap()
    xo = nc.dram_tensor("xo", [D, T], F32, kind="ExternalOutput").ap()
    oacc = nc.dram_tensor("oacc", [NHEAD * 64, T], F32, kind="ExternalOutput")
    ozacc = nc.dram_tensor("ozacc", [NHEAD, T], F32, kind="ExternalOutput")
    mil = emit_s5(ctx, x_in, xo, pewT, qkvwT, owT, taps, onc, onr, sel2, oacc, ozacc)
    rec.finalize(mil)
    return nc


def run_s5(x_full, conv_w, pe_w, qkv_w, out_w):
    nc = build_s5(bass.Bass("TRN2", target_bir_lowering=False,
                            dynamic_dma_scratch_size=16384))
    flat = np.ascontiguousarray(np.asarray(x_full).reshape(2 * 8192, D))
    taps_h = np.ascontiguousarray(np.asarray(conv_w)[:, 0, :, 1])
    taps_t = np.zeros((128, 3 * NCH), np.float32)
    for c in range(NCH):
        taps_t[:, 3 * c:3 * (c + 1)] = taps_h[c * 128:(c + 1) * 128]
    sel2v = np.zeros((2, 128), np.float32)
    sel2v[0, 0:64] = 1.0
    sel2v[1, 64:128] = 1.0
    in_maps = []
    for c in range(NCORES):
        s, e = c * T, (c + 1) * T
        prev = flat[s - 1:s].T if s % 8192 else np.zeros((D, 1), np.float32)
        nxt = flat[e:e + 1].T if e % 8192 else np.zeros((D, 1), np.float32)
        xslab = np.concatenate([prev, flat[s:e].T, nxt], axis=1).astype(np.float32)
        in_maps.append({
            "x": np.ascontiguousarray(xslab),
            "pewT": np.ascontiguousarray(np.asarray(pe_w).T),
            "qkvwT": np.ascontiguousarray(np.asarray(qkv_w).T),
            "owT": np.ascontiguousarray(np.asarray(out_w).T),
            "taps": taps_t, "sel2": sel2v,
            "ones_col": np.ones((128, 1), np.float32),
            "ones_row": np.ones((1, 128), np.float32),
        })
    res = run_bass_kernel_spmd(nc, in_maps, core_ids=list(range(NCORES)))
    out = np.concatenate([res.results[c]["xo"].T for c in range(NCORES)], axis=0)
    return out.reshape(2, 8192, D)


NX1 = 3584
HALO1 = 1536
NW1 = 7
NPAIR = 22
NK2 = 22
TX1 = 513


def emit_s1(ctx, x_in, x1d, xo, wqkvf, w1f, w2f, woT, b1r, b2r,
            tqa, tqb, tka, tkb, qscol, kscol, gcol, bcol,
            perm, hsel, sel2, ident, mband, hv, onc, onr, onr5,
            layers=2, nwin_limit=None):
    nc, rec = ctx.nc, ctx.rec
    TX = TX1

    XW = ctx.sb("XW", [128, NCH * TX], F32)
    XSB = ctx.sb("XSB", [128, NCH * TX], BF16)
    QP = ctx.sb("QP", [128, 4 * WIN], BF16)
    KP = ctx.sb("KP", [128, 4 * 2 * WIN], BF16)
    VP = ctx.sb("VP", [128, 8 * WIN], BF16)
    SCR = ctx.sb("SCR", [128, NPAIR * WIN], BF16)
    XLN = ctx.sb("XLN", [128, NCH * WIN], BF16)
    OFMW = ctx.sb("OFMW", [128, 4 * WIN], BF16)
    SM = ctx.sb("SM", [128, 640], F32)
    PTM = ctx.sb("PTM", [128, 640], BF16)
    PTN = ctx.sb("PTN", [128, 640], BF16)
    PTT = ctx.sb("PTT", [128, 640], BF16)
    TBQA = ctx.sb("TBQA", [128, WIN], BF16)
    TBQB = ctx.sb("TBQB", [128, WIN], BF16)
    TBKA = ctx.sb("TBKA", [128, WIN], BF16)
    TBKB = ctx.sb("TBKB", [128, WIN], BF16)
    QRAW = ctx.sb("QRAW", [128, WIN], BF16)
    QT1 = ctx.sb("QT1", [128, WIN], F32)
    T1Q = ctx.sb("T1Q", [128, WIN], BF16)
    RT = ctx.sb("RT", [128, WIN], BF16)
    QA = ctx.sb("QA", [128, WIN], F32)
    XAF = ctx.sb("XAF", [128, WIN], F32)
    XAF2 = ctx.sb("XAF2", [128, WIN], F32)
    XB = ctx.sb("XB", [128, NCH * WIN], BF16)
    XSQ = ctx.sb("XSQ", [128, WIN], BF16)
    SROW = ctx.sb("SROW", [1, 4 * WIN + 16], F32)
    SRB = ctx.sb("SRB", [1, 4 * WIN + 16], BF16)
    WSLQ = ctx.sb("WSLQ", [128, 2 * 1024], BF16)
    WQV = ctx.sb("WQV", [128, NCH * WIN], BF16)
    W1SL = ctx.sb("W1SL", [128, 2 * 2048], BF16)
    W2SL = ctx.sb("W2SL", [128, 2 * 2816], BF16)
    WOS = ctx.sb("WOS", [128, 4 * D], BF16)
    B1 = ctx.sb("B1", [1, 5460], BF16)
    B2 = ctx.sb("B2", [1, D], BF16)
    RROW = ctx.sb("RROW", [2, 3 * WIN], F32)
    RRB = ctx.sb("RRB", [2, 2 * WIN], BF16)
    RM = ctx.sb("RM", [128, 4], F32)
    PERM = ctx.sb("PERM", [128, 128], BF16)
    HSEL = ctx.sb("HSEL", [128, 2], BF16)
    SEL2 = ctx.sb("SEL2", [2, 128], BF16)
    IDT = ctx.sb("IDT", [128, 128], BF16)
    MBIAS = ctx.sb("MBIAS", [128, 640], F32)
    HV = ctx.sb("HV", [128, 36], F32)
    QS = ctx.sb("QS", [128, 4], F32)
    GB = ctx.sb("GB", [128, 4 * NCH], F32)
    XMCOL = ctx.sb("XMCOL", [128, 16], F32)
    XQCOL = ctx.sb("XQCOL", [128, 8], BF16)
    XQF = ctx.sb("XQF", [128, 8], F32)
    ONC = ctx.sb("ONC", [128, 1], BF16)
    ONR = ctx.sb("ONR", [1, 128], BF16)
    ONR5 = ctx.sb("ONR5", [1, WIN], BF16)
    PS = ctx.ps("PS", [128, 4096], F32)
    bank = lambda b, wdt=512: PS[:, b * 512:b * 512 + wdt]

    x_r = x_in.rearrange("(c p) t -> p c t", p=128)
    x1_r = x1d.ap().rearrange("(c p) t -> p c t", p=128)
    xo_r = xo.rearrange("(c p) t -> p c t", p=128)
    wqkv_r = wqkvf.rearrange("(l k p) m -> p l k m", l=2, p=128)
    w1_r = w1f.rearrange("(l k p) m -> p l k m", l=2, p=128)
    XW_3 = XW.ap().rearrange("p (c t) -> p c t", c=NCH)
    XSB_3 = XSB.ap().rearrange("p (c t) -> p c t", c=NCH)

    for ap_, tgt in ((perm, PERM[:]), (hsel, HSEL[:]), (sel2, SEL2[:]),
                     (ident, IDT[:]), (onc, ONC[:]), (onr, ONR[:]),
                     (onr5, ONR5[:])):
        rec.dma("gpsimd", tgt, ap_, sem="ld_cst")
    for ap_, tgt in ((mband, MBIAS[:]), (hv, HV[:]), (qscol, QS[:, 0:2]),
                     (kscol, QS[:, 2:4]), (gcol, GB[:, 0:2 * NCH]),
                     (bcol, GB[:, 2 * NCH:4 * NCH])):
        mC = rec.dma("sync", tgt, ap_, sem="ld_cst2")
    mCg = ("ld_cst", rec.counts["ld_cst"])
    m_perm = m_hsel = m_sel2 = m_idt = m_onc = m_onr = m_onr5 = mCg
    m_mb = m_hv = m_qs = m_gb = mC
    m_const = [mCg, mC]

    m_store = None
    for w in range(NW1):
        m_xw = rec.dma("sync", XW_3[:, :, 0:WIN],
                       x_r[:, :, 1 + w * WIN:1 + (w + 1) * WIN],
                       waits=after(m_store), sem="ld_xi")
        m_ln = emit_ln(ctx, lambda c, a, b: XW[:, c * TX + a:c * TX + b],
                       lambda c, a, b: XW[:, c * TX + a:c * TX + b],
                       WIN, lambda c, m_xw=m_xw: [m_xw], m_onc, m_onr)
        m_store = rec.dma("sync", x1_r[:, :, 1 + w * WIN:1 + (w + 1) * WIN],
                          XW_3[:, :, 0:WIN], waits=after(m_ln), sem="st_x1")
    m_zc = rec.emit("vector", lambda e: e.memset(XQF[:], 0.0),
                    waits=after(m_store), inc="dv")
    m_store = rec.dma("sync", x1_r[:, :, 0:1], XQF[:], waits=after(m_zc),
                      sem="st_x1", allow_slow_non_contiguous=True)

    env = dict(XW=XW, XSB=XSB, QP=QP, KP=KP, VP=VP, SCR=SCR, XLN=XLN, OFMW=OFMW,
               SM=SM, PTM=PTM, PTN=PTN, PTT=PTT, TBQA=TBQA, TBQB=TBQB, TBKA=TBKA,
               TBKB=TBKB, QRAW=QRAW, QT1=QT1, T1Q=T1Q, RT=RT, QA=QA, XAF=XAF,
               XSQ=XSQ, WSLQ=WSLQ, WQV=WQV, W1SL=W1SL, W2SL=W2SL, WOS=WOS, B1=B1,
               B2=B2, RROW=RROW, RRB=RRB, RM=RM, PERM=PERM, HSEL=HSEL, SEL2=SEL2,
               IDT=IDT, MBIAS=MBIAS, HV=HV, QS=QS, GB=GB, XMCOL=XMCOL,
               XQCOL=XQCOL, ONC=ONC, ONR=ONR, ONR5=ONR5, PS=PS, bank=bank,
               x1_r=x1_r, wqkv_r=wqkv_r, w1_r=w1_r, w2f=w2f, tqa=tqa, tqb=tqb,
               tka=tka, tkb=tkb, XW_3=XW_3, XSB_3=XSB_3, m_onc=m_onc,
               m_onr=m_onr)

    st = {"store": m_store, "attn": None, "tbl": None, "p1dst": None,
          "wqv_use": None, "kvp_done": None, "xmprev": None}

    for l in range(layers):
        t0 = 0 if l == 0 else 512
        nwin = NW1 if l == 0 else NW1 - 1
        if nwin_limit is not None:
            nwin = min(nwin, nwin_limit)
        m_wo = rec.dma("gpsimd", WOS[:],
                       woT.rearrange("(l k p) m -> p l k m", l=2, p=128)[:, l],
                       waits=after(st["store"]), sem="ld_wo")
        m_b1 = rec.dma("gpsimd", B1[:], b1r[l:l + 1, :], waits=after(st["store"]),
                       sem="ld_b")
        m_b2 = rec.dma("gpsimd", B2[:], b2r[l:l + 1, :], sem="ld_b")
        m_xqf = rec.dma("sync", XQF[:], x1_r[:, :, 0:1],
                        waits=after(st["store"]), sem="ld_xq",
                        allow_slow_non_contiguous=True)
        m_xqc = rec.emit("vector", lambda e: e.tensor_copy(out=XQCOL[:], in_=XQF[:]),
                         waits=after(m_xqf), inc="dv")
        m_xm0 = rec.emit("vector", lambda e: e.memset(XMCOL[:, 0:8], 0.0),
                         waits=after(st["store"]), inc="dv")
        m_kv0 = rec.emit("vector", lambda e: e.memset(KP[:], 0.0),
                         waits=after(st["store"]), inc="dv")
        m_kv1 = rec.emit("vector", lambda e: e.memset(VP[:], 0.0), inc="dv")
        st["entry"] = [m_wo, m_b2, m_xqc, m_xm0, m_kv0, m_kv1] + m_const
        st["xmprev"] = m_xm0
        if l == 1:
            emit_s1_win(ctx, l, -1, 0, True, st, env)
        for w in range(nwin):
            emit_s1_win(ctx, l, w, t0, False, st, env)

    X1F = SCR.ap()[:, 0:8192].bitcast(F32)
    m_res_prev = None
    m_st = None
    for w in range(NW):
        m_l1 = rec.dma("sync", XW_3[:, :, 0:WIN],
                       x_r[:, :, 1537 + w * WIN:1537 + (w + 1) * WIN],
                       waits=after(st["store"] if w == 0 else None, m_res_prev, m_st),
                       sem="ld_f%d" % (w % 2))
        m_l2 = rec.dma("sync", X1F.rearrange("p (c t) -> p c t", c=NCH)[:, :, 0:WIN],
                       x1_r[:, :, 1537 + w * WIN:1537 + (w + 1) * WIN],
                       waits=after(st["store"] if w == 0 else None, m_res_prev),
                       sem="ld_f%d" % (w % 2))
        m_res = None
        for c in range(NCH):
            m_res = rec.emit("vector", lambda e, c=c: e.scalar_tensor_tensor(
                out=XW[:, c * TX:c * TX + WIN], in0=X1F[:, c * WIN:(c + 1) * WIN],
                scalar=0.5, in1=XW[:, c * TX:c * TX + WIN], op0=ALU.mult, op1=ALU.add),
                waits=after(m_l2 if c == 0 else None),
                inc="dv")
        m_res_prev = m_res
        m_st = rec.dma("sync", xo_r[:, :, w * WIN:(w + 1) * WIN], XW_3[:, :, 0:WIN],
                       waits=after(m_res), sem="st_o%d" % (w % 2))
    return [("st_o0", rec.counts["st_o0"]), ("st_o1", rec.counts["st_o1"])]


def emit_s1_win(ctx, l, w, t0, kv_only, st, env):
    nc, rec = ctx.nc, ctx.rec
    (XW, XSB, QP, KP, VP, SCR, XLN, OFMW, SM, PTM, PTN, PTT, TBQA, TBQB, TBKA,
     TBKB, QRAW, QT1, T1Q, RT, QA, XAF, XSQ, WSLQ, WQV, W1SL, W2SL, WOS, B1, B2,
     RROW, RRB, RM, PERM, HSEL, SEL2, IDT, MBIAS, HV, QS, GB, XMCOL, XQCOL, ONC,
     ONR, ONR5, PS, bank, x1_r, wqkv_r, w1_r, w2f, tqa, tqb, tka, tkb, XW_3,
     XSB_3, m_onc, m_onr) = (
        env["XW"], env["XSB"], env["QP"], env["KP"], env["VP"], env["SCR"],
        env["XLN"], env["OFMW"], env["SM"], env["PTM"], env["PTN"], env["PTT"],
        env["TBQA"], env["TBQB"], env["TBKA"], env["TBKB"], env["QRAW"],
        env["QT1"], env["T1Q"], env["RT"], env["QA"], env["XAF"], env["XSQ"],
        env["WSLQ"], env["WQV"], env["W1SL"], env["W2SL"], env["WOS"], env["B1"],
        env["B2"], env["RROW"], env["RRB"], env["RM"], env["PERM"], env["HSEL"],
        env["SEL2"], env["IDT"], env["MBIAS"], env["HV"], env["QS"], env["GB"],
        env["XMCOL"], env["XQCOL"], env["ONC"], env["ONR"], env["ONR5"],
        env["PS"], env["bank"], env["x1_r"], env["wqkv_r"], env["w1_r"],
        env["w2f"], env["tqa"], env["tqb"], env["tka"], env["tkb"], env["XW_3"],
        env["XSB_3"], env["m_onc"], env["m_onr"])
    TX = TX1
    XWs = lambda c, a, b: XW[:, c * TX + a:c * TX + b]
    XSBs = lambda c, a, b: XSB[:, c * TX + a:c * TX + b]
    W0 = 0 if kv_only else t0 + w * WIN
    entry = st.pop("entry", None) or []

    m_kvcp = None
    if not (l == 0 and w == 0) and not kv_only:
        kvp = st.pop("kvp_done", None) or []
        mk = None
        for c in range(4):
            mk = rec.emit("vector", lambda e, c=c: e.tensor_copy(
                out=KP[:, c * 1024:c * 1024 + 512],
                in_=KP[:, c * 1024 + 512:c * 1024 + 1024]),
                waits=after(st["attn"] if c == 0 else None,
                            st["p1dst"] if c == 0 else None,
                            *(kvp if c == 0 else [])), inc="dv")
        m_kvcp = rec.emit("vector", lambda e: e.tensor_copy(
            out=VP[:, 0:4 * WIN], in_=VP[:, 4 * WIN:8 * WIN]),
            waits=after(mk), inc="dv")

    m_xw = rec.dma("sync", XW_3[:, :, 0:TX], x1_r[:, :, W0:W0 + TX],
                   waits=after(st["store"], m_kvcp, *entry), sem="ld_w%d" % (w % 2))
    m_cast = []
    for c in range(NCH):
        m_cast.append(rec.emit("vector", lambda e, c=c: e.tensor_copy(
            out=XSBs(c, 0, TX), in_=XWs(c, 0, TX)),
            waits=after(m_xw if c == 0 else None), inc="dv"))
    m_fix = None
    for c in range(4, NCH):
        m_fix = rec.emit("vector", lambda e, c=c: e.tensor_copy(
            out=XSBs(c, 0, 1), in_=XQCOL[:, c:c + 1]),
            waits=after(m_cast[c]), inc="dv")
    m_stash = rec.emit("vector", lambda e: e.tensor_copy(
        out=XQCOL[:], in_=XSB_3[:, :, TX - 1]),
        waits=after(m_cast[-1], m_fix), inc="dv")
    m_p0 = m_stash

    m_tb = {}
    tbl_loads = ([] if kv_only else [("qa", TBQA, tqa), ("qb", TBQB, tqb)]) + \
        [("ka", TBKA, tka), ("kb", TBKB, tkb)]
    for nm, buf, srcm in tbl_loads:
        m_tb[nm] = rec.dma("gpsimd", buf[:], srcm[:, W0:W0 + WIN],
                           waits=after(st["tbl"]), sem="ld_tb" + nm)
    ocs = list(range(4, 8) if kv_only else range(0, 8))
    m_xsq = {}
    m_qraw = {}
    m_tseps = {}
    m_t1 = {}
    m_t1q = {}
    m_rt = {}
    m_qbd = {}
    m_dst = {}
    for oc in ocs:
        bk = oc % 2
        buf = oc % 2
        m_wsl = rec.dma("gpsimd", WSLQ[:, buf * 1024:(buf + 1) * 1024],
                        wqkv_r[:, l, :, oc * 128:(oc + 1) * 128],
                        waits=after(m_qraw.get(oc - 2), m_xsq.get(oc - 2)),
                        sem="ld_qk%d" % buf)
        mm = None
        for kc in range(NCH):
            sh = 1 if kc < 4 else 0
            mm = rec.emit("tensor", lambda e, kc=kc, bk=bk, buf=buf, sh=sh: e.matmul(
                bank(bk), WSLQ[:, buf * 1024 + kc * 128:buf * 1024 + (kc + 1) * 128],
                XSBs(kc, sh, sh + WIN), start=(kc == 0), stop=(kc == NCH - 1)),
                waits=after(m_wsl if kc == 0 else None,
                            m_p0 if kc == 0 else None,
                            m_qraw.get(oc - 2) if kc == 0 else None,
                            m_xsq.get(oc - 2) if kc == 0 else None),
                inc="ts" if kc == NCH - 1 else None)
        m_xsq[oc] = rec.emit("scalar", lambda e, bk=bk: e.activation(
            out=XSQ[:], in_=bank(bk), func=AF.Square), waits=after(mm), inc="ac")
        m_qraw[oc] = rec.emit("scalar", lambda e, bk=bk: e.copy(
            out=QRAW[:], in_=bank(bk)),
            waits=after(mm, m_xsq[oc], m_t1.get(oc - 1)), inc="ac")
        m_ss = rec.emit("tensor", lambda e: e.matmul(
            PS[0:2, 6 * 512:6 * 512 + 512], HSEL[:], XSQ[:], start=True, stop=True),
            waits=after(m_xsq[oc], m_tseps.get(oc - 1)), inc="ts")
        m_eps = rec.emit("vector", lambda e: e.tensor_scalar(
            out=RROW[:, 0:512], in0=PS[0:2, 6 * 512:6 * 512 + 512], scalar1=1e-24,
            scalar2=None, op0=ALU.add), waits=after(m_ss), inc="dv")
        m_tseps[oc] = m_eps
        m_lnr = rec.emit("scalar", lambda e: e.activation(
            out=RROW[:, 512:1024], in_=RROW[:, 0:512], func=AF.Ln),
            waits=after(m_eps), inc="ac")
        m_rsq = rec.emit("scalar", lambda e: e.activation(
            out=RROW[:, 1024:1536], in_=RROW[:, 512:1024], func=AF.Exp, scale=-0.5),
            waits=after(m_lnr), inc="ac")
        m_hl = emit_hilo(ctx, RROW[:, 1024:1536], RRB[:, 0:512], RRB[:, 512:1024],
                         m_rsq)
        rec.emit("tensor", lambda e: e.matmul(
            bank(3), SEL2[:], RRB[:, 0:512], start=True, stop=False),
            waits=after(m_hl, m_t1.get(oc - 1)), inc=None)
        m_bc = rec.emit("tensor", lambda e: e.matmul(
            bank(3), SEL2[:], RRB[:, 512:1024], start=False, stop=True), inc="ts")
        m_t1[oc] = rec.emit("vector", lambda e: e.tensor_tensor(
            out=QT1[:], in0=QRAW[:], in1=bank(3), op=ALU.mult),
            waits=after(m_bc, m_qraw[oc]), inc="dv")
        sccol = l if oc < 4 else (2 + l)
        m_t1q[oc] = rec.emit("vector", lambda e, sccol=sccol: e.tensor_scalar(
            out=T1Q[:], in0=QT1[:], scalar1=QS[:, sccol:sccol + 1], scalar2=None,
            op0=ALU.mult), waits=after(m_t1[oc]), inc="dv")
        m_pmm = rec.emit("tensor", lambda e: e.matmul(
            bank(2), PERM[:], T1Q[:], start=True, stop=True),
            waits=after(m_t1q[oc], m_rt.get(oc - 1)), inc="ts")
        m_rt[oc] = rec.emit("scalar", lambda e: e.copy(
            out=RT[:], in_=bank(2)),
            waits=after(m_pmm, m_qbd.get(oc - 1)), inc="ac")
        TA, TB = (TBQA, TBQB) if oc < 4 else (TBKA, TBKB)
        tka_m = m_tb["qa" if oc < 4 else "ka"]
        tkb_m = m_tb["qb" if oc < 4 else "kb"]
        m_qa = rec.emit("vector", lambda e, TA=TA: e.tensor_tensor(
            out=QA[:], in0=T1Q[:], in1=TA[:], op=ALU.mult),
            waits=after(m_t1q[oc], tka_m), inc="dv")
        m_qbd[oc] = rec.emit("vector", lambda e, TB=TB: e.tensor_tensor(
            out=XAF[:], in0=RT[:], in1=TB[:], op=ALU.mult),
            waits=after(m_rt[oc], tkb_m), inc="dv")
        if oc < 4:
            dstf = lambda oc=oc: QP[:, (oc % 4) * WIN:((oc % 4) + 1) * WIN]
            extra = st["attn"]
        else:
            dstf = lambda oc=oc: KP[:, (oc - 4) * 1024 + 512:(oc - 4) * 1024 + 1024]
            extra = m_kvcp
        m_dst[oc] = rec.emit("vector", lambda e, dstf=dstf: e.tensor_tensor(
            out=dstf(), in0=QA[:], in1=XAF[:], op=ALU.add),
            waits=after(m_qa, m_qbd[oc], extra), inc="dv")
    m_wqv = rec.dma("gpsimd", WQV[:], wqkv_r[:, l, :, 1024:1536],
                    waits=after(st["wqv_use"]), sem="ld_qv")
    m_vev = {}
    for tb in range(4):
        bk = tb % 2
        mm = None
        for kc in range(NCH):
            sh = 1 if kc < 4 else 0
            mm = rec.emit("tensor", lambda e, tb=tb, kc=kc, bk=bk, sh=sh: e.matmul(
                bank(bk), XSBs(kc, sh + tb * 128, sh + (tb + 1) * 128),
                WQV[:, kc * WIN:(kc + 1) * WIN], start=(kc == 0),
                stop=(kc == NCH - 1)),
                waits=after(m_wqv if tb == 0 and kc == 0 else None,
                            (m_qraw[ocs[-1]] if tb == 0 and kc == 0 else None),
                            (m_xsq[ocs[-1]] if tb == 0 and kc == 0 else None),
                            (m_vev.get(tb - 2) if kc == 0 else None)),
                inc="ts" if kc == NCH - 1 else None)
        m_vev[tb] = rec.emit("scalar", lambda e, tb=tb, bk=bk: e.copy(
            out=VP[:, (4 + tb) * WIN:(5 + tb) * WIN], in_=bank(bk)),
            waits=after(mm, m_kvcp if tb == 0 else None), inc="ac")
    st["tbl"] = m_dst[ocs[-1]]
    st["wqv_use"] = ("ts", rec.counts["ts"])
    st["p1dst"] = m_dst[ocs[-1]]
    m_p1 = [m_dst[oc] for oc in ocs] + [m_vev[tb] for tb in range(4)]
    if kv_only:
        st["attn"] = None
        st["kvp_done"] = m_p1
        return

    qb_glob0 = W0 // 128
    m_mask_prev = None
    m_exp_prev = None
    m_tsp_prev = None
    m_trA = {}
    m_tev_prev = {}
    m_oev = None
    m_tr_last = None
    for h in range(8):
        hb = (h % 2) * 64
        c = h // 2
        for qb in range(4):
            first = (h == 0 and qb == 0)
            m_s1 = rec.emit("tensor", lambda e, hb=hb, c=c, qb=qb: e.matmul(
                bank(0), QP[hb:hb + 64, c * WIN + qb * 128:c * WIN + (qb + 1) * 128],
                KP[hb:hb + 64, c * 1024 + qb * 128:c * 1024 + qb * 128 + 512],
                start=True, stop=True),
                waits=after(*(m_p1 if first else [m_mask_prev])), inc="ts")
            m_s2 = rec.emit("tensor", lambda e, hb=hb, c=c, qb=qb: e.matmul(
                PS[:, 512:640],
                QP[hb:hb + 64, c * WIN + qb * 128:c * WIN + (qb + 1) * 128],
                KP[hb:hb + 64, c * 1024 + qb * 128 + 512:c * 1024 + qb * 128 + 640],
                start=True, stop=True), inc="ts")
            m_mask = rec.emit("vector", lambda e: e.scalar_tensor_tensor(
                out=SM[:], in0=PS[:, 0:640], scalar=1.0, in1=MBIAS[:],
                op0=ALU.mult, op1=ALU.add),
                waits=after(m_s1, m_s2, m_exp_prev), inc="dv")
            m_mask_prev = m_mask
            m_rmx = rec.emit("vector", lambda e: e.reduce_max(
                out=RM[:, 0:1], in_=SM[:], axis=AX.X), waits=after(m_mask), inc="dv")
            m_nrm = rec.emit("vector", lambda e: e.tensor_scalar(
                out=RM[:, 1:2], in0=RM[:, 0:1], scalar1=-1.0, scalar2=None,
                op0=ALU.mult), waits=after(m_rmx), inc="dv")
            m_exp = rec.emit("scalar", lambda e: e.activation(
                out=PTM[:], in_=SM[:], func=AF.Exp, bias=RM[:, 1:2], scale=1.0),
                waits=after(m_nrm, m_tr_last), inc="ac")
            m_exp_prev = m_exp
            m_hvl = None
            for sb in range(5):
                kcol = min(qb_glob0 + qb + sb, 35)
                m_hvl = rec.emit("vector", lambda e, sb=sb, kcol=kcol: e.tensor_scalar(
                    out=PTN[:, sb * 128:(sb + 1) * 128],
                    in0=PTM[:, sb * 128:(sb + 1) * 128],
                    scalar1=HV[:, kcol:kcol + 1], scalar2=None, op0=ALU.mult),
                    waits=after(m_exp if sb == 0 else None,
                                m_tsp_prev if sb == 0 else None), inc="dv")
            m_z = rec.emit("vector", lambda e: e.reduce_sum(
                out=RM[:, 2:3], in_=PTN[:], axis=AX.X), waits=after(m_hvl), inc="dv")
            m_ze = rec.emit("vector", lambda e: e.tensor_scalar(
                out=RM[:, 3:4], in0=RM[:, 2:3], scalar1=1e-30, scalar2=None,
                op0=ALU.add), waits=after(m_z), inc="dv")
            m_zi = rec.emit("vector", lambda e: e.reciprocal(
                out=RM[:, 0:1], in_=RM[:, 3:4]), waits=after(m_ze), inc="dv")
            m_tsp = rec.emit("vector", lambda e: e.tensor_scalar(
                out=PTM[:], in0=PTN[:], scalar1=RM[:, 0:1], scalar2=None,
                op0=ALU.mult), waits=after(m_zi), inc="dv")
            m_tsp_prev = m_tsp
            m_omms = []
            for sb in range(5):
                bkt = 2 + sb % 2
                m_tr = rec.emit("tensor", lambda e, sb=sb, bkt=bkt: e.matmul(
                    PS[:, bkt * 512:bkt * 512 + 128], PTM[:, sb * 128:(sb + 1) * 128],
                    IDT[:], start=True, stop=True),
                    waits=after(m_tsp if sb == 0 else None, m_trA.get(bkt)),
                    inc="ts")
                if sb == 4:
                    m_tr_last = m_tr
                m_tev = rec.emit("scalar", lambda e, sb=sb, bkt=bkt: e.copy(
                    out=PTT[:, sb * 128:(sb + 1) * 128],
                    in_=PS[:, bkt * 512:bkt * 512 + 128]),
                    waits=after(m_tr, m_tev_prev.get(sb)), inc="ac")
                m_trA[bkt] = m_tev
                m_omm = rec.emit("tensor", lambda e, sb=sb, h=h, qb=qb: e.matmul(
                    PS[0:64, 4 * 512:4 * 512 + 128],
                    VP[:, (qb + sb) * WIN + h * 64:(qb + sb) * WIN + h * 64 + 64],
                    PTT[:, sb * 128:(sb + 1) * 128], start=(sb == 0), stop=(sb == 4)),
                    waits=after(m_tev, m_oev if sb == 0 else None), inc="ts")
                m_omms.append(m_omm)
            m_tev_prev = {sb: m_omms[sb] for sb in range(5)}
            m_oev = rec.emit("scalar", lambda e, hb=hb, c=c, qb=qb: e.copy(
                out=OFMW[hb:hb + 64, c * WIN + qb * 128:c * WIN + (qb + 1) * 128],
                in_=PS[0:64, 4 * 512:4 * 512 + 128]),
                waits=after(m_omms[-1]), inc="ac")
    st["attn"] = m_oev
    m_p2 = m_oev

    m_stt3 = {}
    for m in range(NCH):
        bk = m % 2
        mm = None
        for kc in range(4):
            mm = rec.emit("tensor", lambda e, m=m, kc=kc, bk=bk: e.matmul(
                bank(bk), WOS[:, kc * D + m * 128:kc * D + (m + 1) * 128],
                OFMW[:, kc * WIN:(kc + 1) * WIN], start=(kc == 0), stop=(kc == 3)),
                waits=after(m_p2 if m == 0 and kc == 0 else None,
                            m_stt3.get(m - 2) if kc == 0 else None),
                inc="ts" if kc == 3 else None)
        m_stt3[m] = rec.emit("vector", lambda e, m=m, bk=bk: e.scalar_tensor_tensor(
            out=XWs(m, 1, 1 + WIN), in0=bank(bk), scalar=1.0, in1=XWs(m, 1, 1 + WIN),
            op0=ALU.mult, op1=ALU.add), waits=after(mm), inc="dv")
    m_p3 = [m_stt3[m] for m in range(NCH)]

    nb, cb = (w + 1) % 2, w % 2
    m_xm = rec.emit("vector", lambda e, nb=nb: e.tensor_copy(
        out=XMCOL[:, nb * 8:nb * 8 + 8], in_=XW_3[:, :, TX - 1]),
        waits=after(*m_p3), inc="dv")
    XSH = SCR.ap()[:, 0:8192].bitcast(F32)
    m_sh = []
    for c in range(NCH):
        sh = 1 if c < 4 else 0
        m1 = rec.emit("vector", lambda e, c=c, sh=sh: e.tensor_copy(
            out=XSH[:, c * WIN:(c + 1) * WIN], in_=XWs(c, sh, sh + WIN)),
            waits=after(*(m_p3 if c == 0 else [])), inc="dv")
        if c >= 4:
            m1 = rec.emit("vector", lambda e, c=c, cb=cb: e.tensor_copy(
                out=XSH[:, c * WIN:c * WIN + 1],
                in_=XMCOL[:, cb * 8 + c:cb * 8 + c + 1]),
                waits=after(m1, st["xmprev"]), inc="dv")
        m_sh.append(m1)
    st["xmprev"] = m_xm
    m_ln = emit_ln(ctx, lambda c, a, b: XSH[:, c * WIN + a:c * WIN + b],
                   lambda c, a, b: XLN[:, c * WIN + a:c * WIN + b],
                   WIN, lambda c: [m_sh[c], m_sh[-1]], m_onc, m_onr,
                   g_ap=lambda c, l=l: GB[:, l * NCH + c:l * NCH + c + 1],
                   b_ap=lambda c, l=l: GB[:, 2 * NCH + l * NCH + c:2 * NCH + l * NCH + c + 1])

    m_ff = {}
    m_gev_prev = None
    for i in range(NPAIR):
        Ma = 128 if i < NPAIR - 1 else 42
        buf = i % 2
        m_w1a = rec.dma("gpsimd", W1SL[:, buf * 2048:buf * 2048 + 8 * Ma],
                        w1_r[:, l, :, i * 128:i * 128 + Ma],
                        waits=after(m_ff.get(i - 2)), sem="ld_w1%d" % buf)
        m_w1g = rec.dma("gpsimd", W1SL[:, buf * 2048 + 1024:buf * 2048 + 1024 + 8 * Ma],
                        w1_r[:, l, :, 2730 + i * 128:2730 + i * 128 + Ma],
                        sem="ld_w1%d" % buf)
        ab = (i % 2) * 2 * 512
        gb = 512 + (i % 2) * 2 * 512
        mma = rec.emit("tensor", lambda e, i=i, Ma=Ma, ab=ab: e.matmul(
            PS[0:Ma, ab:ab + 512], B1[0:1, i * 128:i * 128 + Ma], ONR5[:],
            start=True, stop=False),
            waits=after(m_w1g, m_ln if i == 0 else None, m_ff.get(i - 2)), inc=None)
        for kc in range(NCH):
            mma = rec.emit("tensor", lambda e, kc=kc, Ma=Ma, buf=buf, ab=ab: e.matmul(
                PS[0:Ma, ab:ab + 512],
                W1SL[:, buf * 2048 + kc * Ma:buf * 2048 + (kc + 1) * Ma],
                XLN[:, kc * WIN:(kc + 1) * WIN], start=False, stop=(kc == NCH - 1)),
                inc="ts" if kc == NCH - 1 else None)
        mmg = rec.emit("tensor", lambda e, i=i, Ma=Ma, gb=gb: e.matmul(
            PS[0:Ma, gb:gb + 512], B1[0:1, 2730 + i * 128:2730 + i * 128 + Ma], ONR5[:],
            start=True, stop=False), waits=after(m_w1g, m_gev_prev), inc=None)
        for kc in range(NCH):
            mmg = rec.emit("tensor", lambda e, kc=kc, Ma=Ma, buf=buf, gb=gb: e.matmul(
                PS[0:Ma, gb:gb + 512],
                W1SL[:, buf * 2048 + 1024 + kc * Ma:buf * 2048 + 1024 + (kc + 1) * Ma],
                XLN[:, kc * WIN:(kc + 1) * WIN], start=False, stop=(kc == NCH - 1)),
                inc="ts" if kc == NCH - 1 else None)
        m_gev = rec.emit("scalar", lambda e, Ma=Ma, gb=gb: e.activation(
            out=QRAW[0:Ma, :], in_=PS[0:Ma, gb:gb + 512], func=AF.Gelu),
            waits=after(mmg, m_ff.get(i - 1)), inc="ac")
        m_gev_prev = m_gev
        m_ff[i] = rec.emit("vector", lambda e, i=i, Ma=Ma, ab=ab: e.tensor_tensor(
            out=SCR[0:Ma, i * WIN:(i + 1) * WIN], in0=PS[0:Ma, ab:ab + 512],
            in1=QRAW[0:Ma, :], op=ALU.mult), waits=after(mma, m_gev), inc="dv")
    m_stt5 = {}
    for m in range(NCH):
        bk = 2 + m % 2
        buf = m % 2
        m_w2a = rec.dma("gpsimd", W2SL[:, buf * 2816:buf * 2816 + 21 * 128],
                        w2f[l * 2730:l * 2730 + 2688, m * 128:(m + 1) * 128]
                        .rearrange("(k p) m -> p k m", p=128),
                        waits=after(m_stt5.get(m - 2)), sem="ld_w2%d" % buf)
        m_w2b = rec.dma("gpsimd", W2SL[0:42, buf * 2816 + 21 * 128:buf * 2816 + 22 * 128],
                        w2f[l * 2730 + 2688:l * 2730 + 2730, m * 128:(m + 1) * 128],
                        sem="ld_w2%d" % buf)
        mm = rec.emit("tensor", lambda e, m=m, bk=bk: e.matmul(
            bank(bk), B2[0:1, m * 128:(m + 1) * 128], ONR5[:], start=True, stop=False),
            waits=after(m_w2b, m_ff[NPAIR - 1] if m <= 1 else None,
                        m_ff[NPAIR - 2] if m <= 1 else None,
                        m_stt5.get(m - 2)), inc=None)
        for k2 in range(NK2):
            Ka = 128 if k2 < NK2 - 1 else 42
            mm = rec.emit("tensor", lambda e, k2=k2, Ka=Ka, bk=bk, buf=buf: e.matmul(
                bank(bk), W2SL[0:Ka, buf * 2816 + k2 * 128:buf * 2816 + k2 * 128 + 128],
                SCR[0:Ka, k2 * WIN:(k2 + 1) * WIN], start=False, stop=(k2 == NK2 - 1)),
                waits=after(m_ff[k2] if m == 0 else None),
                inc="ts" if k2 == NK2 - 1 else None)
        m_stt5[m] = rec.emit("vector", lambda e, m=m, bk=bk: e.scalar_tensor_tensor(
            out=XWs(m, 1, 1 + WIN), in0=bank(bk), scalar=1.0, in1=XWs(m, 1, 1 + WIN),
            op0=ALU.mult, op1=ALU.add), waits=after(mm), inc="dv")

    st["store"] = rec.dma("sync", x1_r[:, :, W0 + 1:W0 + 1 + WIN],
                          XW_3[:, :, 1:1 + WIN],
                          waits=after(*[m_stt5[m] for m in range(NCH)]),
                          sem="st_x1")


def build_s1(nc, layers=2, nwin_limit=None):
    rec = Rec(nc)
    ctx = Ctx(nc, rec)
    x_in = nc.dram_tensor("x", [D, 1 + NX1], F32, kind="ExternalInput").ap()
    wqkvf = nc.dram_tensor("wqkvf", [2 * D, 1536], F32, kind="ExternalInput").ap()
    w1f = nc.dram_tensor("w1f", [2 * D, 5460], F32, kind="ExternalInput").ap()
    w2f = nc.dram_tensor("w2f", [2 * 2730, 1024], F32, kind="ExternalInput").ap()
    woT = nc.dram_tensor("woT", [2 * 512, 1024], F32, kind="ExternalInput").ap()
    b1r = nc.dram_tensor("b1r", [2, 5460], F32, kind="ExternalInput").ap()
    b2r = nc.dram_tensor("b2r", [2, 1024], F32, kind="ExternalInput").ap()
    tqa = nc.dram_tensor("tqa", [128, NX1], F32, kind="ExternalInput").ap()
    tqb = nc.dram_tensor("tqb", [128, NX1], F32, kind="ExternalInput").ap()
    tka = nc.dram_tensor("tka", [128, NX1], F32, kind="ExternalInput").ap()
    tkb = nc.dram_tensor("tkb", [128, NX1], F32, kind="ExternalInput").ap()
    qscol = nc.dram_tensor("qscol", [128, 2], F32, kind="ExternalInput").ap()
    kscol = nc.dram_tensor("kscol", [128, 2], F32, kind="ExternalInput").ap()
    gcol = nc.dram_tensor("gcol", [128, 16], F32, kind="ExternalInput").ap()
    bcol = nc.dram_tensor("bcol", [128, 16], F32, kind="ExternalInput").ap()
    perm = nc.dram_tensor("perm", [128, 128], F32, kind="ExternalInput").ap()
    hsel = nc.dram_tensor("hsel", [128, 2], F32, kind="ExternalInput").ap()
    sel2 = nc.dram_tensor("sel2", [2, 128], F32, kind="ExternalInput").ap()
    ident = nc.dram_tensor("ident", [128, 128], F32, kind="ExternalInput").ap()
    mband = nc.dram_tensor("mband", [128, 640], F32, kind="ExternalInput").ap()
    hv = nc.dram_tensor("hv", [128, 36], F32, kind="ExternalInput").ap()
    onc = nc.dram_tensor("ones_col", [128, 1], F32, kind="ExternalInput").ap()
    onr = nc.dram_tensor("ones_row", [1, 128], F32, kind="ExternalInput").ap()
    onr5 = nc.dram_tensor("ones_row5", [1, WIN], F32, kind="ExternalInput").ap()
    x1d = nc.dram_tensor("x1d", [D, 1 + NX1], F32, kind="ExternalOutput")
    xo = nc.dram_tensor("xo", [D, T], F32, kind="ExternalOutput").ap()
    mil = emit_s1(ctx, x_in, x1d, xo, wqkvf, w1f, w2f, woT, b1r, b2r,
                  tqa, tqb, tka, tkb, qscol, kscol, gcol, bcol,
                  perm, hsel, sel2, ident, mband, hv, onc, onr, onr5,
                  layers=layers, nwin_limit=nwin_limit)
    rec.finalize(mil)
    return nc


def s1_host_tables(sb):
    half = np.arange(0, 64, 2, dtype=np.float64)
    inv_freq = 1.0 / 10000.0 ** (half / 64.0)
    base = (half + 0.4 * 64) / (1.4 * 64)
    p = np.arange(sb - HALO1, sb + T, dtype=np.float64)
    fr = p[:, None] * inv_freq[None, :]
    power = (p - 4096.0) / 256.0
    sc = base[None, :] ** power[:, None]
    cos, sin = np.cos(fr), np.sin(fr)

    def pack(t):
        t64 = np.concatenate([t, t], axis=1)
        return np.ascontiguousarray(
            np.concatenate([t64, t64], axis=1).T.astype(np.float32))
    return (pack(8.0 * cos * sc), pack(8.0 * sin * sc),
            pack(cos / sc), pack(sin / sc))


def s1_host_consts():
    permv = np.zeros((128, 128), np.float32)
    for b in (0, 64):
        for m in range(32):
            permv[b + m + 32, b + m] = -1.0
            permv[b + m, b + m + 32] = 1.0
    hselv = np.zeros((128, 2), np.float32)
    hselv[0:64, 0] = 1.0
    hselv[64:128, 1] = 1.0
    sel2v = np.zeros((2, 128), np.float32)
    sel2v[0, 0:64] = 1.0
    sel2v[1, 64:128] = 1.0
    r = np.arange(128)[:, None]
    cc = np.arange(640)[None, :]
    mbandv = np.where((cc >= r) & (cc <= r + 512), 0.0, -1e9).astype(np.float32)
    return permv, hselv, sel2v, mbandv


def s1_host_hv(batch_start):
    hvv = np.ones((128, 36), np.float32)
    hvv[:, 0:4] = 0.0
    if batch_start:
        hvv[:, 4:16] = 0.0
    return hvv


def run_s1(x_full, wqkv, q_scale, k_scale, wo, ln_g, ln_b, w1, b1, w2, b2,
           layers=2, nwin_limit=None, want_x1=False):
    nc = build_s1(bass.Bass("TRN2", target_bir_lowering=False,
                            dynamic_dma_scratch_size=16384),
                  layers=layers, nwin_limit=nwin_limit)
    flat = np.ascontiguousarray(np.asarray(x_full).reshape(2 * 8192, D))
    permv, hselv, sel2v, mbandv = s1_host_consts()
    wqkv_h = np.ascontiguousarray(np.asarray(wqkv).reshape(2 * D, 1536))
    w1_h = np.ascontiguousarray(np.asarray(w1).reshape(2 * D, 5460))
    w2_h = np.ascontiguousarray(np.asarray(w2).reshape(2 * 2730, 1024))
    wo_h = np.ascontiguousarray(np.asarray(wo).reshape(2 * 512, 1024))
    qsc = np.zeros((128, 2), np.float32)
    ksc = np.zeros((128, 2), np.float32)
    gcv = np.zeros((128, 16), np.float32)
    bcv = np.zeros((128, 16), np.float32)
    for l in range(2):
        qsc[:, l] = np.tile(np.asarray(q_scale)[l], 2)
        ksc[:, l] = np.tile(np.asarray(k_scale)[l], 2)
        for c in range(NCH):
            gcv[:, l * 8 + c] = np.asarray(ln_g)[l, c * 128:(c + 1) * 128]
            bcv[:, l * 8 + c] = np.asarray(ln_b)[l, c * 128:(c + 1) * 128]
    in_maps = []
    for c in range(NCORES):
        s = c * T
        sb = s % 8192
        xs = np.zeros((1 + NX1, D), np.float32)
        if sb >= HALO1:
            xs[1:1 + HALO1] = flat[s - HALO1:s]
        xs[1 + HALO1:] = flat[s:s + T]
        tqa_, tqb_, tka_, tkb_ = s1_host_tables(sb)
        in_maps.append({
            "x": np.ascontiguousarray(xs.T),
            "wqkvf": wqkv_h, "w1f": w1_h, "w2f": w2_h, "woT": wo_h,
            "b1r": np.ascontiguousarray(np.asarray(b1, np.float32)),
            "b2r": np.ascontiguousarray(np.asarray(b2, np.float32)),
            "tqa": tqa_, "tqb": tqb_, "tka": tka_, "tkb": tkb_,
            "qscol": qsc, "kscol": ksc, "gcol": gcv, "bcol": bcv,
            "perm": permv, "hsel": hselv, "sel2": sel2v,
            "ident": np.eye(128, dtype=np.float32),
            "mband": mbandv, "hv": s1_host_hv(sb == 0),
            "ones_col": np.ones((128, 1), np.float32),
            "ones_row": np.ones((1, 128), np.float32),
            "ones_row5": np.ones((1, WIN), np.float32),
        })
    res = run_bass_kernel_spmd(nc, in_maps, core_ids=list(range(NCORES)))
    out = np.concatenate([res.results[c]["xo"].T for c in range(NCORES)], axis=0)
    if want_x1:
        x1s = [res.results[c]["x1d"].T for c in range(NCORES)]
        return out.reshape(2, 8192, D), x1s
    return out.reshape(2, 8192, D)



def shard_tokens(x_full):
    flat = np.ascontiguousarray(x_full.reshape(2 * 8192, D))
    return [np.ascontiguousarray(flat[c * T:(c + 1) * T].T) for c in range(NCORES)]


def boundary_cols(x_full):
    flat = x_full.reshape(2 * 8192, D)
    outs = []
    for c in range(NCORES):
        s = c * T
        if s % 8192 == 0:
            outs.append(np.zeros((D, 1), np.float32))
        else:
            outs.append(np.ascontiguousarray(flat[s - 1:s].T))
    return outs


def run_s4(x_full, W_r, W_k, W_v, mu_r, mu_k):
    nc = build_s4(bass.Bass("TRN2", target_bir_lowering=False,
                            dynamic_dma_scratch_size=16384))
    shards = shard_tokens(x_full)
    bnds = boundary_cols(x_full)
    wrT = np.ascontiguousarray(W_r.T)
    wkT = np.ascontiguousarray(W_k.T)
    wvT = np.ascontiguousarray(W_v.T)
    in_maps = []
    for c in range(NCORES):
        in_maps.append({
            "x": np.concatenate([bnds[c], shards[c]], axis=1),
            "wrT": wrT, "wkT": wkT, "wvT": wvT,
            "mur": np.ascontiguousarray(mu_r.reshape(NCH, 128).T),
            "muk": np.ascontiguousarray(mu_k.reshape(NCH, 128).T),
            "ones_col": np.ones((128, 1), np.float32),
            "ones_row": np.ones((1, 128), np.float32),
        })
    res = run_bass_kernel_spmd(nc, in_maps, core_ids=list(range(NCORES)))
    out = np.concatenate([res.results[c]["xo"].T for c in range(NCORES)], axis=0)
    return out.reshape(2, 8192, D)


def run_s2(x_full, Wq, Wk, Wv, Wo, Kq, Kk, stage=100):
    nc = build_s2(bass.Bass("TRN2", target_bir_lowering=False,
                            dynamic_dma_scratch_size=16384), stage=stage)
    shards = shard_tokens(x_full)
    kqp, kkp, tri = s2_host_inputs(np.asarray(Kq), np.asarray(Kk))
    in_maps = []
    for c in range(NCORES):
        selv = np.zeros((128, NCORES), np.float32)
        b0 = (c // 4) * 4
        for r in range(b0, c):
            selv[:, r] = 1.0
        in_maps.append({
            "x": shards[c], "wq": np.ascontiguousarray(Wq), "wk": np.ascontiguousarray(Wk),
            "wv": np.ascontiguousarray(Wv), "wo": np.ascontiguousarray(Wo),
            "kqp": kqp, "kkp": kkp, "sel": selv,
            "ones_col": np.ones((128, 1), np.float32),
            "ones_row": np.ones((1, 128), np.float32),
            "tri": tri, "ident": np.eye(128, dtype=np.float32),
        })
    res = run_bass_kernel_spmd(nc, in_maps, core_ids=list(range(NCORES)))
    out = np.concatenate([res.results[c]["xo"].T for c in range(NCORES)], axis=0)
    return out.reshape(2, 8192, D)




def emit_xhalo(ctx, pfx, src_body, dstL, dstR, L, R, selL, selR):
    nc, rec = ctx.nc, ctx.rec
    ncols = L + R
    cc_in = nc.dram_tensor(pfx + "cci", [D, ncols], F32)
    cc_out = nc.dram_tensor(pfx + "cco", [4 * D, ncols], F32)
    SEL = ctx.sb("SEL", [128, 8], F32)
    HB = ctx.sb("HB", [128, NCH * 512], F32)
    RB = ctx.sb("RBUF", [128, 2 * NCH * 512], F32)

    m_sl = rec.dma("sync", SEL[:, 0:4], selL, sem="hx_sl")
    m_sr = rec.dma("sync", SEL[:, 4:8], selR, sem="hx_sr")
    ms = []
    if L:
        ms.append(rec.dma("sync", cc_in.ap()[:, 0:L], src_body[:, T - L:T],
                          sem="hx_cil",
                          allow_slow_non_contiguous=(L < 8)))
    if R:
        ms.append(rec.dma("sync", cc_in.ap()[:, L:L + R], src_body[:, 0:R],
                          sem="hx_cir",
                          allow_slow_non_contiguous=True))
    m_ag = rec.emit("gpsimd", lambda e: e.collective_compute(
        "AllGather", ALU.bypass, replica_groups=[[0, 1, 2, 3], [4, 5, 6, 7]],
        ins=[cc_in.ap()], outs=[cc_out.ap()]), waits=after(*ms), inc="cc")
    outms = []
    m_add = {}
    for (side, base, ncol, dst, selo) in (("L", 0, L, dstL, 0),
                                          ("R", L, R, dstR, 4)):
        if ncol == 0:
            continue
        for c0 in range(0, ncol, 512):
            ck = min(512, ncol - c0)
            mz = rec.emit("vector", lambda e, ck=ck: e.memset(
                HB[:, 0:NCH * ck], 0.0), waits=after(outms[-1] if outms else None),
                inc="dv")
            ma = mz
            for r in range(4):
                m_ld = rec.dma("sync",
                               RB.ap()[:, (r % 2) * NCH * 512:(r % 2) * NCH * 512 + NCH * ck]
                               .rearrange("p (c t) -> p c t", c=NCH),
                               cc_out.ap()[r * D:(r + 1) * D, base + c0:base + c0 + ck]
                               .rearrange("(c p) t -> p c t", p=128),
                               waits=after(m_ag if r == 0 else None,
                                           m_add.get(r - 2)),
                               sem="hx_" + side + "r%d" % (r % 2),
                               allow_slow_non_contiguous=(ck < 8))
                ma = rec.emit("vector", lambda e, r=r, ck=ck, selo=selo: e.scalar_tensor_tensor(
                    out=HB[:, 0:NCH * ck], in0=RB[:, (r % 2) * NCH * 512:(r % 2) * NCH * 512 + NCH * ck],
                    scalar=SEL[:, selo + r:selo + r + 1], in1=HB[:, 0:NCH * ck],
                    op0=ALU.mult, op1=ALU.add),
                    waits=after(m_ld, ma, m_sl if r == 0 else None,
                                m_sr if r == 0 else None), inc="dv")
                m_add[r] = ma
            m_st = rec.dma("sync", dst[:, c0:c0 + ck]
                           .rearrange("(c p) t -> p c t", p=128),
                           HB.ap()[:, 0:NCH * ck].rearrange("p (c t) -> p c t", c=NCH),
                           waits=after(ma), sem="hx_" + side + "st",
                           allow_slow_non_contiguous=(ck < 8))
            outms.append(m_st)
    return outms


def build_fused(nc):
    rec = Rec(nc)
    ctx = Ctx(nc, rec, arena=(17024, 229376))
    EI = lambda n, sh: nc.dram_tensor(n, sh, F32, kind="ExternalInput").ap()
    x_in = EI("x", [D, 1 + NX1])
    wqkvf = EI("wqkvf", [2 * D, 1536]); w1f = EI("w1f", [2 * D, 5460])
    w2f = EI("w2f", [2 * 2730, 1024]); woT = EI("woT", [2 * 512, 1024])
    b1r = EI("b1r", [2, 5460]); b2r = EI("b2r", [2, 1024])
    tqa = EI("tqa", [128, NX1]); tqb = EI("tqb", [128, NX1])
    tka = EI("tka", [128, NX1]); tkb = EI("tkb", [128, NX1])
    qscol = EI("qscol", [128, 2]); kscol = EI("kscol", [128, 2])
    gcol = EI("gcol", [128, 16]); bcol = EI("bcol", [128, 16])
    perm = EI("perm", [128, 128]); hsel = EI("hsel", [128, 2])
    sel2 = EI("sel2", [2, 128]); ident = EI("ident", [128, 128])
    mband = EI("mband", [128, 640]); hv = EI("hv", [128, 36])
    onc = EI("ones_col", [128, 1]); onr = EI("ones_row", [1, 128])
    onr5 = EI("ones_row5", [1, WIN])
    wq = EI("wq", [D, D]); wk = EI("wk", [D, D]); wv = EI("wv", [D, D])
    wo2 = EI("wo", [D, D])
    kqp = EI("kqp", [NHEAD * 128, 128]); kkp = EI("kkp", [NHEAD * 128, 128])
    sel = EI("sel", [128, NCORES]); tri = EI("tri", [128, 128])
    wrT = EI("wrT", [D, D]); wkT = EI("wkT", [D, D]); wvT = EI("wvT", [D, D])
    mur = EI("mur", [128, NCH]); muk = EI("muk", [128, NCH])
    pewT = EI("pewT", [D, D]); qkvwT = EI("qkvwT", [D, 3 * D])
    owT = EI("owT", [D, D]); taps = EI("taps", [128, 3 * NCH])
    selL = EI("selL", [128, 4]); selR = EI("selR", [128, 4])
    x1d = nc.dram_tensor("x1d", [D, 1 + NX1], F32, kind="ExternalOutput")
    xs2 = nc.dram_tensor("xs2", [D, T], F32, kind="ExternalOutput")
    xs3 = nc.dram_tensor("xs3", [D, 1 + NX1], F32, kind="ExternalOutput")
    xs4 = nc.dram_tensor("xs4", [D, T + 1], F32, kind="ExternalOutput")
    xs5 = nc.dram_tensor("xs5", [D, T + 2], F32, kind="ExternalOutput")
    xs6 = nc.dram_tensor("xs6", [D, T + 1], F32, kind="ExternalOutput")
    oacc = nc.dram_tensor("oacc", [NHEAD * 64, T], F32, kind="ExternalOutput")
    ozacc = nc.dram_tensor("ozacc", [NHEAD, T], F32, kind="ExternalOutput")
    xo = nc.dram_tensor("xo", [D, T], F32, kind="ExternalOutput").ap()

    s1_args = (wqkvf, w1f, w2f, woT, b1r, b2r, tqa, tqb, tka, tkb, qscol,
               kscol, gcol, bcol, perm, hsel, sel2, ident, mband, hv, onc,
               onr, onr5)
    ctx.reset("s1_")
    m1 = emit_s1(ctx, x_in, x1d, xs2.ap(), *s1_args)
    rec.gate = tuple(m1)
    ctx.reset("s2_")
    m2 = emit_s2(ctx, xs2.ap(), xs3.ap()[:, 1 + HALO1:1 + NX1], wq, wk, wv,
                 wo2, kqp, kkp, sel, onc, onr, tri, ident)
    rec.gate = tuple(m2)
    ctx.reset("h3_")
    mh3 = emit_xhalo(ctx, "h3_", xs3.ap()[:, 1 + HALO1:1 + NX1],
                     xs3.ap()[:, 1:1 + HALO1], None, HALO1, 0, selL, selR)
    rec.gate = tuple(m2) + tuple(mh3)
    ctx.reset("s3_")
    m3 = emit_s1(ctx, xs3.ap(), x1d, xs4.ap()[:, 1:1 + T], *s1_args)
    rec.gate = tuple(m3)
    ctx.reset("h4_")
    mh4 = emit_xhalo(ctx, "h4_", xs4.ap()[:, 1:1 + T], xs4.ap()[:, 0:1],
                     None, 1, 0, selL, selR)
    rec.gate = tuple(m3) + tuple(mh4)
    ctx.reset("s4_")
    m4 = emit_s4(ctx, xs4.ap(), xs5.ap()[:, 1:1 + T], wrT, wkT, wvT, mur,
                 muk, onc, onr)
    rec.gate = tuple(m4)
    ctx.reset("h5_")
    mh5 = emit_xhalo(ctx, "h5_", xs5.ap()[:, 1:1 + T], xs5.ap()[:, 0:1],
                     xs5.ap()[:, 1 + T:2 + T], 1, 1, selL, selR)
    rec.gate = tuple(m4) + tuple(mh5)
    ctx.reset("s5_")
    m5 = emit_s5(ctx, xs5.ap(), xs6.ap()[:, 1:1 + T], pewT, qkvwT, owT, taps,
                 onc, onr, sel2, oacc, ozacc)
    rec.gate = tuple(m5)
    ctx.reset("h6_")
    mh6 = emit_xhalo(ctx, "h6_", xs6.ap()[:, 1:1 + T], xs6.ap()[:, 0:1],
                     None, 1, 0, selL, selR)
    rec.gate = tuple(m5) + tuple(mh6)
    ctx.reset("s6_")
    m6 = emit_s4(ctx, xs6.ap(), xo, wrT, wkT, wvT, mur, muk, onc, onr)
    rec.finalize(m6)
    return nc


def kernel_fused(inputs):
    nc = build_fused(bass.Bass("TRN2", target_bir_lowering=False,
                               dynamic_dma_scratch_size=16384))
    flat = np.ascontiguousarray(
        np.asarray(inputs["L0_attn_x"], np.float32).reshape(2 * 8192, D))
    permv, hselv, sel2v, mbandv = s1_host_consts()
    kqp, kkp, tri = s2_host_inputs(np.asarray(inputs["L0_attn_Kq"]),
                                   np.asarray(inputs["L0_attn_Kk"]))
    wqkv_h = np.ascontiguousarray(np.asarray(inputs["L0_ffn_wqkv"]).reshape(2 * D, 1536))
    w1_h = np.ascontiguousarray(np.asarray(inputs["L0_ffn_w1"]).reshape(2 * D, 5460))
    w2_h = np.ascontiguousarray(np.asarray(inputs["L0_ffn_w2"]).reshape(2 * 2730, 1024))
    wo_h = np.ascontiguousarray(np.asarray(inputs["L0_ffn_wo"]).reshape(2 * 512, 1024))
    qsc = np.zeros((128, 2), np.float32); ksc = np.zeros((128, 2), np.float32)
    gcv = np.zeros((128, 16), np.float32); bcv = np.zeros((128, 16), np.float32)
    for l in range(2):
        qsc[:, l] = np.tile(np.asarray(inputs["L0_ffn_q_scale"])[l], 2)
        ksc[:, l] = np.tile(np.asarray(inputs["L0_ffn_k_scale"])[l], 2)
        for c in range(NCH):
            gcv[:, l * 8 + c] = np.asarray(inputs["L0_ffn_ln_g"])[l, c * 128:(c + 1) * 128]
            bcv[:, l * 8 + c] = np.asarray(inputs["L0_ffn_ln_b"])[l, c * 128:(c + 1) * 128]
    taps_h = np.ascontiguousarray(np.asarray(inputs["L1_attn_conv_w"], np.float32)[:, 0, :, 1])
    taps_t = np.zeros((128, 3 * NCH), np.float32)
    for c in range(NCH):
        taps_t[:, 3 * c:3 * (c + 1)] = taps_h[c * 128:(c + 1) * 128]
    mu_r = np.asarray(inputs["L1_ffn_mu_r"], np.float32)
    mu_k = np.asarray(inputs["L1_ffn_mu_k"], np.float32)
    in_maps = []
    for c in range(NCORES):
        s = c * T
        sb = s % 8192
        xs = np.zeros((1 + NX1, D), np.float32)
        if sb >= HALO1:
            xs[1:1 + HALO1] = flat[s - HALO1:s]
        xs[1 + HALO1:] = flat[s:s + T]
        tqa_, tqb_, tka_, tkb_ = s1_host_tables(sb)
        selv = np.zeros((128, NCORES), np.float32)
        for r in range((c // 4) * 4, c):
            selv[:, r] = 1.0
        slv = np.zeros((128, 4), np.float32)
        srv = np.zeros((128, 4), np.float32)
        if c % 4 > 0:
            slv[:, c % 4 - 1] = 1.0
        if c % 4 < 3:
            srv[:, c % 4 + 1] = 1.0
        in_maps.append({
            "x": np.ascontiguousarray(xs.T),
            "wqkvf": wqkv_h, "w1f": w1_h, "w2f": w2_h, "woT": wo_h,
            "b1r": np.ascontiguousarray(np.asarray(inputs["L0_ffn_b1"], np.float32)),
            "b2r": np.ascontiguousarray(np.asarray(inputs["L0_ffn_b2"], np.float32)),
            "tqa": tqa_, "tqb": tqb_, "tka": tka_, "tkb": tkb_,
            "qscol": qsc, "kscol": ksc, "gcol": gcv, "bcol": bcv,
            "perm": permv, "hsel": hselv, "sel2": sel2v,
            "ident": np.eye(128, dtype=np.float32),
            "mband": mbandv, "hv": s1_host_hv(sb == 0),
            "ones_col": np.ones((128, 1), np.float32),
            "ones_row": np.ones((1, 128), np.float32),
            "ones_row5": np.ones((1, WIN), np.float32),
            "wq": np.ascontiguousarray(np.asarray(inputs["L0_attn_Wq"], np.float32)),
            "wk": np.ascontiguousarray(np.asarray(inputs["L0_attn_Wk"], np.float32)),
            "wv": np.ascontiguousarray(np.asarray(inputs["L0_attn_Wv"], np.float32)),
            "wo": np.ascontiguousarray(np.asarray(inputs["L0_attn_Wo"], np.float32)),
            "kqp": kqp, "kkp": kkp, "sel": selv, "tri": tri,
            "wrT": np.ascontiguousarray(np.asarray(inputs["L1_ffn_W_r"], np.float32).T),
            "wkT": np.ascontiguousarray(np.asarray(inputs["L1_ffn_W_k"], np.float32).T),
            "wvT": np.ascontiguousarray(np.asarray(inputs["L1_ffn_W_v"], np.float32).T),
            "mur": np.ascontiguousarray(mu_r.reshape(NCH, 128).T),
            "muk": np.ascontiguousarray(mu_k.reshape(NCH, 128).T),
            "pewT": np.ascontiguousarray(np.asarray(inputs["L1_attn_pe_w"], np.float32).T),
            "qkvwT": np.ascontiguousarray(np.asarray(inputs["L1_attn_qkv_w"], np.float32).T),
            "owT": np.ascontiguousarray(np.asarray(inputs["L1_attn_out_w"], np.float32).T),
            "taps": taps_t,
            "selL": slv, "selR": srv,
        })
    res = run_bass_kernel_spmd(nc, in_maps, core_ids=list(range(NCORES)))
    out = np.concatenate([res.results[c]["xo"].T for c in range(NCORES)], axis=0)
    return np.ascontiguousarray(out.reshape(2, 8192, D).astype(np.float32))


def kernel(**inputs):
    x = np.asarray(inputs["L0_attn_x"], np.float32)
    s1_args = (inputs["L0_ffn_wqkv"], inputs["L0_ffn_q_scale"],
               inputs["L0_ffn_k_scale"], inputs["L0_ffn_wo"],
               inputs["L0_ffn_ln_g"], inputs["L0_ffn_ln_b"],
               inputs["L0_ffn_w1"], inputs["L0_ffn_b1"],
               inputs["L0_ffn_w2"], inputs["L0_ffn_b2"])
    s4_args = (np.asarray(inputs["L1_ffn_W_r"], np.float32),
               np.asarray(inputs["L1_ffn_W_k"], np.float32),
               np.asarray(inputs["L1_ffn_W_v"], np.float32),
               np.asarray(inputs["L1_ffn_mu_r"], np.float32),
               np.asarray(inputs["L1_ffn_mu_k"], np.float32))
    del x, s1_args, s4_args
    return kernel_fused(inputs)


# version 04
# speedup vs baseline: 1.0809x; 1.0791x over previous
import numpy as np
import ml_dtypes
import concourse.bass as bass
import concourse.mybir as mybir
from concourse.bass_utils import run_bass_kernel_spmd

F32 = mybir.dt.float32
BF16 = mybir.dt.bfloat16
AF = mybir.ActivationFunctionType
ALU = mybir.AluOpType
AX = mybir.AxisListType
BF = ml_dtypes.bfloat16

NCORES = 8
T = 2048
D = 1024
NCH = 8
WIN = 512
NW = T // WIN

ENGINES = ("tensor", "scalar", "vector", "gpsimd", "sync")


class Rec:

    def __init__(self, nc):
        self.nc = nc
        self.steps = {e: [] for e in ENGINES}
        self.counts = {}
        self.gate = ()

    def emit(self, eng, fn, waits=(), inc=None, inc_n=1):
        ws = tuple((m[0], m[1]) for m in tuple(self.gate) + tuple(waits)
                   if m is not None)
        self.steps[eng].append((fn, ws, inc, inc_n))
        if inc is None:
            return None
        self.counts[inc] = self.counts.get(inc, 0) + inc_n
        return (inc, self.counts[inc])

    def dma(self, eng, out, in_, waits=(), sem="ld", **kw):
        return self.emit(eng, lambda e, o=out, i=in_, kw=kw: e.dma_start(out=o, in_=i, **kw),
                         waits=waits, inc=sem, inc_n=16)

    def finalize(self, final_waits):
        nc = self.nc
        self.emit("vector", lambda e: e.engine_nop(), waits=list(final_waits))
        handles = {n: nc.alloc_semaphore(n) for n in self.counts}
        for n, c in self.counts.items():
            assert c < 60000, f"semaphore {n} count {c} too large"
        with nc.Block() as block:
            for eng in ENGINES:
                steps = self.steps[eng]

                def body(e, steps=steps):
                    last = {}
                    for fn, waits, inc, inc_n in steps:
                        for (s, v) in waits:
                            if last.get(s, -1) < v:
                                e.wait_ge(handles[s], v)
                                last[s] = v
                        ins = fn(e)
                        if inc is not None:
                            ins.then_inc(handles[inc], inc_n)
                getattr(block, eng)(body)


def after(*ms):
    return [m for m in ms if m is not None]


class Ctx:
    def __init__(self, nc, rec, arena=None):
        self.nc = nc
        self.rec = rec
        self.buf = {}
        self.arena = arena
        self.ptr = arena[0] if arena else None
        self.pfx = ""
        self.ps_shared = None

    def reset(self, pfx):
        self.ptr = self.arena[0]
        self.pfx = pfx
        self.buf = {}

    def sb(self, name, shape, dtype):
        if self.arena is None:
            t = self.nc.alloc_sbuf_tensor(name, shape, dtype)
        else:
            nb = shape[1] * (4 if dtype == F32 else 2)
            off = (self.ptr + 31) & ~31
            assert off + nb <= self.arena[1], (self.pfx, name, off, nb)
            t = self.nc.alloc_sbuf_tensor_at(self.pfx + name, shape, dtype,
                                             offset=off)
            self.ptr = off + nb
        self.buf[name] = t
        return t

    def ps(self, name, shape, dtype=F32):
        if self.ps_shared is not None:
            self.buf[name] = self.ps_shared
            return self.ps_shared
        t = self.nc.alloc_psum_tensor(name, shape, dtype)
        self.buf[name] = t
        if self.arena is not None:
            self.ps_shared = t
        return t



def emit_ln(ctx, src, dst, width, cast_waits, ones_col_m, ones_row_m,
            g_ap=None, b_ap=None):
    nc, rec = ctx.nc, ctx.rec
    w = width
    XB, XSQ, SROW, PS = ctx.buf["XB"], ctx.buf["XSQ"], ctx.buf["SROW"], ctx.buf["PS"]
    SRB, XAF = ctx.buf["SRB"], ctx.buf["XAF"]
    ONC = ctx.buf["ONC"]
    sx = PS[0:1, 7 * 512:7 * 512 + w]
    sxx = PS[0:1, 6 * 512:6 * 512 + w]
    bA = PS[:, 4 * 512:4 * 512 + w]
    bB = PS[:, 5 * 512:5 * 512 + w]

    m_cast = []
    for c in range(NCH):
        m = rec.emit("vector", lambda e, c=c: e.tensor_copy(
            out=XB[:, c * w:(c + 1) * w], in_=src(c, 0, w)),
            waits=after(*(cast_waits(c) or [])), inc="dv")
        m_cast.append(m)
    m_sx = None
    for c in range(NCH):
        m_sx = rec.emit("tensor", lambda e, c=c: e.matmul(
            sx, ONC[:, 0:1], XB[:, c * w:(c + 1) * w],
            start=(c == 0), stop=(c == NCH - 1)),
            waits=after(m_cast[-1], ones_col_m) if c == 0 else (),
            inc="ts" if c == NCH - 1 else None)
    m_sxx = None
    for c in range(NCH):
        sqm = rec.emit("scalar", lambda e, c=c: e.activation(
            out=XSQ[:, 0:w], in_=XB[:, c * w:(c + 1) * w],
            func=AF.Square),
            waits=after(m_cast[c], m_sxx if c >= 1 else None), inc="ac")
        m_sxx = rec.emit("tensor", lambda e, c=c: e.matmul(
            sxx, ONC[:, 0:1], XSQ[:, 0:w],
            start=(c == 0), stop=(c == NCH - 1)),
            waits=after(sqm), inc="ts")
    m1 = rec.emit("vector", lambda e: e.tensor_scalar(
        out=SROW[0:1, 0:w], in0=sx, scalar1=1.0 / D, scalar2=None, op0=ALU.mult),
        waits=after(m_sx, m_sxx), inc="dv")
    m2 = rec.emit("vector", lambda e: e.tensor_scalar(
        out=SROW[0:1, w:2 * w], in0=sxx, scalar1=1.0 / D, scalar2=None, op0=ALU.mult),
        inc="dv")
    m3 = rec.emit("vector", lambda e: e.tensor_tensor(
        out=SROW[0:1, 3 * w:4 * w], in0=SROW[0:1, 0:w], in1=SROW[0:1, 0:w], op=ALU.mult),
        waits=after(m1), inc="dv")
    m4 = rec.emit("vector", lambda e: e.tensor_tensor(
        out=SROW[0:1, 2 * w:3 * w], in0=SROW[0:1, w:2 * w], in1=SROW[0:1, 3 * w:4 * w],
        op=ALU.subtract), waits=after(m2, m3), inc="dv")
    m_var = rec.emit("vector", lambda e: e.tensor_scalar(
        out=SROW[0:1, 2 * w:3 * w], in0=SROW[0:1, 2 * w:3 * w], scalar1=1e-5,
        scalar2=None, op0=ALU.add), waits=after(m4), inc="dv")
    m_ln2 = rec.emit("scalar", lambda e: e.activation(
        out=SROW[0:1, 2 * w:3 * w], in_=SROW[0:1, 2 * w:3 * w], func=AF.Ln),
        waits=after(m_var), inc="ac")
    m_rstd = rec.emit("scalar", lambda e: e.activation(
        out=SROW[0:1, 2 * w:3 * w], in_=SROW[0:1, 2 * w:3 * w], func=AF.Exp, scale=-0.5),
        waits=after(m_ln2), inc="ac")
    m_B = rec.emit("vector", lambda e: e.scalar_tensor_tensor(
        out=SROW[0:1, 3 * w:4 * w], in0=SROW[0:1, 0:w], scalar=-1.0,
        in1=SROW[0:1, 2 * w:3 * w], op0=ALU.mult, op1=ALU.mult),
        waits=after(m_rstd), inc="dv")
    m_hl = emit_hilo(ctx, SROW[0:1, 2 * w:3 * w], SRB[0:1, 0:w], SRB[0:1, w:2 * w], m_rstd)
    m_hlB = emit_hilo(ctx, SROW[0:1, 3 * w:4 * w], SRB[0:1, 2 * w:3 * w], SRB[0:1, 3 * w:4 * w], m_B)
    m_Ab = emit_bcast(ctx, bA, SRB[0:1, 0:w], SRB[0:1, w:2 * w], after(m_hl, ones_row_m))
    m_Bb = emit_bcast(ctx, bB, SRB[0:1, 2 * w:3 * w], SRB[0:1, 3 * w:4 * w], after(m_hlB))
    m_last = None
    for c in range(NCH):
        mm1 = rec.emit("vector", lambda e, c=c: e.tensor_tensor(
            out=XAF[:, 0:w], in0=src(c, 0, w), in1=bA, op=ALU.mult),
            waits=after(m_Ab if c == 0 else None, m_last), inc="dv")
        if g_ap is None:
            m_last = rec.emit("vector", lambda e, c=c: e.tensor_tensor(
                out=dst(c, 0, w), in0=XAF[:, 0:w], in1=bB, op=ALU.add),
                waits=after(m_Bb if c == 0 else None, mm1), inc="dv")
        else:
            XAF2 = ctx.buf["XAF2"]
            mmid = rec.emit("vector", lambda e: e.tensor_tensor(
                out=XAF2[:, 0:w], in0=XAF[:, 0:w], in1=bB, op=ALU.add),
                waits=after(m_Bb if c == 0 else None, mm1), inc="dv")
            m_last = rec.emit("vector", lambda e, c=c, s1=g_ap(c), s2=b_ap(c): e.tensor_scalar(
                out=dst(c, 0, w), in0=XAF2[:, 0:w], scalar1=s1, scalar2=s2,
                op0=ALU.mult, op1=ALU.add), waits=after(mmid), inc="dv")
    return m_last


def emit_hilo(ctx, row_f32, hi_bf, lo_bf, dep):
    rec = ctx.rec
    mh = rec.emit("vector", lambda e: e.tensor_copy(out=hi_bf, in_=row_f32),
                  waits=after(dep), inc="dv")
    return rec.emit("vector", lambda e: e.tensor_tensor(
        out=lo_bf, in0=row_f32, in1=hi_bf, op=ALU.subtract), waits=after(mh), inc="dv")


def emit_bcast(ctx, psum_dst, hi_bf, lo_bf, waits):
    rec = ctx.rec
    ONR = ctx.buf["ONR"]
    rec.emit("tensor", lambda e: e.matmul(
        psum_dst, ONR[0:1, 0:128], hi_bf, start=True, stop=False),
        waits=waits, inc=None)
    return rec.emit("tensor", lambda e: e.matmul(
        psum_dst, ONR[0:1, 0:128], lo_bf, start=False, stop=True), inc="ts")



def emit_s4(ctx, x_in, xo, wrT, wkT, wvT, mur, muk, onc, onr):
    nc, rec = ctx.nc, ctx.rec
    TX = T + 1
    X = ctx.sb("X", [128, NCH * TX], F32)
    XH = ctx.sb("XH", [128, NCH * (WIN + 1)], BF16)
    XB = ctx.sb("XB", [128, NCH * WIN], BF16)
    XSQ = ctx.sb("XSQ", [128, WIN], BF16)
    XR = ctx.sb("XR", [128, NCH * WIN], BF16)
    XK = ctx.sb("XK", [128, NCH * WIN], BF16)
    K2 = ctx.sb("K2", [128, NCH * WIN], BF16)
    R = ctx.sb("R", [128, NCH * WIN], BF16)
    TMP = ctx.sb("TMP", [128, NCH * WIN], BF16)
    TMPF = ctx.sb("TMPF", [128, 2 * WIN], F32)
    WR = ctx.sb("WR", [128, NCH * D], BF16)
    WK = ctx.sb("WK", [128, NCH * D], BF16)
    WV = ctx.sb("WV", [128, NCH * D], BF16)
    MU = ctx.sb("MU", [128, 4 * NCH], F32)
    ONC = ctx.sb("ONC", [128, 1], BF16)
    ONR = ctx.sb("ONR", [1, 128], BF16)
    SROW = ctx.sb("SROW", [1, 4 * WIN + 16], F32)
    SRB = ctx.sb("SRB", [1, 4 * WIN + 16], BF16)
    XAF = ctx.sb("XAF", [128, WIN], F32)
    PS = ctx.ps("PS", [128, 4096], F32)
    bank = lambda b: PS[:, b * 512:b * 512 + WIN]

    m_x = rec.dma("sync", X[:], x_in.rearrange("(c p) t -> p c t", p=128), sem="ld_x")
    m_wr = rec.dma("gpsimd", WR[:], wrT.rearrange("(k p) m -> p k m", p=128), sem="ld_wr")
    m_wk = rec.dma("gpsimd", WK[:], wkT.rearrange("(k p) m -> p k m", p=128), sem="ld_wk")
    m_wv = rec.dma("gpsimd", WV[:], wvT.rearrange("(k p) m -> p k m", p=128), sem="ld_wv")
    rec.dma("sync", MU[:, 0:NCH], mur, sem="ld_mu")
    m_mu = rec.dma("sync", MU[:, NCH:2 * NCH], muk, sem="ld_mu")
    m_onc = rec.dma("gpsimd", ONC[:], onc, sem="ld_onc")
    m_onr = rec.dma("gpsimd", ONR[:], onr, sem="ld_onr")
    m_mu1m = rec.emit("vector", lambda e: e.tensor_scalar(
        out=MU[:, 2 * NCH:4 * NCH], in0=MU[:, 0:2 * NCH],
        scalar1=-1.0, scalar2=1.0, op0=ALU.mult, op1=ALU.add),
        waits=after(m_mu), inc="dv")

    Xs = lambda c, a, b: X[:, c * TX + a:c * TX + b]
    XHs = lambda c, a, b: XH[:, c * (WIN + 1) + a:c * (WIN + 1) + b]

    m_resid = [None] * NCH
    m_xh_last = None
    m_store = None
    for w in range(NW):
        c0 = 1 + w * WIN
        carry = []
        if w > 0:
            for c in range(NCH):
                mcp = rec.emit("vector", lambda e, c=c: e.tensor_copy(
                    out=XHs(c, 0, 1), in_=XHs(c, WIN, WIN + 1)),
                    waits=after(m_xh_last), inc="dv")
                carry.append(mcp)
        if w == 0:
            cw = lambda c: [m_x, m_resid[c]]
        else:
            cw = lambda c, carry=carry: [carry[c], m_resid[c]]
        m_xh_last = emit_ln(ctx, lambda c, a, b, c0=c0: Xs(c, c0 + a, c0 + b),
                            lambda c, a, b: XHs(c, 1 + a, 1 + b),
                            WIN, cw, m_onc, m_onr)
        if w == 0:
            m_xh_last = emit_ln(ctx, lambda c, a, b: Xs(c, a, b),
                                lambda c, a, b: XHs(c, a, b),
                                1, lambda c, m=m_xh_last: [m], m_onc, m_onr)
        m_mix_last = None
        for c in range(NCH):
            for (dst, mu_i) in ((XR, 0), (XK, 1)):
                rec.emit("vector", lambda e, c=c, mu_i=mu_i: e.tensor_scalar(
                    out=TMP[:, c * WIN:(c + 1) * WIN], in0=XHs(c, 0, WIN),
                    scalar1=MU[:, (2 + mu_i) * NCH + c:(2 + mu_i) * NCH + c + 1],
                    scalar2=None, op0=ALU.mult),
                    waits=after(m_xh_last, m_mu1m) if c == 0 and mu_i == 0 else (),
                    inc="dv")
                m_mix_last = rec.emit("vector", lambda e, c=c, mu_i=mu_i, dst=dst: e.scalar_tensor_tensor(
                    out=dst[:, c * WIN:(c + 1) * WIN], in0=XHs(c, 1, WIN + 1),
                    scalar=MU[:, mu_i * NCH + c:mu_i * NCH + c + 1],
                    in1=TMP[:, c * WIN:(c + 1) * WIN], op0=ALU.mult, op1=ALU.add),
                    inc="dv")
        m_evac = {}
        for m in range(NCH):
            bk = m % 2
            mm = None
            for k in range(NCH):
                mm = rec.emit("tensor", lambda e, m=m, k=k, bk=bk: e.matmul(
                    bank(bk), WR[:, k * D + m * 128:k * D + (m + 1) * 128],
                    XR[:, k * WIN:(k + 1) * WIN], start=(k == 0), stop=(k == NCH - 1)),
                    waits=after(m_mix_last if m == 0 and k == 0 else None,
                                m_wr if m == 0 and k == 0 else None,
                                m_evac.get(("r", m - 2)) if k == 0 else None),
                    inc="ts" if k == NCH - 1 else None)
            m_evac[("r", m)] = rec.emit("scalar", lambda e, m=m, bk=bk: e.activation(
                out=R[:, m * WIN:(m + 1) * WIN], in_=bank(bk), func=AF.Sigmoid),
                waits=after(mm), inc="ac")
        m_sq_last = None
        for m in range(NCH):
            bk = 2 + m % 2
            mm = None
            for k in range(NCH):
                mm = rec.emit("tensor", lambda e, m=m, k=k, bk=bk: e.matmul(
                    bank(bk), WK[:, k * D + m * 128:k * D + (m + 1) * 128],
                    XK[:, k * WIN:(k + 1) * WIN], start=(k == 0), stop=(k == NCH - 1)),
                    waits=after(m_wk if m == 0 and k == 0 else None,
                                m_evac.get(("k", m - 2)) if k == 0 else None),
                    inc="ts" if k == NCH - 1 else None)
            mr = rec.emit("vector", lambda e, m=m, bk=bk: e.tensor_scalar_max(
                out=TMP[:, m * WIN:(m + 1) * WIN], in0=bank(bk), scalar1=0.0),
                waits=after(mm), inc="dv")
            m_evac[("k", m)] = mr
            m_sq_last = rec.emit("scalar", lambda e, m=m: e.activation(
                out=K2[:, m * WIN:(m + 1) * WIN], in_=TMP[:, m * WIN:(m + 1) * WIN],
                func=AF.Square), waits=after(mr), inc="ac")
        for m in range(NCH):
            bk = 4 + m % 2
            mm = None
            for k in range(NCH):
                mm = rec.emit("tensor", lambda e, m=m, k=k, bk=bk: e.matmul(
                    bank(bk), WV[:, k * D + m * 128:k * D + (m + 1) * 128],
                    K2[:, k * WIN:(k + 1) * WIN], start=(k == 0), stop=(k == NCH - 1)),
                    waits=after(m_sq_last if m == 0 and k == 0 else None,
                                m_wv if m == 0 and k == 0 else None,
                                m_evac.get(("v", m - 2)) if k == 0 else None),
                    inc="ts" if k == NCH - 1 else None)
            mt = rec.emit("vector", lambda e, m=m, bk=bk: e.tensor_tensor(
                out=TMPF[:, (m % 2) * WIN:(m % 2) * WIN + WIN], in0=bank(bk),
                in1=R[:, m * WIN:(m + 1) * WIN], op=ALU.mult),
                waits=after(mm), inc="dv")
            m_evac[("v", m)] = mt
            m_resid[m] = rec.emit("vector", lambda e, m=m, c0=c0: e.scalar_tensor_tensor(
                out=Xs(m, c0, c0 + WIN), in0=TMPF[:, (m % 2) * WIN:(m % 2) * WIN + WIN],
                scalar=0.5, in1=Xs(m, c0, c0 + WIN), op0=ALU.mult, op1=ALU.add),
                inc="dv")
        if xo is not None and w == NW - 1:
            xo_r = xo.rearrange("(c p) t -> p c t", p=128)
            for c in range(NCH):
                m_store = rec.dma("sync", xo_r[:, c, 0:T], Xs(c, 1, 1 + T),
                                  waits=after(*m_resid), sem="st")
    return [m_store]


def build_s4(nc):
    rec = Rec(nc)
    ctx = Ctx(nc, rec)
    TX = T + 1
    x_in = nc.dram_tensor("x", [D, TX], F32, kind="ExternalInput").ap()
    wrT = nc.dram_tensor("wrT", [D, D], F32, kind="ExternalInput").ap()
    wkT = nc.dram_tensor("wkT", [D, D], F32, kind="ExternalInput").ap()
    wvT = nc.dram_tensor("wvT", [D, D], F32, kind="ExternalInput").ap()
    mur = nc.dram_tensor("mur", [128, NCH], F32, kind="ExternalInput").ap()
    muk = nc.dram_tensor("muk", [128, NCH], F32, kind="ExternalInput").ap()
    onc = nc.dram_tensor("ones_col", [128, 1], F32, kind="ExternalInput").ap()
    onr = nc.dram_tensor("ones_row", [1, 128], F32, kind="ExternalInput").ap()
    xo = nc.dram_tensor("xo", [D, T], F32, kind="ExternalOutput").ap()
    mil = emit_s4(ctx, x_in, xo, wrT, wkT, wvT, mur, muk, onc, onr)
    rec.finalize(mil)
    return nc


NHEAD = 16
NTB = T // 128
F2 = 128


def emit_s2(ctx, x_in, xo, wq, wk, wv, wo, kqp, kkp, sel, onc, onr, tri, ident, stage=100):
    nc, rec = ctx.nc, ctx.rec

    cc_in = nc.dram_tensor("cc_in", [128, NHEAD * 65], F32)
    cc_out = nc.dram_tensor("cc_out", [NCORES * 128, NHEAD * 65], F32)
    o_dram = nc.dram_tensor("o_dram", [D, T], BF16, kind="ExternalOutput")

    XWIN = ctx.sb("XWIN", [128, NCH * WIN], F32)
    XH = ctx.sb("XH2", [128, NCH * T], BF16)
    V = ctx.sb("V", [128, NTB * D], BF16)
    WQ = ctx.sb("WQ", [128, NCH * D], BF16)
    WK = ctx.sb("WK2", [128, NCH * D], BF16)
    WX = ctx.sb("WX", [128, NCH * D], BF16)
    KQT = ctx.sb("KQT", [128, NHEAD * F2], BF16)
    KKT = ctx.sb("KKT", [128, NHEAD * F2], BF16)
    SEL = ctx.sb("SEL", [128, NCORES], F32)
    QP = ctx.sb("QP", [128, T], BF16)
    KP = ctx.sb("KP", [128, T], BF16)
    QF = ctx.sb("QF", [128, T], BF16)
    KFF = ctx.sb("KFF", [128, T], BF16)
    KFTP = ctx.sb("KFTP", [128, NTB * 2 * F2], BF16)
    VAP = ctx.sb("VAP", [128, NTB * 2 * 65], BF16)
    ZK = ctx.sb("ZK", [128, 4 * NTB], F32)
    OST = ctx.sb("OST", [128, NTB * 128], F32)
    OTB = ctx.sb("OTB", [128, NTB * 128], BF16)
    IDTF = ctx.sb("IDTF", [128, 128], F32)
    PT = ctx.sb("PT", [128, F2], BF16)
    STSB = ctx.sb("STSB", [128, 65], BF16)
    ZI = ctx.sb("ZI", [128, 4], F32)
    AGS = ctx.sb("AGS", [128, NHEAD * 65], F32)
    PRE = AGS
    PREB = ctx.sb("PREB", [128, NHEAD * 65], BF16)
    TRI = ctx.sb("TRI", [128, 128], BF16)
    IDT = ctx.sb("IDT", [128, 128], BF16)
    XB = ctx.sb("XB", [128, NCH * WIN], BF16)
    XSQ = ctx.sb("XSQ", [128, WIN], BF16)
    XAF = ctx.sb("XAF", [128, WIN], F32)
    SROW = ctx.sb("SROW", [1, 4 * WIN + 16], F32)
    SRB = ctx.sb("SRB", [1, 4 * WIN + 16], BF16)
    ONC = ctx.sb("ONC", [128, 1], BF16)
    ONR = ctx.sb("ONR", [1, 128], BF16)
    PS = ctx.ps("PS", [128, 4096], F32)
    bank = lambda b, wdt=512: PS[:, b * 512:b * 512 + wdt]
    OFP = XB
    PRT = XWIN

    m_wq = rec.dma("gpsimd", WQ[:], wq.rearrange("(k p) m -> p k m", p=128), sem="ld_wq")
    m_wk = rec.dma("gpsimd", WK[:], wk.rearrange("(k p) m -> p k m", p=128), sem="ld_wk")
    m_wv = rec.dma("gpsimd", WX[:], wv.rearrange("(k p) m -> p k m", p=128), sem="ld_wv")
    m_kq = rec.dma("gpsimd", KQT[:], kqp.rearrange("(h p) f -> p h f", p=128), sem="ld_kq")
    m_kk = rec.dma("gpsimd", KKT[:], kkp.rearrange("(h p) f -> p h f", p=128), sem="ld_kk")
    m_sel = rec.dma("sync", SEL[:], sel, sem="ld_sel")
    m_onc = rec.dma("gpsimd", ONC[:], onc, sem="ld_onc")
    m_onr = rec.dma("gpsimd", ONR[:], onr, sem="ld_onr")
    m_tri = rec.dma("gpsimd", TRI[:], tri, sem="ld_tri")
    m_idt = rec.dma("gpsimd", IDT[:], ident, sem="ld_idt")
    m_idtf = rec.dma("sync", IDTF[:], ident, sem="ld_idtf")
    m_neg = None
    for h in range(NHEAD):
        for KT in (KQT, KKT):
            m_neg = rec.emit("vector", lambda e, h=h, KT=KT: e.tensor_scalar(
                out=KT[:, h * F2 + 64:h * F2 + 128], in0=KT[:, h * F2:h * F2 + 64],
                scalar1=-1.0, scalar2=None, op0=ALU.mult),
                waits=after(m_kq, m_kk) if h == 0 else (), inc="dv")

    m_ln = None
    for w in range(NW):
        m_xw = rec.dma("sync", XWIN[:],
                       x_in.rearrange("(c p) t -> p c t", p=128)[:, :, w * WIN:(w + 1) * WIN],
                       waits=after(m_ln), sem="ld_xw")
        m_ln = emit_ln(ctx, lambda c, a, b: XWIN[:, c * WIN + a:c * WIN + b],
                       lambda c, a, b, w=w: XH[:, c * T + w * WIN + a:c * T + w * WIN + b],
                       WIN, lambda c, m_xw=m_xw: [m_xw], m_onc, m_onr)
    if stage <= 1:
        ms = rec.dma("gpsimd", xo.rearrange("(c p) t -> p c t", p=128), XH[:],
                     waits=after(m_ln), sem="st")
        return [ms]

    m_prev_evac = [None, None]
    m_v = None
    for tb in range(NTB):
        mm = [None, None]
        for k in range(NCH):
            for half in range(2):
                mm[half] = rec.emit("tensor", lambda e, tb=tb, k=k, half=half: e.matmul(
                    bank(half), XH[:, k * T + tb * 128:k * T + (tb + 1) * 128],
                    WX[:, k * D + half * 512:k * D + (half + 1) * 512],
                    start=(k == 0), stop=(k == NCH - 1)),
                    waits=after(m_ln if tb == 0 and k == 0 and half == 0 else None,
                                m_wv if tb == 0 and k == 0 and half == 0 else None,
                                m_prev_evac[half] if k == 0 else None),
                    inc="ts" if k == NCH - 1 else None)
        for half in range(2):
            m_v = rec.emit("vector", lambda e, tb=tb, half=half: e.tensor_copy(
                out=V[:, tb * D + half * 512:tb * D + (half + 1) * 512], in_=bank(half)),
                waits=after(mm[half]), inc="dv")
            m_prev_evac[half] = m_v
    m_v_done = m_v
    if stage <= 2:
        ms = rec.dma("gpsimd", xo.rearrange("(c p) t -> p c t", p=128)[:, 0:8, :],
                     V[:, 0:8 * 2048], waits=after(m_v_done), sem="st")
        return [ms]

    def proj_pair(p, Wsb, m_w, OUT, m_prev_use):
        m_ev = None
        for w in range(NW):
            mm = None
            for k in range(NCH):
                mm = rec.emit("tensor", lambda e, p=p, w=w, k=k, Wsb=Wsb: e.matmul(
                    bank(2), Wsb[:, k * D + p * 128:k * D + (p + 1) * 128],
                    XH[:, k * T + w * WIN:k * T + (w + 1) * WIN],
                    start=(k == 0), stop=(k == NCH - 1)),
                    waits=after(m_w if w == 0 and k == 0 else None,
                                m_prev_use if w == 0 and k == 0 else None,
                                m_ev if k == 0 else None),
                    inc="ts" if k == NCH - 1 else None)
            m_ev = rec.emit("vector", lambda e, w=w, OUT=OUT: e.tensor_copy(
                out=OUT[:, w * WIN:(w + 1) * WIN], in_=bank(2)),
                waits=after(mm), inc="dv")
        return m_ev

    def pair_features(p, m_kp, m_prev_done):
        m_e = None
        m_prev_exp = m_prev_done
        for tb in range(NTB):
            mm = None
            for half, hd in ((0, p * 2), (1, p * 2 + 1)):
                mm = rec.emit("tensor", lambda e, tb=tb, half=half, hd=hd: e.matmul(
                    PS[:, 3 * 512 + half * 128:3 * 512 + half * 128 + 128],
                    KP[:, tb * 128:(tb + 1) * 128], KKT[:, hd * F2:(hd + 1) * F2],
                    start=True, stop=True),
                    waits=after(m_kp if tb == 0 and half == 0 else None,
                                m_neg if tb == 0 and half == 0 else None,
                                m_prev_exp if half == 0 else None),
                    inc="ts" if half == 1 else None)
            m_exp = rec.emit("scalar", lambda e, tb=tb: e.activation(
                out=KFTP[:, tb * 2 * F2:(tb + 1) * 2 * F2], in_=PS[:, 3 * 512:3 * 512 + 256],
                func=AF.Exp), waits=after(mm), inc="ac")
            m_prev_exp = m_exp
            m_zk = rec.emit("vector", lambda e, tb=tb: e.reduce_sum(
                out=ZK[:, 2 * tb:2 * tb + 2],
                in_=KFTP.ap().rearrange("p (tb s f) -> p tb s f", tb=NTB, s=2)[:, tb],
                axis=AX.X), waits=after(m_exp), inc="dv")
            m_zki = rec.emit("vector", lambda e, tb=tb: e.reciprocal(
                out=ZK[:, 2 * NTB + 2 * tb:2 * NTB + 2 * tb + 2], in_=ZK[:, 2 * tb:2 * tb + 2]),
                waits=after(m_zk), inc="dv")
            for half, hd in ((0, p * 2), (1, p * 2 + 1)):
                m1 = rec.emit("vector", lambda e, tb=tb, half=half, hd=hd: e.tensor_scalar(
                    out=VAP[:, tb * 130 + half * 65:tb * 130 + half * 65 + 64],
                    in0=V[:, tb * D + hd * 64:tb * D + (hd + 1) * 64],
                    scalar1=ZK[:, 2 * NTB + 2 * tb + half:2 * NTB + 2 * tb + half + 1],
                    scalar2=None, op0=ALU.mult),
                    waits=after(m_zki, m_v_done if tb == 0 and half == 0 else None), inc="dv")
                m_e = rec.emit("vector", lambda e, tb=tb, half=half: e.tensor_copy(
                    out=VAP[:, tb * 130 + half * 65 + 64:tb * 130 + half * 65 + 65],
                    in_=ZK[:, 2 * NTB + 2 * tb + half:2 * NTB + 2 * tb + half + 1]),
                    waits=after(m1), inc="dv")
        return m_e

    def head_fm_feats(h, SRC, msrc, KT, OUT, bk, m_prev):
        m_q = m_prev
        for w in range(NW):
            mq = rec.emit("tensor", lambda e, h=h, w=w, bk=bk, KT=KT, SRC=SRC: e.matmul(
                bank(bk, 512), KT[:, h * F2:(h + 1) * F2], SRC[:, w * WIN:(w + 1) * WIN],
                start=True, stop=True),
                waits=after(msrc if w == 0 else None, m_neg if w == 0 else None, m_q),
                inc="ts")
            m_q = rec.emit("scalar", lambda e, w=w, bk=bk, OUT=OUT: e.activation(
                out=OUT[:, w * WIN:(w + 1) * WIN], in_=bank(bk, 512), func=AF.Exp),
                waits=after(mq), inc="ac")
        return m_q

    m_pair_use = m_v_done
    m_tot_evac = None
    for p in range(NCH):
        m_kp = proj_pair(p, WK, m_wk, KP, m_pair_use)
        m_va = pair_features(p, m_kp, m_pair_use)
        for hh in range(2):
            h = 2 * p + hh
            m_upd = None
            for tb in range(NTB):
                m_upd = rec.emit("tensor", lambda e, tb=tb, hh=hh: e.matmul(
                    PS[:, 0:65], KFTP[:, tb * 2 * F2 + hh * F2:tb * 2 * F2 + (hh + 1) * F2],
                    VAP[:, tb * 130 + hh * 65:tb * 130 + (hh + 1) * 65],
                    start=(tb == 0), stop=(tb == NTB - 1)),
                    waits=after(m_va if tb == 0 else None,
                                m_tot_evac if tb == 0 else None),
                    inc="ts" if tb == NTB - 1 else None)
            m_tot_evac = rec.emit("vector", lambda e, h=h: e.tensor_copy(
                out=AGS[:, h * 65:(h + 1) * 65], in_=PS[:, 0:65]),
                waits=after(m_upd), inc="dv")
        m_pair_use = m_tot_evac
    m_agin = rec.dma("gpsimd", cc_in.ap(), AGS[:], waits=after(m_tot_evac), sem="cc_in")
    m_ag = rec.emit("gpsimd", lambda e: e.collective_compute(
        "AllGather", ALU.bypass, replica_groups=[list(range(NCORES))],
        ins=[cc_in.ap()], outs=[cc_out.ap()]),
        waits=after(m_agin), inc="cc")
    m_pre = rec.emit("vector", lambda e: e.memset(PRE[:], 0.0), waits=after(m_agin), inc="dv")
    m_pre_hist = [None, None]
    for r in range(NCORES):
        m_prt = rec.dma("sync", PRT[:, (r % 2) * NHEAD * 65:(r % 2 + 1) * NHEAD * 65],
                        cc_out.ap()[r * 128:(r + 1) * 128, :],
                        waits=after(m_ag, m_pre_hist[r % 2]), sem="ld_prt%d" % (r % 2))
        m_pre = rec.emit("vector", lambda e, r=r: e.scalar_tensor_tensor(
            out=PRE[:], in0=PRT[:, (r % 2) * NHEAD * 65:(r % 2 + 1) * NHEAD * 65],
            scalar=SEL[:, r:r + 1], in1=PRE[:], op0=ALU.mult, op1=ALU.add),
            waits=after(m_prt, m_pre, m_sel), inc="dv")
        m_pre_hist[r % 2] = m_pre
    m_preb = rec.emit("vector", lambda e: e.tensor_copy(out=PREB[:], in_=PRE[:]),
                      waits=after(m_pre), inc="dv")
    if stage <= 3:
        ms = rec.dma("sync", xo.rearrange("(c p) t -> p c t", p=128)[:, 0, 0:NHEAD * 65],
                     PRE[:], waits=after(m_pre), sem="st")
        return [ms]

    m_ostore = []
    m_store_prev = None
    for p in range(NCH):
        m_kp = proj_pair(p, WK, m_wk, KP, m_pair_use)
        m_qp = proj_pair(p, WQ, m_wq, QP, m_kp)
        m_va = pair_features(p, m_kp, m_pair_use)
        if stage == 33:
            m_ostore.append(m_va)
            m_pair_use = m_va
            continue
        for hh in range(2):
            h = 2 * p + hh
            m_qf = head_fm_feats(h, QP, m_qp, KQT, QF, 5,
                                 m_ostore[-1] if m_ostore else None)
            m_kff = head_fm_feats(h, KP, m_kp, KKT, KFF, 6,
                                  m_ostore[-1] if m_ostore else None)
            if stage == 34:
                m_ostore.append(m_kff)
                continue
            m_init = rec.emit("tensor", lambda e, h=h: e.matmul(
                PS[:, 0:65], IDT[:], PREB[:, h * 65:(h + 1) * 65],
                start=True, stop=False),
                waits=after(m_preb, m_idt, m_va, m_qf, m_kff,
                            m_ostore[-1] if m_ostore else None), inc="ts")
            m_state = m_init
            m_sev = None
            m_pt = None
            m_ost = None
            for tb in range(NTB):
                if stage >= 37:
                    m_sev = rec.emit("scalar", lambda e: e.activation(
                        out=STSB[:], in_=PS[:, 0:65], func=AF.Copy),
                        waits=after(m_state, m_ost), inc="ac")
                m_smm = rec.emit("tensor", lambda e, tb=tb: e.matmul(
                    PS[:, 2 * 512:2 * 512 + 128], KFF[:, tb * 128:(tb + 1) * 128],
                    QF[:, tb * 128:(tb + 1) * 128], start=True, stop=True),
                    waits=after(m_pt), inc="ts")
                m_pt = rec.emit("vector", lambda e: e.tensor_tensor(
                    out=PT[:], in0=PS[:, 2 * 512:2 * 512 + 128], in1=TRI[:], op=ALU.mult),
                    waits=after(m_smm, m_tri), inc="dv")
                if stage >= 38:
                    m_o1 = rec.emit("tensor", lambda e, tb=tb, hh=hh: e.matmul(
                        PS[:, 1 * 512:1 * 512 + 65], PT[:],
                        VAP[:, tb * 130 + hh * 65:tb * 130 + hh * 65 + 65],
                        start=True, stop=False),
                        waits=after(m_pt, m_ost), inc=None)
                    m_o2 = rec.emit("tensor", lambda e, tb=tb: e.matmul(
                        PS[:, 1 * 512:1 * 512 + 65], QF[:, tb * 128:(tb + 1) * 128], STSB[:],
                        start=False, stop=True),
                        waits=after(m_sev), inc="ts")
                m_state = rec.emit("tensor", lambda e, tb=tb, hh=hh: e.matmul(
                    PS[:, 0:65], KFTP[:, tb * 2 * F2 + hh * F2:tb * 2 * F2 + (hh + 1) * F2],
                    VAP[:, tb * 130 + hh * 65:tb * 130 + hh * 65 + 65],
                    start=False, stop=(tb == NTB - 1)),
                    waits=after(m_sev), inc="ts")
                if stage >= 38:
                    m_z1 = rec.emit("vector", lambda e: e.tensor_scalar(
                        out=ZI[:, 0:1], in0=PS[:, 1 * 512 + 64:1 * 512 + 65], scalar1=1e-30,
                        scalar2=None, op0=ALU.add), waits=after(m_o2), inc="dv")
                    m_z2 = rec.emit("vector", lambda e: e.reciprocal(
                        out=ZI[:, 1:2], in_=ZI[:, 0:1]), waits=after(m_z1), inc="dv")
                    m_ost = rec.emit("vector", lambda e, tb=tb, hh=hh: e.tensor_scalar(
                        out=OST[:, tb * 128 + hh * 64:tb * 128 + hh * 64 + 64],
                        in0=PS[:, 1 * 512:1 * 512 + 64],
                        scalar1=ZI[:, 1:2], scalar2=None, op0=ALU.mult),
                        waits=after(m_z2), inc="dv")
            m_ostore.append(m_ost if m_ost is not None else m_state)
        if stage not in (35, 36, 37):
            m_tc = None
            for tb in range(NTB):
                mtr = rec.emit("tensor", lambda e, tb=tb: e.transpose(
                    PS[:, 2 * 512:2 * 512 + 128], OST[:, tb * 128:(tb + 1) * 128],
                    IDTF[:]),
                    waits=after(m_ostore[-1] if tb == 0 else None,
                                m_idtf if tb == 0 else None, m_tc), inc="ts")
                m_tc = rec.emit("vector", lambda e, tb=tb: e.tensor_copy(
                    out=OTB[:, tb * 128:(tb + 1) * 128], in_=PS[:, 2 * 512:2 * 512 + 128]),
                    waits=after(mtr, m_store_prev if tb == 0 else None), inc="dv")
            if stage == 39:
                m_ostore.append(m_tc)
            else:
                m_store_prev = rec.dma("gpsimd", o_dram.ap()[p * 128:(p + 1) * 128, :], OTB[:],
                                       waits=after(m_tc), sem="st_o%d" % p)
                m_ostore.append(m_tc)
        m_pair_use = m_ostore[-1]
    stws = [("st_o%d" % p, 16) for p in range(NCH) if ("st_o%d" % p) in rec.counts]
    if stage <= 4 or stage in (33, 34, 35, 36, 37, 39):
        ms = rec.dma("gpsimd", xo.rearrange("(c p) t -> p c t", p=128)[:, 0:1, 0:NTB * 128],
                     OST[:, 0:NTB * 128], waits=after(m_ostore[-1], *stws), sem="st")
        return [ms]

    if stage == 48:
        ms = rec.dma("gpsimd", xo, o_dram.ap(), waits=after(m_ostore[-1], *stws), sem="st")
        return [ms]
    m_wo = rec.dma("gpsimd", WX[:], wo.rearrange("(k p) m -> p k m", p=128),
                   waits=after(m_v_done, m_pair_use), sem="ld_wo")
    m_store = None
    m_resid_prev = [None] * NCH
    for w in range(NW if stage != 52 else 1):
        m_xw2 = rec.dma("sync", XWIN[:],
                        x_in.rearrange("(c p) t -> p c t", p=128)[:, :, w * WIN:(w + 1) * WIN],
                        waits=after(m_ln if w == 0 else None,
                                    m_pre if w == 0 else None,
                                    ("st%d" % ((w - 1) % 2), rec.counts.get("st%d" % ((w - 1) % 2), 0)) if w > 0 else None,
                                    *(m_resid_prev if w > 0 else [])),
                        sem="ld_xw2")
        m_pan = []
        for k in range(NCH):
            mp = rec.dma("sync", OFP[:, k * WIN:(k + 1) * WIN],
                         o_dram.ap()[k * 128:(k + 1) * 128, w * WIN:(w + 1) * WIN],
                         waits=after(*(stws if w == 0 and k == 0 else []),
                                     m_ln if w == 0 and k == 0 else None,
                                     *(m_resid_prev if w > 0 and k == 0 else [])),
                         sem="ld_ofp%d" % (w % 2))
            m_pan.append(mp)
        if stage == 49 and w == 0:
            ms1 = rec.dma("gpsimd", xo.rearrange("(c p) t -> p c t", p=128)[:, :, 0:WIN],
                          OFP[:], waits=after(m_pan[-1]), sem="st0")
            ms2 = rec.dma("sync", xo.rearrange("(c p) t -> p c t", p=128)[:, :, WIN:2 * WIN],
                          XWIN[:], waits=after(m_xw2), sem="st0")
            return [ms1, ms2]
        for m in range(NCH):
            bk = m % 2
            mm = None
            for k in range(NCH):
                mm = rec.emit("tensor", lambda e, m=m, k=k, bk=bk: e.matmul(
                    bank(bk), WX[:, k * D + m * 128:k * D + (m + 1) * 128],
                    OFP[:, k * WIN:(k + 1) * WIN], start=(k == 0), stop=(k == NCH - 1)),
                    waits=after(m_pan[-1] if m == 0 and k == 0 else None,
                                m_wo if m == 0 and k == 0 else None,
                                m_resid_prev[m - 2] if k == 0 and m >= 2 else None),
                    inc="ts" if k == NCH - 1 else None)
            if stage == 50:
                m_res = rec.emit("vector", lambda e, m=m, bk=bk: e.tensor_copy(
                    out=XWIN[:, m * WIN:(m + 1) * WIN], in_=bank(bk)),
                    waits=after(mm, m_xw2), inc="dv")
            else:
                m_res = rec.emit("vector", lambda e, m=m, bk=bk: e.scalar_tensor_tensor(
                    out=XWIN[:, m * WIN:(m + 1) * WIN], in0=bank(bk), scalar=1.0,
                    in1=XWIN[:, m * WIN:(m + 1) * WIN], op0=ALU.mult, op1=ALU.add),
                    waits=after(mm, m_xw2), inc="dv")
            m_resid_prev[m] = m_res
            m_store = rec.dma("sync", xo.rearrange("(c p) t -> p c t", p=128)[:, m, w * WIN:(w + 1) * WIN],
                              XWIN[:, m * WIN:(m + 1) * WIN], waits=after(m_res), sem="st%d" % (w % 2))
    return [(s, rec.counts[s]) for s in ("st0", "st1") if s in rec.counts]


def build_s2(nc, stage=100):
    rec = Rec(nc)
    ctx = Ctx(nc, rec)
    x_in = nc.dram_tensor("x", [D, T], F32, kind="ExternalInput").ap()
    wq = nc.dram_tensor("wq", [D, D], F32, kind="ExternalInput").ap()
    wk = nc.dram_tensor("wk", [D, D], F32, kind="ExternalInput").ap()
    wv = nc.dram_tensor("wv", [D, D], F32, kind="ExternalInput").ap()
    wo = nc.dram_tensor("wo", [D, D], F32, kind="ExternalInput").ap()
    kqp = nc.dram_tensor("kqp", [NHEAD * 128, 128], F32, kind="ExternalInput").ap()
    kkp = nc.dram_tensor("kkp", [NHEAD * 128, 128], F32, kind="ExternalInput").ap()
    sel = nc.dram_tensor("sel", [128, NCORES], F32, kind="ExternalInput").ap()
    onc = nc.dram_tensor("ones_col", [128, 1], F32, kind="ExternalInput").ap()
    onr = nc.dram_tensor("ones_row", [1, 128], F32, kind="ExternalInput").ap()
    tri = nc.dram_tensor("tri", [128, 128], F32, kind="ExternalInput").ap()
    ident = nc.dram_tensor("ident", [128, 128], F32, kind="ExternalInput").ap()
    xo = nc.dram_tensor("xo", [D, T], F32, kind="ExternalOutput").ap()
    mil = emit_s2(ctx, x_in, xo, wq, wk, wv, wo, kqp, kkp, sel, onc, onr, tri, ident, stage=stage)
    rec.finalize(mil)
    return nc


def s2_host_inputs(Kq, Kk):
    kqp = np.zeros((NHEAD * 128, 128), np.float32)
    kkp = np.zeros((NHEAD * 128, 128), np.float32)
    for h in range(NHEAD):
        r0 = h * 128 + (h % 2) * 64
        kqp[r0:r0 + 64, 0:64] = Kq[h]
        kkp[r0:r0 + 64, 0:64] = Kk[h]
    tri = np.triu(np.ones((128, 128), np.float32))
    return kqp, kkp, tri


def emit_s5(ctx, x_in, xo, pewT, qkvwT, owT, taps, onc, onr, sel2, oacc, ozacc):
    nc, rec = ctx.nc, ctx.rec
    TXX = T + 2
    NG = 8
    cc_k = nc.dram_tensor("cc_k", [D, T], BF16)
    cc_v = nc.dram_tensor("cc_v", [T, D], BF16)
    ko = nc.dram_tensor("ko", [4 * D, T], BF16)
    vo = nc.dram_tensor("vo", [4 * T, D], BF16)

    XWIN = ctx.sb("XWIN", [128, NCH * WIN], F32)
    BIG = ctx.sb("BIG", [128, 33000], BF16)
    XHo = 0
    XS2o = 16400
    KBo = 0
    VBo = 8192
    PTo = 16400
    OFMo = 0
    OWo = 16400
    XSW = ctx.sb("XSW", [128, NCH * WIN], BF16)
    Q = ctx.sb("Q", [128, NCH * T], BF16)
    PEW = ctx.sb("PEW", [128, NCH * D], BF16)
    WSL = ctx.sb("WSL", [128, 2 * NCH * 128], BF16)
    KEV = ctx.sb("KEV", [128, 2 * WIN], BF16)
    VEV = ctx.sb("VEV", [128, 2 * WIN], BF16)
    OLOAD = ctx.sb("OLOAD", [128, T], F32)
    ZROW = ctx.sb("ZROW", [2, 2 * T], F32)
    ZRB = ctx.sb("ZRB", [2, 2 * T], BF16)
    OEV = ctx.sb("OEV", [128, WIN], F32)
    ZEV = ctx.sb("ZEV", [1, WIN], F32)
    TAPS = ctx.sb("TAPS", [128, 3 * NCH], F32)
    SEL2 = ctx.sb("SEL2", [2, 128], BF16)
    CTMP = ctx.sb("CTMP", [128, WIN], F32)
    XB = ctx.sb("XB", [128, NCH * WIN], BF16)
    XSQ = ctx.sb("XSQ", [128, WIN], BF16)
    XAF = ctx.sb("XAF", [128, WIN], F32)
    SROW = ctx.sb("SROW", [1, 4 * WIN + 16], F32)
    SRB = ctx.sb("SRB", [1, 4 * WIN + 16], BF16)
    ONC = ctx.sb("ONC", [128, 1], BF16)
    ONR = ctx.sb("ONR", [1, 128], BF16)
    PS = ctx.ps("PS", [128, 4096], F32)
    bank = lambda b, wdt=512: PS[:, b * 512:b * 512 + wdt]

    m_pew = rec.dma("gpsimd", PEW[:], pewT.rearrange("(k p) m -> p k m", p=128), sem="ld_pe")
    m_taps = rec.dma("sync", TAPS[:], taps, sem="ld_taps")
    m_onc = rec.dma("gpsimd", ONC[:], onc, sem="ld_onc")
    m_onr = rec.dma("gpsimd", ONR[:], onr, sem="ld_onr")
    m_sel2 = rec.dma("gpsimd", SEL2[:], sel2, sem="ld_sel2")

    m_ln = None
    xr_ = x_in.rearrange("(c p) t -> p c t", p=128)
    for w in range(NW):
        m_xw = rec.dma("sync", XWIN[:], xr_[:, :, 1 + w * WIN:1 + (w + 1) * WIN],
                       waits=after(m_ln), sem="ld_xw")
        m_ln = emit_ln(ctx, lambda c, a, b: XWIN[:, c * WIN + a:c * WIN + b],
                       lambda c, a, b, w=w: BIG[:, c * TXX + 1 + w * WIN + a:c * TXX + 1 + w * WIN + b],
                       WIN, lambda c, m_xw=m_xw: [m_xw], m_onc, m_onr)
    m_xb = rec.dma("sync", XWIN[:, 0:NCH], xr_[:, :, 0:1], waits=after(m_ln), sem="ld_xb",
                   allow_slow_non_contiguous=True)
    m_ln = emit_ln(ctx, lambda c, a, b: XWIN[:, c + a:c + b],
                   lambda c, a, b: BIG[:, c * TXX + a:c * TXX + b],
                   1, lambda c, m_xb=m_xb, m_ln=m_ln: [m_xb, m_ln], m_onc, m_onr)
    m_xb2 = rec.dma("sync", XWIN[:, NCH:2 * NCH], xr_[:, :, T + 1:T + 2],
                    waits=after(m_ln), sem="ld_xb", allow_slow_non_contiguous=True)
    m_ln = emit_ln(ctx, lambda c, a, b: XWIN[:, NCH + c + a:NCH + c + b],
                   lambda c, a, b: BIG[:, c * TXX + T + 1 + a:c * TXX + T + 1 + b],
                   1, lambda c, m_xb2=m_xb2, m_ln=m_ln: [m_xb2, m_ln], m_onc, m_onr)

    m_xs2 = None
    m_ev = {}
    m_xs = None
    for w in range(NW):
        for c in range(NCH):
            c0 = c * TXX + 1 + w * WIN
            m1 = rec.emit("vector", lambda e, c=c, c0=c0: e.tensor_scalar(
                out=CTMP[:], in0=BIG[:, c0 - 1:c0 - 1 + WIN],
                scalar1=TAPS[:, 3 * c:3 * c + 1], scalar2=None, op0=ALU.mult),
                waits=after(m_ln if w == 0 and c == 0 else None,
                            m_taps if w == 0 and c == 0 else None,
                            m_xs,
                            m_ev.get((w - 1, NCH - 1)) if c == 0 else None), inc="dv")
            m2 = rec.emit("vector", lambda e, c=c, c0=c0: e.scalar_tensor_tensor(
                out=CTMP[:], in0=BIG[:, c0:c0 + WIN],
                scalar=TAPS[:, 3 * c + 1:3 * c + 2], in1=CTMP[:],
                op0=ALU.mult, op1=ALU.add), waits=after(m1), inc="dv")
            m_xs = rec.emit("vector", lambda e, c=c, c0=c0: e.scalar_tensor_tensor(
                out=XSW[:, c * WIN:(c + 1) * WIN], in0=BIG[:, c0 + 1:c0 + 1 + WIN],
                scalar=TAPS[:, 3 * c + 2:3 * c + 3], in1=CTMP[:],
                op0=ALU.mult, op1=ALU.add), waits=after(m2), inc="dv")
        for m in range(NCH):
            bk = m % 2
            mm = None
            for k in range(NCH):
                mm = rec.emit("tensor", lambda e, m=m, k=k, bk=bk: e.matmul(
                    bank(bk), PEW[:, k * D + m * 128:k * D + (m + 1) * 128],
                    XSW[:, k * WIN:(k + 1) * WIN],
                    start=(k == 0), stop=(k == NCH - 1)),
                    waits=after(m_xs if m == 0 and k == 0 else None,
                                m_pew if w == 0 and m == 0 and k == 0 else None,
                                (m_ev.get((w, m - 2)) or m_ev.get((w - 1, m + 6))) if k == 0 else None),
                    inc="ts" if k == NCH - 1 else None)
            m_xs2 = rec.emit("vector", lambda e, w=w, m=m, bk=bk: e.scalar_tensor_tensor(
                out=BIG[:, XS2o + m * T + w * WIN:XS2o + m * T + (w + 1) * WIN],
                in0=bank(bk), scalar=1.0, in1=XSW[:, m * WIN:(m + 1) * WIN],
                op0=ALU.mult, op1=ALU.add), waits=after(mm), inc="dv")
            m_ev[(w, m)] = m_xs2

    XS2s = lambda k, w: BIG[:, XS2o + k * T + w * WIN:XS2o + k * T + (w + 1) * WIN]
    m_wslice = [None, None]
    m_evq = {}
    m_kst = [None, None]
    for m in range(16):
        db = m % 2
        m_wslice[db] = rec.dma("gpsimd", WSL[:, db * NCH * 128:(db + 1) * NCH * 128],
                               qkvwT.rearrange("(k p) j -> p k j", p=128)[:, :, m * 128:(m + 1) * 128],
                               waits=after(m_evq.get(m - 2)), sem="ld_wsl%d" % db)
        mev = None
        for w in range(NW):
            bk = 2 + (w % 2)
            mm = None
            for k in range(NCH):
                mm = rec.emit("tensor", lambda e, m=m, w=w, k=k, bk=bk, db=db: e.matmul(
                    bank(bk), WSL[:, db * NCH * 128 + k * 128:db * NCH * 128 + (k + 1) * 128],
                    XS2s(k, w), start=(k == 0), stop=(k == NCH - 1)),
                    waits=after(m_wslice[db] if w == 0 and k == 0 else None,
                                m_xs2 if m == 0 and w == 0 and k == 0 else None,
                                (m_evq.get((m, w - 2)) if w >= 2 else m_evq.get((m - 1, w + 2))) if k == 0 else None),
                    inc="ts" if k == NCH - 1 else None)
            if m < 8:
                mev = rec.emit("vector", lambda e, m=m, w=w, bk=bk: e.tensor_copy(
                    out=Q[:, m * T + w * WIN:m * T + (w + 1) * WIN], in_=bank(bk)),
                    waits=after(mm), inc="dv")
            else:
                mev = rec.emit("vector", lambda e, w=w, bk=bk: e.tensor_copy(
                    out=KEV[:, (w % 2) * WIN:(w % 2) * WIN + WIN], in_=bank(bk)),
                    waits=after(mm, m_kst[w % 2]), inc="dv")
                m_kst[w % 2] = rec.dma("sync", cc_k.ap()[(m - 8) * 128:(m - 7) * 128, w * WIN:(w + 1) * WIN],
                                       KEV[:, (w % 2) * WIN:(w % 2) * WIN + WIN],
                                       waits=after(mev), sem="st_k%d" % (w % 2))
            m_evq[(m, w)] = mev
        m_evq[m] = mev
    WV = PEW
    m_wv = rec.dma("gpsimd", WV[:], qkvwT.rearrange("(k p) j -> p k j", p=128)[:, :, 2048:3072],
                   waits=after(m_ev[(NW - 1, NCH - 1)]), sem="ld_wv")
    m_vst = [None, None]
    m_vev = {}
    for tb in range(NTB):
        for half in range(2):
            bk = 6 + half
            mm = None
            for k in range(NCH):
                mm = rec.emit("tensor", lambda e, tb=tb, half=half, k=k, bk=bk: e.matmul(
                    bank(bk), BIG[:, XS2o + k * T + tb * 128:XS2o + k * T + (tb + 1) * 128],
                    WV[:, k * D + half * 512:k * D + (half + 1) * 512],
                    start=(k == 0), stop=(k == NCH - 1)),
                    waits=after(m_wv if tb == 0 and half == 0 and k == 0 else None,
                                m_vev.get((tb - 1, half)) if k == 0 else None),
                    inc="ts" if k == NCH - 1 else None)
            mev = rec.emit("vector", lambda e, tb=tb, half=half, bk=bk: e.tensor_copy(
                out=VEV[:, half * WIN:half * WIN + WIN], in_=bank(bk)),
                waits=after(mm, m_vst[half]), inc="dv")
            m_vev[(tb, half)] = mev
            m_vst[half] = rec.dma("sync", cc_v.ap()[tb * 128:(tb + 1) * 128, half * 512:(half + 1) * 512],
                                  VEV[:, half * WIN:half * WIN + WIN],
                                  waits=after(mev), sem="st_v%d" % half)

    rec.emit("gpsimd", lambda e: e.collective_compute(
        "AllGather", ALU.bypass, replica_groups=[[0, 1, 2, 3], [4, 5, 6, 7]],
        ins=[cc_k.ap()], outs=[ko.ap()]),
        waits=after(("st_k0", rec.counts["st_k0"]), ("st_k1", rec.counts["st_k1"]),
                    ("st_v0", rec.counts["st_v0"]), ("st_v1", rec.counts["st_v1"])),
        inc="cc")
    m_ag = rec.emit("gpsimd", lambda e: e.collective_compute(
        "AllGather", ALU.bypass, replica_groups=[[0, 1, 2, 3], [4, 5, 6, 7]],
        ins=[cc_v.ap()], outs=[vo.ap()]),
        waits=after(("cc", rec.counts["cc"] + 1)) if False else (), inc="cc")
    m_ag = ("cc", rec.counts["cc"])

    m_kb = None
    m_vb = None
    m_oacc_gate = None
    m_accum_prev = None
    m_last_exp = [None, None]
    m_oev = None
    m_zev = None
    for g in range(NG):
        r, lo = g // 2, (g % 2) * 1024
        m_kb = rec.dma("sync", BIG[:, KBo:KBo + 8192],
                       ko.ap()[r * D:(r + 1) * D, lo:lo + 1024].rearrange("(c p) t -> p c t", p=128),
                       waits=after(m_ag if g == 0 else None, m_ln if g == 0 else None,
                                   m_last_exp[0] if g > 0 else None, m_last_exp[1] if g > 0 else None,
                                   m_oev if g > 0 else None),
                       sem="ld_kb%d" % (g % 2))
        m_vb = rec.dma("sync", BIG[:, VBo:VBo + 8192],
                       vo.ap()[g * 1024:(g + 1) * 1024, :].rearrange("(b p) j -> p b j", p=128),
                       waits=after(m_ag if g == 0 else None,
                                   m_oev if g > 0 else None),
                       sem="ld_vb%d" % (g % 2))
        m_ga_prev = m_oacc_gate
        for h in range(NHEAD):
            hb = (h % 2) * 64
            ch = h // 2
            for qw in range(NW):
                m_omm = None
                m_zmm = None
                for kb in range(8):
                    sbk = kb % 2
                    m_smm = rec.emit("tensor", lambda e, h=h, qw=qw, kb=kb, sbk=sbk, hb=hb, ch=ch: e.matmul(
                        bank(sbk), BIG[hb:hb + 64, KBo + ch * 1024 + kb * 128:KBo + ch * 1024 + (kb + 1) * 128],
                        Q[hb:hb + 64, ch * T + qw * WIN:ch * T + (qw + 1) * WIN],
                        start=True, stop=True),
                        waits=after(m_kb if h == 0 and qw == 0 and kb == 0 else None,
                                    m_last_exp[sbk]), inc="ts")
                    m_exp = rec.emit("scalar", lambda e, kb=kb, sbk=sbk: e.activation(
                        out=BIG[:, PTo + sbk * 512:PTo + (sbk + 1) * 512], in_=bank(sbk),
                        func=AF.Exp, scale=0.125),
                        waits=after(m_smm, m_omm if kb >= 2 else None,
                                    m_zmm if kb >= 2 else None), inc="ac")
                    m_last_exp[sbk] = m_exp
                    m_omm = rec.emit("tensor", lambda e, h=h, kb=kb, sbk=sbk: e.matmul(
                        PS[0:64, 2 * 512:2 * 512 + 512],
                        BIG[:, VBo + kb * 1024 + h * 64:VBo + kb * 1024 + (h + 1) * 64],
                        BIG[:, PTo + sbk * 512:PTo + (sbk + 1) * 512],
                        start=(kb == 0), stop=(kb == 7)),
                        waits=after(m_exp, m_vb if h == 0 and qw == 0 and kb == 0 else None,
                                    m_oev if kb == 0 else None), inc="ts")
                    m_zmm = rec.emit("tensor", lambda e, kb=kb, sbk=sbk: e.matmul(
                        PS[0:1, 3 * 512:3 * 512 + 512], ONC[:, 0:1],
                        BIG[:, PTo + sbk * 512:PTo + (sbk + 1) * 512],
                        start=(kb == 0), stop=(kb == 7)),
                        waits=after(m_zev if kb == 0 else None,
                                    m_onc if g == 0 and h == 0 and qw == 0 and kb == 0 else None),
                        inc="ts")
                m_oev = rec.emit("vector", lambda e: e.tensor_copy(
                    out=OEV[0:64, :], in_=PS[0:64, 2 * 512:2 * 512 + 512]),
                    waits=after(m_omm, m_accum_prev), inc="dv")
                m_zev = rec.emit("vector", lambda e: e.tensor_copy(
                    out=ZEV[:], in_=PS[0:1, 3 * 512:3 * 512 + 512]),
                    waits=after(m_zmm), inc="dv")
                aop = ALU.add if g > 0 else ALU.bypass
                m_o1 = rec.dma("gpsimd", oacc.ap()[h * 64:(h + 1) * 64, qw * WIN:(qw + 1) * WIN],
                               OEV[0:64, :], waits=after(m_oev, m_ga_prev), sem="st_oa",
                               accum_op=aop)
                m_accum_prev = rec.dma("gpsimd", ozacc.ap()[h:h + 1, qw * WIN:(qw + 1) * WIN],
                                       ZEV[:], waits=after(m_zev), sem="st_oa",
                                       accum_op=aop)
        m_oacc_gate = ("st_oa", rec.counts["st_oa"])
        m_accum_prev = m_oacc_gate

    m_owt = rec.dma("gpsimd", BIG[:, OWo:OWo + 8192].tensor[:, OWo:OWo + 8192] if False else BIG[:, OWo:OWo + 8192],
                    owT.rearrange("(k p) m -> p k m", p=128),
                    waits=after(m_last_exp[0], m_last_exp[1], m_oev), sem="ld_ow")
    m_ofm = None
    m_old = None
    m_zld = None
    m_zi = None
    for pch in range(NCH):
        ha, hc = 2 * pch, 2 * pch + 1
        m_old1 = rec.dma("sync", OLOAD[0:64, :], oacc.ap()[ha * 64:(ha + 1) * 64, :],
                         waits=after(m_oacc_gate, m_ofm), sem="ld_oa%d" % (pch % 2))
        m_old = rec.dma("sync", OLOAD[64:128, :], oacc.ap()[hc * 64:(hc + 1) * 64, :],
                        sem="ld_oa%d" % (pch % 2))
        m_zld = rec.dma("sync", ZROW[:, 0:T], ozacc.ap()[ha:ha + 2, :],
                        waits=after(m_zld, m_oacc_gate if pch == 0 else None,
                                    m_zi if pch >= 1 else None), sem="ld_oz%d" % (pch % 2))
        m_zi = rec.emit("vector", lambda e: e.reciprocal(
            out=ZROW[:, T:2 * T], in_=ZROW[:, 0:T]), waits=after(m_zld), inc="dv")
        m_hl = emit_hilo(ctx, ZROW[:, T:2 * T], ZRB[:, 0:T], ZRB[:, T:2 * T], m_zi)
        for w in range(NW):
            rec.emit("tensor", lambda e, w=w: e.matmul(
                bank(4), SEL2[:], ZRB[:, w * WIN:(w + 1) * WIN], start=True, stop=False),
                waits=after(m_hl, m_sel2 if pch == 0 and w == 0 else None,
                            m_ofm), inc=None)
            m_zb = rec.emit("tensor", lambda e, w=w: e.matmul(
                bank(4), SEL2[:], ZRB[:, T + w * WIN:T + (w + 1) * WIN], start=False, stop=True),
                inc="ts")
            m_ofm = rec.emit("vector", lambda e, pch=pch, w=w: e.tensor_tensor(
                out=BIG[:, OFMo + pch * T + w * WIN:OFMo + pch * T + (w + 1) * WIN],
                in0=OLOAD[:, w * WIN:(w + 1) * WIN], in1=bank(4), op=ALU.mult),
                waits=after(m_zb, m_old), inc="dv")
    m_store = None
    m_resid_prev = [None] * NCH
    for w in range(NW):
        m_xw2 = rec.dma("sync", XWIN[:], xr_[:, :, 1 + w * WIN:1 + (w + 1) * WIN],
                        waits=after(m_ln if w == 0 else None,
                                    ("st%d" % ((w - 1) % 2), rec.counts.get("st%d" % ((w - 1) % 2), 0)) if w > 0 else None,
                                    *(m_resid_prev if w > 0 else [])),
                        sem="ld_xw2")
        for m in range(NCH):
            bk = m % 2
            mm = None
            for k in range(NCH):
                mm = rec.emit("tensor", lambda e, m=m, k=k, bk=bk, w=w: e.matmul(
                    bank(bk), BIG[:, OWo + k * D + m * 128:OWo + k * D + (m + 1) * 128],
                    BIG[:, OFMo + k * T + w * WIN:OFMo + k * T + (w + 1) * WIN],
                    start=(k == 0), stop=(k == NCH - 1)),
                    waits=after(m_ofm if w == 0 and m == 0 and k == 0 else None,
                                m_owt if w == 0 and m == 0 and k == 0 else None,
                                m_resid_prev[m - 2] if k == 0 else None),
                    inc="ts" if k == NCH - 1 else None)
            m_res = rec.emit("vector", lambda e, m=m, bk=bk: e.scalar_tensor_tensor(
                out=XWIN[:, m * WIN:(m + 1) * WIN], in0=bank(bk), scalar=1.0,
                in1=XWIN[:, m * WIN:(m + 1) * WIN], op0=ALU.mult, op1=ALU.add),
                waits=after(mm, m_xw2), inc="dv")
            m_resid_prev[m] = m_res
            m_store = rec.dma("sync", xo.rearrange("(c p) t -> p c t", p=128)[:, m, w * WIN:(w + 1) * WIN],
                              XWIN[:, m * WIN:(m + 1) * WIN], waits=after(m_res), sem="st%d" % (w % 2))
    return [(s, rec.counts[s]) for s in ("st0", "st1") if s in rec.counts]


def build_s5(nc):
    rec = Rec(nc)
    ctx = Ctx(nc, rec)
    x_in = nc.dram_tensor("x", [D, T + 2], F32, kind="ExternalInput").ap()
    pewT = nc.dram_tensor("pewT", [D, D], F32, kind="ExternalInput").ap()
    qkvwT = nc.dram_tensor("qkvwT", [D, 3 * D], F32, kind="ExternalInput").ap()
    owT = nc.dram_tensor("owT", [D, D], F32, kind="ExternalInput").ap()
    taps = nc.dram_tensor("taps", [128, 3 * NCH], F32, kind="ExternalInput").ap()
    onc = nc.dram_tensor("ones_col", [128, 1], F32, kind="ExternalInput").ap()
    onr = nc.dram_tensor("ones_row", [1, 128], F32, kind="ExternalInput").ap()
    sel2 = nc.dram_tensor("sel2", [2, 128], F32, kind="ExternalInput").ap()
    xo = nc.dram_tensor("xo", [D, T], F32, kind="ExternalOutput").ap()
    oacc = nc.dram_tensor("oacc", [NHEAD * 64, T], F32, kind="ExternalOutput")
    ozacc = nc.dram_tensor("ozacc", [NHEAD, T], F32, kind="ExternalOutput")
    mil = emit_s5(ctx, x_in, xo, pewT, qkvwT, owT, taps, onc, onr, sel2, oacc, ozacc)
    rec.finalize(mil)
    return nc


def run_s5(x_full, conv_w, pe_w, qkv_w, out_w):
    nc = build_s5(bass.Bass("TRN2", target_bir_lowering=False,
                            dynamic_dma_scratch_size=16384))
    flat = np.ascontiguousarray(np.asarray(x_full).reshape(2 * 8192, D))
    taps_h = np.ascontiguousarray(np.asarray(conv_w)[:, 0, :, 1])
    taps_t = np.zeros((128, 3 * NCH), np.float32)
    for c in range(NCH):
        taps_t[:, 3 * c:3 * (c + 1)] = taps_h[c * 128:(c + 1) * 128]
    sel2v = np.zeros((2, 128), np.float32)
    sel2v[0, 0:64] = 1.0
    sel2v[1, 64:128] = 1.0
    in_maps = []
    for c in range(NCORES):
        s, e = c * T, (c + 1) * T
        prev = flat[s - 1:s].T if s % 8192 else np.zeros((D, 1), np.float32)
        nxt = flat[e:e + 1].T if e % 8192 else np.zeros((D, 1), np.float32)
        xslab = np.concatenate([prev, flat[s:e].T, nxt], axis=1).astype(np.float32)
        in_maps.append({
            "x": np.ascontiguousarray(xslab),
            "pewT": np.ascontiguousarray(np.asarray(pe_w).T),
            "qkvwT": np.ascontiguousarray(np.asarray(qkv_w).T),
            "owT": np.ascontiguousarray(np.asarray(out_w).T),
            "taps": taps_t, "sel2": sel2v,
            "ones_col": np.ones((128, 1), np.float32),
            "ones_row": np.ones((1, 128), np.float32),
        })
    res = run_bass_kernel_spmd(nc, in_maps, core_ids=list(range(NCORES)))
    out = np.concatenate([res.results[c]["xo"].T for c in range(NCORES)], axis=0)
    return out.reshape(2, 8192, D)


NX1 = 3584
HALO1 = 1536
NW1 = 7
NPAIR = 22
NK2 = 22
TX1 = 513


def emit_s1(ctx, x_in, x1d, xo, wqkvf, w1f, w2f, woT, b1r, b2r,
            tqa, tqb, tka, tkb, qscol, kscol, gcol, bcol,
            perm, hsel, sel2, ident, mband, hv, onc, onr, onr5,
            hvw=None, layers=2, nwin_limit=None):
    nc, rec = ctx.nc, ctx.rec
    TX = TX1

    XW = ctx.sb("XW", [128, NCH * TX], F32)
    XSB = ctx.sb("XSB", [128, NCH * TX], BF16)
    QP = ctx.sb("QP", [128, 4 * WIN], BF16)
    KP = ctx.sb("KP", [128, 4 * 2 * WIN], BF16)
    VP = ctx.sb("VP", [128, 8 * WIN], BF16)
    SCR = ctx.sb("SCR", [128, NPAIR * WIN], BF16)
    XLN = ctx.sb("XLN", [128, NCH * WIN], BF16)
    OFMW = ctx.sb("OFMW", [128, 4 * WIN], BF16)
    SM = ctx.sb("SM", [128, 640], F32)
    PTM = ctx.sb("PTM", [128, 640], BF16)
    PTN = ctx.sb("PTN", [128, 640], BF16)
    PTT = ctx.sb("PTT", [128, 640], BF16)
    TBQA = ctx.sb("TBQA", [128, WIN], BF16)
    TBQB = ctx.sb("TBQB", [128, WIN], BF16)
    TBKA = ctx.sb("TBKA", [128, WIN], BF16)
    TBKB = ctx.sb("TBKB", [128, WIN], BF16)
    QRAW = ctx.sb("QRAW", [128, WIN], BF16)
    QT1 = ctx.sb("QT1", [128, WIN], F32)
    T1Q = ctx.sb("T1Q", [128, WIN], BF16)
    RT = ctx.sb("RT", [128, WIN], BF16)
    QA = ctx.sb("QA", [128, WIN], F32)
    XAF = ctx.sb("XAF", [128, WIN], F32)
    XAF2 = ctx.sb("XAF2", [128, WIN], F32)
    XB = ctx.sb("XB", [128, NCH * WIN], BF16)
    XSQ = ctx.sb("XSQ", [128, WIN], BF16)
    SROW = ctx.sb("SROW", [1, 4 * WIN + 16], F32)
    SRB = ctx.sb("SRB", [1, 4 * WIN + 16], BF16)
    WSLQ = ctx.sb("WSLQ", [128, 2 * 1024], BF16)
    WQV = ctx.sb("WQV", [128, NCH * WIN], BF16)
    W1SL = ctx.sb("W1SL", [128, 2 * 2048], BF16)
    W2SL = ctx.sb("W2SL", [128, 2 * 2816], BF16)
    WOS = ctx.sb("WOS", [128, 4 * D], BF16)
    B1 = ctx.sb("B1", [1, 5460], BF16)
    B2 = ctx.sb("B2", [1, D], BF16)
    RROW = ctx.sb("RROW", [2, 3 * WIN], F32)
    RRB = ctx.sb("RRB", [2, 2 * WIN], BF16)
    RM = ctx.sb("RM", [128, 4], F32)
    PERM = ctx.sb("PERM", [128, 128], BF16)
    HSEL = ctx.sb("HSEL", [128, 2], BF16)
    SEL2 = ctx.sb("SEL2", [2, 128], BF16)
    IDT = ctx.sb("IDT", [128, 128], BF16)
    MBIAS = ctx.sb("MBIAS", [128, 640], F32)
    HV = ctx.sb("HV", [128, 36], F32)
    HVW = ctx.sb("HVW", [128, 4 * 640], BF16)
    QS = ctx.sb("QS", [128, 4], F32)
    GB = ctx.sb("GB", [128, 4 * NCH], F32)
    XMCOL = ctx.sb("XMCOL", [128, 16], F32)
    XQCOL = ctx.sb("XQCOL", [128, 8], BF16)
    XQF = ctx.sb("XQF", [128, 8], F32)
    ONC = ctx.sb("ONC", [128, 1], BF16)
    ONR = ctx.sb("ONR", [1, 128], BF16)
    ONR5 = ctx.sb("ONR5", [1, WIN], BF16)
    PS = ctx.ps("PS", [128, 4096], F32)
    bank = lambda b, wdt=512: PS[:, b * 512:b * 512 + wdt]

    x_r = x_in.rearrange("(c p) t -> p c t", p=128)
    x1_r = x1d.ap().rearrange("(c p) t -> p c t", p=128)
    xo_r = xo.rearrange("(c p) t -> p c t", p=128)
    wqkv_r = wqkvf.rearrange("(l k p) m -> p l k m", l=2, p=128)
    w1_r = w1f.rearrange("(l k p) m -> p l k m", l=2, p=128)
    XW_3 = XW.ap().rearrange("p (c t) -> p c t", c=NCH)
    XSB_3 = XSB.ap().rearrange("p (c t) -> p c t", c=NCH)

    for ap_, tgt in ((perm, PERM[:]), (hsel, HSEL[:]), (sel2, SEL2[:]),
                     (ident, IDT[:]), (onc, ONC[:]), (onr, ONR[:]),
                     (onr5, ONR5[:])):
        rec.dma("gpsimd", tgt, ap_, sem="ld_cst")
    for ap_, tgt in ((mband, MBIAS[:]), (hv, HV[:]), (qscol, QS[:, 0:2]),
                     (kscol, QS[:, 2:4]), (gcol, GB[:, 0:2 * NCH]),
                     (bcol, GB[:, 2 * NCH:4 * NCH])):
        mC = rec.dma("sync", tgt, ap_, sem="ld_cst2")
    mCg = ("ld_cst", rec.counts["ld_cst"])
    m_perm = m_hsel = m_sel2 = m_idt = m_onc = m_onr = m_onr5 = mCg
    m_mb = m_hv = m_qs = m_gb = mC
    m_const = [mCg, mC]

    m_store = None
    for w in range(NW1):
        m_xw = rec.dma("sync", XW_3[:, :, 0:WIN],
                       x_r[:, :, 1 + w * WIN:1 + (w + 1) * WIN],
                       waits=after(m_store), sem="ld_xi")
        m_ln = emit_ln(ctx, lambda c, a, b: XW[:, c * TX + a:c * TX + b],
                       lambda c, a, b: XW[:, c * TX + a:c * TX + b],
                       WIN, lambda c, m_xw=m_xw: [m_xw], m_onc, m_onr)
        m_store = rec.dma("sync", x1_r[:, :, 1 + w * WIN:1 + (w + 1) * WIN],
                          XW_3[:, :, 0:WIN], waits=after(m_ln), sem="st_x1")
    m_zc = rec.emit("vector", lambda e: e.memset(XQF[:], 0.0),
                    waits=after(m_store), inc="dv")
    m_store = rec.dma("sync", x1_r[:, :, 0:1], XQF[:], waits=after(m_zc),
                      sem="st_x1", allow_slow_non_contiguous=True)

    env = dict(XW=XW, XSB=XSB, QP=QP, KP=KP, VP=VP, SCR=SCR, XLN=XLN, OFMW=OFMW,
               SM=SM, PTM=PTM, PTN=PTN, PTT=PTT, TBQA=TBQA, TBQB=TBQB, TBKA=TBKA,
               TBKB=TBKB, QRAW=QRAW, QT1=QT1, T1Q=T1Q, RT=RT, QA=QA, XAF=XAF,
               XSQ=XSQ, WSLQ=WSLQ, WQV=WQV, W1SL=W1SL, W2SL=W2SL, WOS=WOS, B1=B1,
               B2=B2, RROW=RROW, RRB=RRB, RM=RM, PERM=PERM, HSEL=HSEL, SEL2=SEL2,
               IDT=IDT, MBIAS=MBIAS, HV=HV, QS=QS, GB=GB, XMCOL=XMCOL,
               XQCOL=XQCOL, ONC=ONC, ONR=ONR, ONR5=ONR5, PS=PS, bank=bank,
               x1_r=x1_r, wqkv_r=wqkv_r, w1_r=w1_r, w2f=w2f, tqa=tqa, tqb=tqb,
               tka=tka, tkb=tkb, XW_3=XW_3, XSB_3=XSB_3, m_onc=m_onc,
               m_onr=m_onr, HVW=HVW, hvw=hvw)

    st = {"store": m_store, "attn": None, "tbl": None, "p1dst": None,
          "wqv_use": None, "kvp_done": None, "xmprev": None}

    for l in range(layers):
        t0 = 0 if l == 0 else 512
        nwin = NW1 if l == 0 else NW1 - 1
        if nwin_limit is not None:
            nwin = min(nwin, nwin_limit)
        m_wo = rec.dma("gpsimd", WOS[:],
                       woT.rearrange("(l k p) m -> p l k m", l=2, p=128)[:, l],
                       waits=after(st["store"]), sem="ld_wo")
        m_b1 = rec.dma("gpsimd", B1[:], b1r[l:l + 1, :], waits=after(st["store"]),
                       sem="ld_b")
        m_b2 = rec.dma("gpsimd", B2[:], b2r[l:l + 1, :], sem="ld_b")
        m_xqf = rec.dma("sync", XQF[:], x1_r[:, :, 0:1],
                        waits=after(st["store"]), sem="ld_xq",
                        allow_slow_non_contiguous=True)
        m_xqc = rec.emit("vector", lambda e: e.tensor_copy(out=XQCOL[:], in_=XQF[:]),
                         waits=after(m_xqf), inc="dv")
        m_xm0 = rec.emit("vector", lambda e: e.memset(XMCOL[:, 0:8], 0.0),
                         waits=after(st["store"]), inc="dv")
        m_kv0 = rec.emit("vector", lambda e: e.memset(KP[:], 0.0),
                         waits=after(st["store"]), inc="dv")
        m_kv1 = rec.emit("vector", lambda e: e.memset(VP[:], 0.0), inc="dv")
        st["entry"] = [m_wo, m_b2, m_xqc, m_xm0, m_kv0, m_kv1] + m_const
        st["xmprev"] = m_xm0
        if l == 1:
            emit_s1_win(ctx, l, -1, 0, True, st, env)
        for w in range(nwin):
            emit_s1_win(ctx, l, w, t0, False, st, env)

    X1F = SCR.ap()[:, 0:8192].bitcast(F32)
    m_res_prev = None
    m_st = None
    for w in range(NW):
        m_l1 = rec.dma("sync", XW_3[:, :, 0:WIN],
                       x_r[:, :, 1537 + w * WIN:1537 + (w + 1) * WIN],
                       waits=after(st["store"] if w == 0 else None, m_res_prev, m_st),
                       sem="ld_f%d" % (w % 2))
        m_l2 = rec.dma("sync", X1F.rearrange("p (c t) -> p c t", c=NCH)[:, :, 0:WIN],
                       x1_r[:, :, 1537 + w * WIN:1537 + (w + 1) * WIN],
                       waits=after(st["store"] if w == 0 else None, m_res_prev),
                       sem="ld_f%d" % (w % 2))
        m_res = None
        for c in range(NCH):
            m_res = rec.emit("vector", lambda e, c=c: e.scalar_tensor_tensor(
                out=XW[:, c * TX:c * TX + WIN], in0=X1F[:, c * WIN:(c + 1) * WIN],
                scalar=0.5, in1=XW[:, c * TX:c * TX + WIN], op0=ALU.mult, op1=ALU.add),
                waits=after(m_l2 if c == 0 else None),
                inc="dv")
        m_res_prev = m_res
        m_st = rec.dma("sync", xo_r[:, :, w * WIN:(w + 1) * WIN], XW_3[:, :, 0:WIN],
                       waits=after(m_res), sem="st_o%d" % (w % 2))
    return [("st_o0", rec.counts["st_o0"]), ("st_o1", rec.counts["st_o1"])]


def emit_s1_win(ctx, l, w, t0, kv_only, st, env):
    nc, rec = ctx.nc, ctx.rec
    (XW, XSB, QP, KP, VP, SCR, XLN, OFMW, SM, PTM, PTN, PTT, TBQA, TBQB, TBKA,
     TBKB, QRAW, QT1, T1Q, RT, QA, XAF, XSQ, WSLQ, WQV, W1SL, W2SL, WOS, B1, B2,
     RROW, RRB, RM, PERM, HSEL, SEL2, IDT, MBIAS, HV, QS, GB, XMCOL, XQCOL, ONC,
     ONR, ONR5, PS, bank, x1_r, wqkv_r, w1_r, w2f, tqa, tqb, tka, tkb, XW_3,
     XSB_3, m_onc, m_onr) = (
        env["XW"], env["XSB"], env["QP"], env["KP"], env["VP"], env["SCR"],
        env["XLN"], env["OFMW"], env["SM"], env["PTM"], env["PTN"], env["PTT"],
        env["TBQA"], env["TBQB"], env["TBKA"], env["TBKB"], env["QRAW"],
        env["QT1"], env["T1Q"], env["RT"], env["QA"], env["XAF"], env["XSQ"],
        env["WSLQ"], env["WQV"], env["W1SL"], env["W2SL"], env["WOS"], env["B1"],
        env["B2"], env["RROW"], env["RRB"], env["RM"], env["PERM"], env["HSEL"],
        env["SEL2"], env["IDT"], env["MBIAS"], env["HV"], env["QS"], env["GB"],
        env["XMCOL"], env["XQCOL"], env["ONC"], env["ONR"], env["ONR5"],
        env["PS"], env["bank"], env["x1_r"], env["wqkv_r"], env["w1_r"],
        env["w2f"], env["tqa"], env["tqb"], env["tka"], env["tkb"], env["XW_3"],
        env["XSB_3"], env["m_onc"], env["m_onr"])
    TX = TX1
    XWs = lambda c, a, b: XW[:, c * TX + a:c * TX + b]
    XSBs = lambda c, a, b: XSB[:, c * TX + a:c * TX + b]
    W0 = 0 if kv_only else t0 + w * WIN
    entry = st.pop("entry", None) or []

    m_kvcp = None
    if not (l == 0 and w == 0) and not kv_only:
        kvp = st.pop("kvp_done", None) or []
        mk = None
        for c in range(4):
            mk = rec.emit("vector", lambda e, c=c: e.tensor_copy(
                out=KP[:, c * 1024:c * 1024 + 512],
                in_=KP[:, c * 1024 + 512:c * 1024 + 1024]),
                waits=after(st["attn"] if c == 0 else None,
                            st["p1dst"] if c == 0 else None,
                            *(kvp if c == 0 else [])), inc="dv")
        m_kvcp = rec.emit("vector", lambda e: e.tensor_copy(
            out=VP[:, 0:4 * WIN], in_=VP[:, 4 * WIN:8 * WIN]),
            waits=after(mk), inc="dv")

    m_xw = rec.dma("sync", XW_3[:, :, 0:TX], x1_r[:, :, W0:W0 + TX],
                   waits=after(st["store"], m_kvcp, *entry), sem="ld_w%d" % (w % 2))
    m_cast = []
    for c in range(NCH):
        m_cast.append(rec.emit("vector", lambda e, c=c: e.tensor_copy(
            out=XSBs(c, 0, TX), in_=XWs(c, 0, TX)),
            waits=after(m_xw if c == 0 else None), inc="dv"))
    m_fix = None
    for c in range(4, NCH):
        m_fix = rec.emit("vector", lambda e, c=c: e.tensor_copy(
            out=XSBs(c, 0, 1), in_=XQCOL[:, c:c + 1]),
            waits=after(m_cast[c]), inc="dv")
    m_stash = rec.emit("vector", lambda e: e.tensor_copy(
        out=XQCOL[:], in_=XSB_3[:, :, TX - 1]),
        waits=after(m_cast[-1], m_fix), inc="dv")
    m_p0 = m_stash

    m_tb = {}
    tbl_loads = ([] if kv_only else [("qa", TBQA, tqa), ("qb", TBQB, tqb)]) + \
        [("ka", TBKA, tka), ("kb", TBKB, tkb)]
    for nm, buf, srcm in tbl_loads:
        m_tb[nm] = rec.dma("gpsimd", buf[:], srcm[:, W0:W0 + WIN],
                           waits=after(st["tbl"]), sem="ld_tb" + nm)
    HVW, hvw = env["HVW"], env["hvw"]
    m_hvw = None
    if not kv_only:
        qg0 = W0 // 128
        m_hvw = rec.dma("gpsimd", HVW[:], hvw[:, qg0 * 640:(qg0 + 4) * 640],
                        waits=after(st["tbl"]), sem="ld_tbhv")
    ocs = list(range(4, 8) if kv_only else range(0, 8))
    m_xsq = {}
    m_qraw = {}
    m_tseps = {}
    m_t1 = {}
    m_t1q = {}
    m_rt = {}
    m_qbd = {}
    m_dst = {}
    for oc in ocs:
        bk = oc % 2
        buf = oc % 2
        m_wsl = rec.dma("gpsimd", WSLQ[:, buf * 1024:(buf + 1) * 1024],
                        wqkv_r[:, l, :, oc * 128:(oc + 1) * 128],
                        waits=after(m_qraw.get(oc - 2), m_xsq.get(oc - 2)),
                        sem="ld_qk%d" % buf)
        mm = None
        for kc in range(NCH):
            sh = 1 if kc < 4 else 0
            mm = rec.emit("tensor", lambda e, kc=kc, bk=bk, buf=buf, sh=sh: e.matmul(
                bank(bk), WSLQ[:, buf * 1024 + kc * 128:buf * 1024 + (kc + 1) * 128],
                XSBs(kc, sh, sh + WIN), start=(kc == 0), stop=(kc == NCH - 1)),
                waits=after(m_wsl if kc == 0 else None,
                            m_p0 if kc == 0 else None,
                            m_qraw.get(oc - 2) if kc == 0 else None,
                            m_xsq.get(oc - 2) if kc == 0 else None),
                inc="ts" if kc == NCH - 1 else None)
        m_xsq[oc] = rec.emit("scalar", lambda e, bk=bk: e.activation(
            out=XSQ[:], in_=bank(bk), func=AF.Square), waits=after(mm), inc="ac")
        m_qraw[oc] = rec.emit("scalar", lambda e, bk=bk: e.copy(
            out=QRAW[:], in_=bank(bk)),
            waits=after(mm, m_xsq[oc], m_t1.get(oc - 1)), inc="ac")
        m_ss = rec.emit("tensor", lambda e: e.matmul(
            PS[0:2, 6 * 512:6 * 512 + 512], HSEL[:], XSQ[:], start=True, stop=True),
            waits=after(m_xsq[oc], m_tseps.get(oc - 1)), inc="ts")
        m_eps = rec.emit("vector", lambda e: e.tensor_scalar(
            out=RROW[:, 0:512], in0=PS[0:2, 6 * 512:6 * 512 + 512], scalar1=1e-24,
            scalar2=None, op0=ALU.add), waits=after(m_ss), inc="dv")
        m_tseps[oc] = m_eps
        m_lnr = rec.emit("scalar", lambda e: e.activation(
            out=RROW[:, 512:1024], in_=RROW[:, 0:512], func=AF.Ln),
            waits=after(m_eps), inc="ac")
        m_rsq = rec.emit("scalar", lambda e: e.activation(
            out=RROW[:, 1024:1536], in_=RROW[:, 512:1024], func=AF.Exp, scale=-0.5),
            waits=after(m_lnr), inc="ac")
        m_hl = emit_hilo(ctx, RROW[:, 1024:1536], RRB[:, 0:512], RRB[:, 512:1024],
                         m_rsq)
        rec.emit("tensor", lambda e: e.matmul(
            bank(3), SEL2[:], RRB[:, 0:512], start=True, stop=False),
            waits=after(m_hl, m_t1.get(oc - 1)), inc=None)
        m_bc = rec.emit("tensor", lambda e: e.matmul(
            bank(3), SEL2[:], RRB[:, 512:1024], start=False, stop=True), inc="ts")
        m_t1[oc] = rec.emit("vector", lambda e: e.tensor_tensor(
            out=QT1[:], in0=QRAW[:], in1=bank(3), op=ALU.mult),
            waits=after(m_bc, m_qraw[oc]), inc="dv")
        sccol = l if oc < 4 else (2 + l)
        m_t1q[oc] = rec.emit("vector", lambda e, sccol=sccol: e.tensor_scalar(
            out=T1Q[:], in0=QT1[:], scalar1=QS[:, sccol:sccol + 1], scalar2=None,
            op0=ALU.mult), waits=after(m_t1[oc]), inc="dv")
        m_pmm = rec.emit("tensor", lambda e: e.matmul(
            bank(2), PERM[:], T1Q[:], start=True, stop=True),
            waits=after(m_t1q[oc], m_rt.get(oc - 1)), inc="ts")
        m_rt[oc] = rec.emit("scalar", lambda e: e.copy(
            out=RT[:], in_=bank(2)),
            waits=after(m_pmm, m_qbd.get(oc - 1)), inc="ac")
        TA, TB = (TBQA, TBQB) if oc < 4 else (TBKA, TBKB)
        tka_m = m_tb["qa" if oc < 4 else "ka"]
        tkb_m = m_tb["qb" if oc < 4 else "kb"]
        m_qa = rec.emit("vector", lambda e, TA=TA: e.tensor_tensor(
            out=QA[:], in0=T1Q[:], in1=TA[:], op=ALU.mult),
            waits=after(m_t1q[oc], tka_m), inc="dv")
        m_qbd[oc] = rec.emit("vector", lambda e, TB=TB: e.tensor_tensor(
            out=XAF[:], in0=RT[:], in1=TB[:], op=ALU.mult),
            waits=after(m_rt[oc], tkb_m), inc="dv")
        if oc < 4:
            dstf = lambda oc=oc: QP[:, (oc % 4) * WIN:((oc % 4) + 1) * WIN]
            extra = st["attn"]
        else:
            dstf = lambda oc=oc: KP[:, (oc - 4) * 1024 + 512:(oc - 4) * 1024 + 1024]
            extra = m_kvcp
        m_dst[oc] = rec.emit("vector", lambda e, dstf=dstf: e.tensor_tensor(
            out=dstf(), in0=QA[:], in1=XAF[:], op=ALU.add),
            waits=after(m_qa, m_qbd[oc], extra), inc="dv")
    m_wqv = rec.dma("gpsimd", WQV[:], wqkv_r[:, l, :, 1024:1536],
                    waits=after(st["wqv_use"]), sem="ld_qv")
    m_vev = {}
    for tb in range(4):
        bk = tb % 2
        mm = None
        for kc in range(NCH):
            sh = 1 if kc < 4 else 0
            mm = rec.emit("tensor", lambda e, tb=tb, kc=kc, bk=bk, sh=sh: e.matmul(
                bank(bk), XSBs(kc, sh + tb * 128, sh + (tb + 1) * 128),
                WQV[:, kc * WIN:(kc + 1) * WIN], start=(kc == 0),
                stop=(kc == NCH - 1)),
                waits=after(m_wqv if tb == 0 and kc == 0 else None,
                            (m_qraw[ocs[-1]] if tb == 0 and kc == 0 else None),
                            (m_xsq[ocs[-1]] if tb == 0 and kc == 0 else None),
                            (m_vev.get(tb - 2) if kc == 0 else None)),
                inc="ts" if kc == NCH - 1 else None)
        m_vev[tb] = rec.emit("scalar", lambda e, tb=tb, bk=bk: e.copy(
            out=VP[:, (4 + tb) * WIN:(5 + tb) * WIN], in_=bank(bk)),
            waits=after(mm, m_kvcp if tb == 0 else None), inc="ac")
    st["tbl"] = m_dst[ocs[-1]]
    st["wqv_use"] = ("ts", rec.counts["ts"])
    st["p1dst"] = m_dst[ocs[-1]]
    m_p1 = [m_dst[oc] for oc in ocs] + [m_vev[tb] for tb in range(4)]
    if kv_only:
        st["attn"] = None
        st["kvp_done"] = m_p1
        return

    qb_glob0 = W0 // 128
    m_mask_prev = None
    m_exp_prev = None
    m_tsp_prev = None
    m_oev = None
    m_tr_last = None
    m_tevall = None
    m_omm_last = None
    for h in range(8):
        hb = (h % 2) * 64
        c = h // 2
        for qb in range(4):
            first = (h == 0 and qb == 0)
            m_s1 = rec.emit("tensor", lambda e, hb=hb, c=c, qb=qb: e.matmul(
                bank(0), QP[hb:hb + 64, c * WIN + qb * 128:c * WIN + (qb + 1) * 128],
                KP[hb:hb + 64, c * 1024 + qb * 128:c * 1024 + qb * 128 + 512],
                start=True, stop=True),
                waits=after(*(m_p1 if first else [m_mask_prev])), inc="ts")
            m_s2 = rec.emit("tensor", lambda e, hb=hb, c=c, qb=qb: e.matmul(
                PS[:, 512:640],
                QP[hb:hb + 64, c * WIN + qb * 128:c * WIN + (qb + 1) * 128],
                KP[hb:hb + 64, c * 1024 + qb * 128 + 512:c * 1024 + qb * 128 + 640],
                start=True, stop=True), inc="ts")
            m_mask = rec.emit("vector", lambda e: e.scalar_tensor_tensor(
                out=SM[:], in0=PS[:, 0:640], scalar=1.0, in1=MBIAS[:],
                op0=ALU.mult, op1=ALU.add),
                waits=after(m_s1, m_s2, m_exp_prev), inc="dv")
            m_mask_prev = m_mask
            m_rmx = rec.emit("vector", lambda e: e.reduce_max(
                out=RM[:, 0:1], in_=SM[:], axis=AX.X), waits=after(m_mask), inc="dv")
            m_nrm = rec.emit("vector", lambda e: e.tensor_scalar(
                out=RM[:, 1:2], in0=RM[:, 0:1], scalar1=-1.0, scalar2=None,
                op0=ALU.mult), waits=after(m_rmx), inc="dv")
            m_exp = rec.emit("scalar", lambda e: e.activation(
                out=PTM[:], in_=SM[:], func=AF.Exp, bias=RM[:, 1:2], scale=1.0),
                waits=after(m_nrm, m_tr_last), inc="ac")
            m_exp_prev = m_exp
            m_hvl = rec.emit("vector", lambda e, qb=qb: e.tensor_tensor(
                out=PTN[:], in0=PTM[:], in1=HVW[:, qb * 640:(qb + 1) * 640],
                op=ALU.mult),
                waits=after(m_exp, m_tsp_prev,
                            m_hvw if h == 0 and qb == 0 else None), inc="dv")
            m_z = rec.emit("vector", lambda e: e.reduce_sum(
                out=RM[:, 2:3], in_=PTN[:], axis=AX.X), waits=after(m_hvl), inc="dv")
            m_ze = rec.emit("vector", lambda e: e.tensor_scalar(
                out=RM[:, 3:4], in0=RM[:, 2:3], scalar1=1e-30, scalar2=None,
                op0=ALU.add), waits=after(m_z), inc="dv")
            m_zi = rec.emit("vector", lambda e: e.reciprocal(
                out=RM[:, 0:1], in_=RM[:, 3:4]), waits=after(m_ze), inc="dv")
            m_tsp = rec.emit("vector", lambda e: e.tensor_scalar(
                out=PTM[:], in0=PTN[:], scalar1=RM[:, 0:1], scalar2=None,
                op0=ALU.mult), waits=after(m_zi), inc="dv")
            m_tsp_prev = m_tsp
            m_omms = []
            for sb in range(5):
                m_tr = rec.emit("tensor", lambda e, sb=sb: e.matmul(
                    PS[:, 1024 + sb * 128:1024 + (sb + 1) * 128],
                    PTM[:, sb * 128:(sb + 1) * 128],
                    IDT[:], start=True, stop=True),
                    waits=after(m_tsp if sb == 0 else None,
                                m_tevall if sb == 0 else None),
                    inc="ts" if sb == 4 else None)
                if sb == 4:
                    m_tr_last = m_tr
            m_tevall = rec.emit("scalar", lambda e: e.copy(
                out=PTT[:], in_=PS[:, 1024:1664]),
                waits=after(m_tr_last, m_omm_last), inc="ac")
            for sb in range(5):
                m_omm = rec.emit("tensor", lambda e, sb=sb, h=h, qb=qb: e.matmul(
                    PS[0:64, 4 * 512:4 * 512 + 128],
                    VP[:, (qb + sb) * WIN + h * 64:(qb + sb) * WIN + h * 64 + 64],
                    PTT[:, sb * 128:(sb + 1) * 128], start=(sb == 0), stop=(sb == 4)),
                    waits=after(m_tevall if sb == 0 else None,
                                m_oev if sb == 0 else None), inc="ts")
                m_omms.append(m_omm)
            m_omm_last = m_omms[-1]
            m_oev = rec.emit("scalar", lambda e, hb=hb, c=c, qb=qb: e.copy(
                out=OFMW[hb:hb + 64, c * WIN + qb * 128:c * WIN + (qb + 1) * 128],
                in_=PS[0:64, 4 * 512:4 * 512 + 128]),
                waits=after(m_omms[-1]), inc="ac")
    st["attn"] = m_oev
    m_p2 = m_oev

    m_stt3 = {}
    for m in range(NCH):
        bk = m % 2
        mm = None
        for kc in range(4):
            mm = rec.emit("tensor", lambda e, m=m, kc=kc, bk=bk: e.matmul(
                bank(bk), WOS[:, kc * D + m * 128:kc * D + (m + 1) * 128],
                OFMW[:, kc * WIN:(kc + 1) * WIN], start=(kc == 0), stop=(kc == 3)),
                waits=after(m_p2 if m == 0 and kc == 0 else None,
                            m_stt3.get(m - 2) if kc == 0 else None),
                inc="ts" if kc == 3 else None)
        m_stt3[m] = rec.emit("vector", lambda e, m=m, bk=bk: e.scalar_tensor_tensor(
            out=XWs(m, 1, 1 + WIN), in0=bank(bk), scalar=1.0, in1=XWs(m, 1, 1 + WIN),
            op0=ALU.mult, op1=ALU.add), waits=after(mm), inc="dv")
    m_p3 = [m_stt3[m] for m in range(NCH)]

    nb, cb = (w + 1) % 2, w % 2
    m_xm = rec.emit("vector", lambda e, nb=nb: e.tensor_copy(
        out=XMCOL[:, nb * 8:nb * 8 + 8], in_=XW_3[:, :, TX - 1]),
        waits=after(*m_p3), inc="dv")
    XSH = SCR.ap()[:, 0:8192].bitcast(F32)
    m_sh = []
    for c in range(NCH):
        sh = 1 if c < 4 else 0
        m1 = rec.emit("vector", lambda e, c=c, sh=sh: e.tensor_copy(
            out=XSH[:, c * WIN:(c + 1) * WIN], in_=XWs(c, sh, sh + WIN)),
            waits=after(*(m_p3 if c == 0 else [])), inc="dv")
        if c >= 4:
            m1 = rec.emit("vector", lambda e, c=c, cb=cb: e.tensor_copy(
                out=XSH[:, c * WIN:c * WIN + 1],
                in_=XMCOL[:, cb * 8 + c:cb * 8 + c + 1]),
                waits=after(m1, st["xmprev"]), inc="dv")
        m_sh.append(m1)
    st["xmprev"] = m_xm
    m_ln = emit_ln(ctx, lambda c, a, b: XSH[:, c * WIN + a:c * WIN + b],
                   lambda c, a, b: XLN[:, c * WIN + a:c * WIN + b],
                   WIN, lambda c: [m_sh[c], m_sh[-1]], m_onc, m_onr,
                   g_ap=lambda c, l=l: GB[:, l * NCH + c:l * NCH + c + 1],
                   b_ap=lambda c, l=l: GB[:, 2 * NCH + l * NCH + c:2 * NCH + l * NCH + c + 1])

    m_ff = {}
    m_gev_prev = None
    for i in range(NPAIR):
        Ma = 128 if i < NPAIR - 1 else 42
        buf = i % 2
        m_w1a = rec.dma("gpsimd", W1SL[:, buf * 2048:buf * 2048 + 8 * Ma],
                        w1_r[:, l, :, i * 128:i * 128 + Ma],
                        waits=after(m_ff.get(i - 2)), sem="ld_w1%d" % buf)
        m_w1g = rec.dma("gpsimd", W1SL[:, buf * 2048 + 1024:buf * 2048 + 1024 + 8 * Ma],
                        w1_r[:, l, :, 2730 + i * 128:2730 + i * 128 + Ma],
                        sem="ld_w1%d" % buf)
        ab = (i % 2) * 2 * 512
        gb = 512 + (i % 2) * 2 * 512
        mma = rec.emit("tensor", lambda e, i=i, Ma=Ma, ab=ab: e.matmul(
            PS[0:Ma, ab:ab + 512], B1[0:1, i * 128:i * 128 + Ma], ONR5[:],
            start=True, stop=False),
            waits=after(m_w1g, m_ln if i == 0 else None, m_ff.get(i - 2)), inc=None)
        for kc in range(NCH):
            mma = rec.emit("tensor", lambda e, kc=kc, Ma=Ma, buf=buf, ab=ab: e.matmul(
                PS[0:Ma, ab:ab + 512],
                W1SL[:, buf * 2048 + kc * Ma:buf * 2048 + (kc + 1) * Ma],
                XLN[:, kc * WIN:(kc + 1) * WIN], start=False, stop=(kc == NCH - 1)),
                inc="ts" if kc == NCH - 1 else None)
        mmg = rec.emit("tensor", lambda e, i=i, Ma=Ma, gb=gb: e.matmul(
            PS[0:Ma, gb:gb + 512], B1[0:1, 2730 + i * 128:2730 + i * 128 + Ma], ONR5[:],
            start=True, stop=False), waits=after(m_w1g, m_gev_prev), inc=None)
        for kc in range(NCH):
            mmg = rec.emit("tensor", lambda e, kc=kc, Ma=Ma, buf=buf, gb=gb: e.matmul(
                PS[0:Ma, gb:gb + 512],
                W1SL[:, buf * 2048 + 1024 + kc * Ma:buf * 2048 + 1024 + (kc + 1) * Ma],
                XLN[:, kc * WIN:(kc + 1) * WIN], start=False, stop=(kc == NCH - 1)),
                inc="ts" if kc == NCH - 1 else None)
        m_gev = rec.emit("scalar", lambda e, Ma=Ma, gb=gb: e.activation(
            out=QRAW[0:Ma, :], in_=PS[0:Ma, gb:gb + 512], func=AF.Gelu),
            waits=after(mmg, m_ff.get(i - 1)), inc="ac")
        m_gev_prev = m_gev
        m_ff[i] = rec.emit("vector", lambda e, i=i, Ma=Ma, ab=ab: e.tensor_tensor(
            out=SCR[0:Ma, i * WIN:(i + 1) * WIN], in0=PS[0:Ma, ab:ab + 512],
            in1=QRAW[0:Ma, :], op=ALU.mult), waits=after(mma, m_gev), inc="dv")
    m_stt5 = {}
    for m in range(NCH):
        bk = 2 + m % 2
        buf = m % 2
        m_w2a = rec.dma("gpsimd", W2SL[:, buf * 2816:buf * 2816 + 21 * 128],
                        w2f[l * 2730:l * 2730 + 2688, m * 128:(m + 1) * 128]
                        .rearrange("(k p) m -> p k m", p=128),
                        waits=after(m_stt5.get(m - 2)), sem="ld_w2%d" % buf)
        m_w2b = rec.dma("gpsimd", W2SL[0:42, buf * 2816 + 21 * 128:buf * 2816 + 22 * 128],
                        w2f[l * 2730 + 2688:l * 2730 + 2730, m * 128:(m + 1) * 128],
                        sem="ld_w2%d" % buf)
        mm = rec.emit("tensor", lambda e, m=m, bk=bk: e.matmul(
            bank(bk), B2[0:1, m * 128:(m + 1) * 128], ONR5[:], start=True, stop=False),
            waits=after(m_w2b, m_ff[NPAIR - 1] if m <= 1 else None,
                        m_ff[NPAIR - 2] if m <= 1 else None,
                        m_stt5.get(m - 2)), inc=None)
        for k2 in range(NK2):
            Ka = 128 if k2 < NK2 - 1 else 42
            mm = rec.emit("tensor", lambda e, k2=k2, Ka=Ka, bk=bk, buf=buf: e.matmul(
                bank(bk), W2SL[0:Ka, buf * 2816 + k2 * 128:buf * 2816 + k2 * 128 + 128],
                SCR[0:Ka, k2 * WIN:(k2 + 1) * WIN], start=False, stop=(k2 == NK2 - 1)),
                waits=after(m_ff[k2] if m == 0 else None),
                inc="ts" if k2 == NK2 - 1 else None)
        m_stt5[m] = rec.emit("vector", lambda e, m=m, bk=bk: e.scalar_tensor_tensor(
            out=XWs(m, 1, 1 + WIN), in0=bank(bk), scalar=1.0, in1=XWs(m, 1, 1 + WIN),
            op0=ALU.mult, op1=ALU.add), waits=after(mm), inc="dv")

    st["store"] = rec.dma("sync", x1_r[:, :, W0 + 1:W0 + 1 + WIN],
                          XW_3[:, :, 1:1 + WIN],
                          waits=after(*[m_stt5[m] for m in range(NCH)]),
                          sem="st_x1")


def build_s1(nc, layers=2, nwin_limit=None):
    rec = Rec(nc)
    ctx = Ctx(nc, rec)
    x_in = nc.dram_tensor("x", [D, 1 + NX1], F32, kind="ExternalInput").ap()
    wqkvf = nc.dram_tensor("wqkvf", [2 * D, 1536], F32, kind="ExternalInput").ap()
    w1f = nc.dram_tensor("w1f", [2 * D, 5460], F32, kind="ExternalInput").ap()
    w2f = nc.dram_tensor("w2f", [2 * 2730, 1024], F32, kind="ExternalInput").ap()
    woT = nc.dram_tensor("woT", [2 * 512, 1024], F32, kind="ExternalInput").ap()
    b1r = nc.dram_tensor("b1r", [2, 5460], F32, kind="ExternalInput").ap()
    b2r = nc.dram_tensor("b2r", [2, 1024], F32, kind="ExternalInput").ap()
    tqa = nc.dram_tensor("tqa", [128, NX1], F32, kind="ExternalInput").ap()
    tqb = nc.dram_tensor("tqb", [128, NX1], F32, kind="ExternalInput").ap()
    tka = nc.dram_tensor("tka", [128, NX1], F32, kind="ExternalInput").ap()
    tkb = nc.dram_tensor("tkb", [128, NX1], F32, kind="ExternalInput").ap()
    qscol = nc.dram_tensor("qscol", [128, 2], F32, kind="ExternalInput").ap()
    kscol = nc.dram_tensor("kscol", [128, 2], F32, kind="ExternalInput").ap()
    gcol = nc.dram_tensor("gcol", [128, 16], F32, kind="ExternalInput").ap()
    bcol = nc.dram_tensor("bcol", [128, 16], F32, kind="ExternalInput").ap()
    perm = nc.dram_tensor("perm", [128, 128], F32, kind="ExternalInput").ap()
    hsel = nc.dram_tensor("hsel", [128, 2], F32, kind="ExternalInput").ap()
    sel2 = nc.dram_tensor("sel2", [2, 128], F32, kind="ExternalInput").ap()
    ident = nc.dram_tensor("ident", [128, 128], F32, kind="ExternalInput").ap()
    mband = nc.dram_tensor("mband", [128, 640], F32, kind="ExternalInput").ap()
    hv = nc.dram_tensor("hv", [128, 36], F32, kind="ExternalInput").ap()
    hvw = nc.dram_tensor("hvw", [128, 28 * 640], F32, kind="ExternalInput").ap()
    onc = nc.dram_tensor("ones_col", [128, 1], F32, kind="ExternalInput").ap()
    onr = nc.dram_tensor("ones_row", [1, 128], F32, kind="ExternalInput").ap()
    onr5 = nc.dram_tensor("ones_row5", [1, WIN], F32, kind="ExternalInput").ap()
    x1d = nc.dram_tensor("x1d", [D, 1 + NX1], F32, kind="ExternalOutput")
    xo = nc.dram_tensor("xo", [D, T], F32, kind="ExternalOutput").ap()
    mil = emit_s1(ctx, x_in, x1d, xo, wqkvf, w1f, w2f, woT, b1r, b2r,
                  tqa, tqb, tka, tkb, qscol, kscol, gcol, bcol,
                  perm, hsel, sel2, ident, mband, hv, onc, onr, onr5,
                  hvw=hvw, layers=layers, nwin_limit=nwin_limit)
    rec.finalize(mil)
    return nc


def s1_host_tables(sb):
    half = np.arange(0, 64, 2, dtype=np.float64)
    inv_freq = 1.0 / 10000.0 ** (half / 64.0)
    base = (half + 0.4 * 64) / (1.4 * 64)
    p = np.arange(sb - HALO1, sb + T, dtype=np.float64)
    fr = p[:, None] * inv_freq[None, :]
    power = (p - 4096.0) / 256.0
    sc = base[None, :] ** power[:, None]
    cos, sin = np.cos(fr), np.sin(fr)

    def pack(t):
        t64 = np.concatenate([t, t], axis=1)
        return np.ascontiguousarray(
            np.concatenate([t64, t64], axis=1).T.astype(np.float32))
    return (pack(8.0 * cos * sc), pack(8.0 * sin * sc),
            pack(cos / sc), pack(sin / sc))


def s1_host_consts():
    permv = np.zeros((128, 128), np.float32)
    for b in (0, 64):
        for m in range(32):
            permv[b + m + 32, b + m] = -1.0
            permv[b + m, b + m + 32] = 1.0
    hselv = np.zeros((128, 2), np.float32)
    hselv[0:64, 0] = 1.0
    hselv[64:128, 1] = 1.0
    sel2v = np.zeros((2, 128), np.float32)
    sel2v[0, 0:64] = 1.0
    sel2v[1, 64:128] = 1.0
    r = np.arange(128)[:, None]
    cc = np.arange(640)[None, :]
    mbandv = np.where((cc >= r) & (cc <= r + 512), 0.0, -1e9).astype(np.float32)
    return permv, hselv, sel2v, mbandv


def s1_host_hvw(batch_start):
    hw = np.zeros((128, 28 * 640), np.float32)
    for qg in range(28):
        for j in range(640):
            kbg = qg - 4 + j // 128
            hw[:, qg * 640 + j] = 1.0 if (kbg >= 0 and (kbg >= 12 or not batch_start)) else 0.0
    return hw


def s1_host_hv(batch_start):
    hvv = np.ones((128, 36), np.float32)
    hvv[:, 0:4] = 0.0
    if batch_start:
        hvv[:, 4:16] = 0.0
    return hvv


def run_s1(x_full, wqkv, q_scale, k_scale, wo, ln_g, ln_b, w1, b1, w2, b2,
           layers=2, nwin_limit=None, want_x1=False):
    nc = build_s1(bass.Bass("TRN2", target_bir_lowering=False,
                            dynamic_dma_scratch_size=16384),
                  layers=layers, nwin_limit=nwin_limit)
    flat = np.ascontiguousarray(np.asarray(x_full).reshape(2 * 8192, D))
    permv, hselv, sel2v, mbandv = s1_host_consts()
    wqkv_h = np.ascontiguousarray(np.asarray(wqkv).reshape(2 * D, 1536))
    w1_h = np.ascontiguousarray(np.asarray(w1).reshape(2 * D, 5460))
    w2_h = np.ascontiguousarray(np.asarray(w2).reshape(2 * 2730, 1024))
    wo_h = np.ascontiguousarray(np.asarray(wo).reshape(2 * 512, 1024))
    qsc = np.zeros((128, 2), np.float32)
    ksc = np.zeros((128, 2), np.float32)
    gcv = np.zeros((128, 16), np.float32)
    bcv = np.zeros((128, 16), np.float32)
    for l in range(2):
        qsc[:, l] = np.tile(np.asarray(q_scale)[l], 2)
        ksc[:, l] = np.tile(np.asarray(k_scale)[l], 2)
        for c in range(NCH):
            gcv[:, l * 8 + c] = np.asarray(ln_g)[l, c * 128:(c + 1) * 128]
            bcv[:, l * 8 + c] = np.asarray(ln_b)[l, c * 128:(c + 1) * 128]
    in_maps = []
    for c in range(NCORES):
        s = c * T
        sb = s % 8192
        xs = np.zeros((1 + NX1, D), np.float32)
        if sb >= HALO1:
            xs[1:1 + HALO1] = flat[s - HALO1:s]
        xs[1 + HALO1:] = flat[s:s + T]
        tqa_, tqb_, tka_, tkb_ = s1_host_tables(sb)
        in_maps.append({
            "x": np.ascontiguousarray(xs.T),
            "wqkvf": wqkv_h, "w1f": w1_h, "w2f": w2_h, "woT": wo_h,
            "b1r": np.ascontiguousarray(np.asarray(b1, np.float32)),
            "b2r": np.ascontiguousarray(np.asarray(b2, np.float32)),
            "tqa": tqa_, "tqb": tqb_, "tka": tka_, "tkb": tkb_,
            "qscol": qsc, "kscol": ksc, "gcol": gcv, "bcol": bcv,
            "perm": permv, "hsel": hselv, "sel2": sel2v,
            "ident": np.eye(128, dtype=np.float32),
            "mband": mbandv, "hv": s1_host_hv(sb == 0),
            "hvw": s1_host_hvw(sb == 0),
            "ones_col": np.ones((128, 1), np.float32),
            "ones_row": np.ones((1, 128), np.float32),
            "ones_row5": np.ones((1, WIN), np.float32),
        })
    res = run_bass_kernel_spmd(nc, in_maps, core_ids=list(range(NCORES)))
    out = np.concatenate([res.results[c]["xo"].T for c in range(NCORES)], axis=0)
    if want_x1:
        x1s = [res.results[c]["x1d"].T for c in range(NCORES)]
        return out.reshape(2, 8192, D), x1s
    return out.reshape(2, 8192, D)



def shard_tokens(x_full):
    flat = np.ascontiguousarray(x_full.reshape(2 * 8192, D))
    return [np.ascontiguousarray(flat[c * T:(c + 1) * T].T) for c in range(NCORES)]


def boundary_cols(x_full):
    flat = x_full.reshape(2 * 8192, D)
    outs = []
    for c in range(NCORES):
        s = c * T
        if s % 8192 == 0:
            outs.append(np.zeros((D, 1), np.float32))
        else:
            outs.append(np.ascontiguousarray(flat[s - 1:s].T))
    return outs


def run_s4(x_full, W_r, W_k, W_v, mu_r, mu_k):
    nc = build_s4(bass.Bass("TRN2", target_bir_lowering=False,
                            dynamic_dma_scratch_size=16384))
    shards = shard_tokens(x_full)
    bnds = boundary_cols(x_full)
    wrT = np.ascontiguousarray(W_r.T)
    wkT = np.ascontiguousarray(W_k.T)
    wvT = np.ascontiguousarray(W_v.T)
    in_maps = []
    for c in range(NCORES):
        in_maps.append({
            "x": np.concatenate([bnds[c], shards[c]], axis=1),
            "wrT": wrT, "wkT": wkT, "wvT": wvT,
            "mur": np.ascontiguousarray(mu_r.reshape(NCH, 128).T),
            "muk": np.ascontiguousarray(mu_k.reshape(NCH, 128).T),
            "ones_col": np.ones((128, 1), np.float32),
            "ones_row": np.ones((1, 128), np.float32),
        })
    res = run_bass_kernel_spmd(nc, in_maps, core_ids=list(range(NCORES)))
    out = np.concatenate([res.results[c]["xo"].T for c in range(NCORES)], axis=0)
    return out.reshape(2, 8192, D)


def run_s2(x_full, Wq, Wk, Wv, Wo, Kq, Kk, stage=100):
    nc = build_s2(bass.Bass("TRN2", target_bir_lowering=False,
                            dynamic_dma_scratch_size=16384), stage=stage)
    shards = shard_tokens(x_full)
    kqp, kkp, tri = s2_host_inputs(np.asarray(Kq), np.asarray(Kk))
    in_maps = []
    for c in range(NCORES):
        selv = np.zeros((128, NCORES), np.float32)
        b0 = (c // 4) * 4
        for r in range(b0, c):
            selv[:, r] = 1.0
        in_maps.append({
            "x": shards[c], "wq": np.ascontiguousarray(Wq), "wk": np.ascontiguousarray(Wk),
            "wv": np.ascontiguousarray(Wv), "wo": np.ascontiguousarray(Wo),
            "kqp": kqp, "kkp": kkp, "sel": selv,
            "ones_col": np.ones((128, 1), np.float32),
            "ones_row": np.ones((1, 128), np.float32),
            "tri": tri, "ident": np.eye(128, dtype=np.float32),
        })
    res = run_bass_kernel_spmd(nc, in_maps, core_ids=list(range(NCORES)))
    out = np.concatenate([res.results[c]["xo"].T for c in range(NCORES)], axis=0)
    return out.reshape(2, 8192, D)




def emit_xhalo(ctx, pfx, src_body, dstL, dstR, L, R, selL, selR):
    nc, rec = ctx.nc, ctx.rec
    ncols = L + R
    cc_in = nc.dram_tensor(pfx + "cci", [D, ncols], F32)
    cc_out = nc.dram_tensor(pfx + "cco", [4 * D, ncols], F32)
    SEL = ctx.sb("SEL", [128, 8], F32)
    HB = ctx.sb("HB", [128, NCH * 512], F32)
    RB = ctx.sb("RBUF", [128, 2 * NCH * 512], F32)

    m_sl = rec.dma("sync", SEL[:, 0:4], selL, sem="hx_sl")
    m_sr = rec.dma("sync", SEL[:, 4:8], selR, sem="hx_sr")
    ms = []
    if L:
        ms.append(rec.dma("sync", cc_in.ap()[:, 0:L], src_body[:, T - L:T],
                          sem="hx_cil",
                          allow_slow_non_contiguous=(L < 8)))
    if R:
        ms.append(rec.dma("sync", cc_in.ap()[:, L:L + R], src_body[:, 0:R],
                          sem="hx_cir",
                          allow_slow_non_contiguous=True))
    m_ag = rec.emit("gpsimd", lambda e: e.collective_compute(
        "AllGather", ALU.bypass, replica_groups=[[0, 1, 2, 3], [4, 5, 6, 7]],
        ins=[cc_in.ap()], outs=[cc_out.ap()]), waits=after(*ms), inc="cc")
    outms = []
    m_add = {}
    for (side, base, ncol, dst, selo) in (("L", 0, L, dstL, 0),
                                          ("R", L, R, dstR, 4)):
        if ncol == 0:
            continue
        for c0 in range(0, ncol, 512):
            ck = min(512, ncol - c0)
            mz = rec.emit("vector", lambda e, ck=ck: e.memset(
                HB[:, 0:NCH * ck], 0.0), waits=after(outms[-1] if outms else None),
                inc="dv")
            ma = mz
            for r in range(4):
                m_ld = rec.dma("sync",
                               RB.ap()[:, (r % 2) * NCH * 512:(r % 2) * NCH * 512 + NCH * ck]
                               .rearrange("p (c t) -> p c t", c=NCH),
                               cc_out.ap()[r * D:(r + 1) * D, base + c0:base + c0 + ck]
                               .rearrange("(c p) t -> p c t", p=128),
                               waits=after(m_ag if r == 0 else None,
                                           m_add.get(r - 2)),
                               sem="hx_" + side + "r%d" % (r % 2),
                               allow_slow_non_contiguous=(ck < 8))
                ma = rec.emit("vector", lambda e, r=r, ck=ck, selo=selo: e.scalar_tensor_tensor(
                    out=HB[:, 0:NCH * ck], in0=RB[:, (r % 2) * NCH * 512:(r % 2) * NCH * 512 + NCH * ck],
                    scalar=SEL[:, selo + r:selo + r + 1], in1=HB[:, 0:NCH * ck],
                    op0=ALU.mult, op1=ALU.add),
                    waits=after(m_ld, ma, m_sl if r == 0 else None,
                                m_sr if r == 0 else None), inc="dv")
                m_add[r] = ma
            m_st = rec.dma("sync", dst[:, c0:c0 + ck]
                           .rearrange("(c p) t -> p c t", p=128),
                           HB.ap()[:, 0:NCH * ck].rearrange("p (c t) -> p c t", c=NCH),
                           waits=after(ma), sem="hx_" + side + "st",
                           allow_slow_non_contiguous=(ck < 8))
            outms.append(m_st)
    return outms


def build_fused(nc):
    rec = Rec(nc)
    ctx = Ctx(nc, rec, arena=(17024, 229376))
    EI = lambda n, sh: nc.dram_tensor(n, sh, F32, kind="ExternalInput").ap()
    x_in = EI("x", [D, 1 + NX1])
    wqkvf = EI("wqkvf", [2 * D, 1536]); w1f = EI("w1f", [2 * D, 5460])
    w2f = EI("w2f", [2 * 2730, 1024]); woT = EI("woT", [2 * 512, 1024])
    b1r = EI("b1r", [2, 5460]); b2r = EI("b2r", [2, 1024])
    tqa = EI("tqa", [128, NX1]); tqb = EI("tqb", [128, NX1])
    tka = EI("tka", [128, NX1]); tkb = EI("tkb", [128, NX1])
    qscol = EI("qscol", [128, 2]); kscol = EI("kscol", [128, 2])
    gcol = EI("gcol", [128, 16]); bcol = EI("bcol", [128, 16])
    perm = EI("perm", [128, 128]); hsel = EI("hsel", [128, 2])
    sel2 = EI("sel2", [2, 128]); ident = EI("ident", [128, 128])
    mband = EI("mband", [128, 640]); hv = EI("hv", [128, 36])
    hvw = EI("hvw", [128, 28 * 640])
    onc = EI("ones_col", [128, 1]); onr = EI("ones_row", [1, 128])
    onr5 = EI("ones_row5", [1, WIN])
    wq = EI("wq", [D, D]); wk = EI("wk", [D, D]); wv = EI("wv", [D, D])
    wo2 = EI("wo", [D, D])
    kqp = EI("kqp", [NHEAD * 128, 128]); kkp = EI("kkp", [NHEAD * 128, 128])
    sel = EI("sel", [128, NCORES]); tri = EI("tri", [128, 128])
    wrT = EI("wrT", [D, D]); wkT = EI("wkT", [D, D]); wvT = EI("wvT", [D, D])
    mur = EI("mur", [128, NCH]); muk = EI("muk", [128, NCH])
    pewT = EI("pewT", [D, D]); qkvwT = EI("qkvwT", [D, 3 * D])
    owT = EI("owT", [D, D]); taps = EI("taps", [128, 3 * NCH])
    selL = EI("selL", [128, 4]); selR = EI("selR", [128, 4])
    x1d = nc.dram_tensor("x1d", [D, 1 + NX1], F32, kind="ExternalOutput")
    xs2 = nc.dram_tensor("xs2", [D, T], F32, kind="ExternalOutput")
    xs3 = nc.dram_tensor("xs3", [D, 1 + NX1], F32, kind="ExternalOutput")
    xs4 = nc.dram_tensor("xs4", [D, T + 1], F32, kind="ExternalOutput")
    xs5 = nc.dram_tensor("xs5", [D, T + 2], F32, kind="ExternalOutput")
    xs6 = nc.dram_tensor("xs6", [D, T + 1], F32, kind="ExternalOutput")
    oacc = nc.dram_tensor("oacc", [NHEAD * 64, T], F32, kind="ExternalOutput")
    ozacc = nc.dram_tensor("ozacc", [NHEAD, T], F32, kind="ExternalOutput")
    xo = nc.dram_tensor("xo", [D, T], F32, kind="ExternalOutput").ap()

    s1_args = (wqkvf, w1f, w2f, woT, b1r, b2r, tqa, tqb, tka, tkb, qscol,
               kscol, gcol, bcol, perm, hsel, sel2, ident, mband, hv, onc,
               onr, onr5)
    s1_kw = dict(hvw=hvw)
    ctx.reset("s1_")
    m1 = emit_s1(ctx, x_in, x1d, xs2.ap(), *s1_args, **s1_kw)
    rec.gate = tuple(m1)
    ctx.reset("s2_")
    m2 = emit_s2(ctx, xs2.ap(), xs3.ap()[:, 1 + HALO1:1 + NX1], wq, wk, wv,
                 wo2, kqp, kkp, sel, onc, onr, tri, ident)
    rec.gate = tuple(m2)
    ctx.reset("h3_")
    mh3 = emit_xhalo(ctx, "h3_", xs3.ap()[:, 1 + HALO1:1 + NX1],
                     xs3.ap()[:, 1:1 + HALO1], None, HALO1, 0, selL, selR)
    rec.gate = tuple(m2) + tuple(mh3)
    ctx.reset("s3_")
    m3 = emit_s1(ctx, xs3.ap(), x1d, xs4.ap()[:, 1:1 + T], *s1_args, **s1_kw)
    rec.gate = tuple(m3)
    ctx.reset("h4_")
    mh4 = emit_xhalo(ctx, "h4_", xs4.ap()[:, 1:1 + T], xs4.ap()[:, 0:1],
                     None, 1, 0, selL, selR)
    rec.gate = tuple(m3) + tuple(mh4)
    ctx.reset("s4_")
    m4 = emit_s4(ctx, xs4.ap(), xs5.ap()[:, 1:1 + T], wrT, wkT, wvT, mur,
                 muk, onc, onr)
    rec.gate = tuple(m4)
    ctx.reset("h5_")
    mh5 = emit_xhalo(ctx, "h5_", xs5.ap()[:, 1:1 + T], xs5.ap()[:, 0:1],
                     xs5.ap()[:, 1 + T:2 + T], 1, 1, selL, selR)
    rec.gate = tuple(m4) + tuple(mh5)
    ctx.reset("s5_")
    m5 = emit_s5(ctx, xs5.ap(), xs6.ap()[:, 1:1 + T], pewT, qkvwT, owT, taps,
                 onc, onr, sel2, oacc, ozacc)
    rec.gate = tuple(m5)
    ctx.reset("h6_")
    mh6 = emit_xhalo(ctx, "h6_", xs6.ap()[:, 1:1 + T], xs6.ap()[:, 0:1],
                     None, 1, 0, selL, selR)
    rec.gate = tuple(m5) + tuple(mh6)
    ctx.reset("s6_")
    m6 = emit_s4(ctx, xs6.ap(), xo, wrT, wkT, wvT, mur, muk, onc, onr)
    rec.finalize(m6)
    return nc


def kernel_fused(inputs):
    nc = build_fused(bass.Bass("TRN2", target_bir_lowering=False,
                               dynamic_dma_scratch_size=16384))
    flat = np.ascontiguousarray(
        np.asarray(inputs["L0_attn_x"], np.float32).reshape(2 * 8192, D))
    permv, hselv, sel2v, mbandv = s1_host_consts()
    kqp, kkp, tri = s2_host_inputs(np.asarray(inputs["L0_attn_Kq"]),
                                   np.asarray(inputs["L0_attn_Kk"]))
    wqkv_h = np.ascontiguousarray(np.asarray(inputs["L0_ffn_wqkv"]).reshape(2 * D, 1536))
    w1_h = np.ascontiguousarray(np.asarray(inputs["L0_ffn_w1"]).reshape(2 * D, 5460))
    w2_h = np.ascontiguousarray(np.asarray(inputs["L0_ffn_w2"]).reshape(2 * 2730, 1024))
    wo_h = np.ascontiguousarray(np.asarray(inputs["L0_ffn_wo"]).reshape(2 * 512, 1024))
    qsc = np.zeros((128, 2), np.float32); ksc = np.zeros((128, 2), np.float32)
    gcv = np.zeros((128, 16), np.float32); bcv = np.zeros((128, 16), np.float32)
    for l in range(2):
        qsc[:, l] = np.tile(np.asarray(inputs["L0_ffn_q_scale"])[l], 2)
        ksc[:, l] = np.tile(np.asarray(inputs["L0_ffn_k_scale"])[l], 2)
        for c in range(NCH):
            gcv[:, l * 8 + c] = np.asarray(inputs["L0_ffn_ln_g"])[l, c * 128:(c + 1) * 128]
            bcv[:, l * 8 + c] = np.asarray(inputs["L0_ffn_ln_b"])[l, c * 128:(c + 1) * 128]
    taps_h = np.ascontiguousarray(np.asarray(inputs["L1_attn_conv_w"], np.float32)[:, 0, :, 1])
    taps_t = np.zeros((128, 3 * NCH), np.float32)
    for c in range(NCH):
        taps_t[:, 3 * c:3 * (c + 1)] = taps_h[c * 128:(c + 1) * 128]
    mu_r = np.asarray(inputs["L1_ffn_mu_r"], np.float32)
    mu_k = np.asarray(inputs["L1_ffn_mu_k"], np.float32)
    in_maps = []
    for c in range(NCORES):
        s = c * T
        sb = s % 8192
        xs = np.zeros((1 + NX1, D), np.float32)
        if sb >= HALO1:
            xs[1:1 + HALO1] = flat[s - HALO1:s]
        xs[1 + HALO1:] = flat[s:s + T]
        tqa_, tqb_, tka_, tkb_ = s1_host_tables(sb)
        selv = np.zeros((128, NCORES), np.float32)
        for r in range((c // 4) * 4, c):
            selv[:, r] = 1.0
        slv = np.zeros((128, 4), np.float32)
        srv = np.zeros((128, 4), np.float32)
        if c % 4 > 0:
            slv[:, c % 4 - 1] = 1.0
        if c % 4 < 3:
            srv[:, c % 4 + 1] = 1.0
        in_maps.append({
            "x": np.ascontiguousarray(xs.T),
            "wqkvf": wqkv_h, "w1f": w1_h, "w2f": w2_h, "woT": wo_h,
            "b1r": np.ascontiguousarray(np.asarray(inputs["L0_ffn_b1"], np.float32)),
            "b2r": np.ascontiguousarray(np.asarray(inputs["L0_ffn_b2"], np.float32)),
            "tqa": tqa_, "tqb": tqb_, "tka": tka_, "tkb": tkb_,
            "qscol": qsc, "kscol": ksc, "gcol": gcv, "bcol": bcv,
            "perm": permv, "hsel": hselv, "sel2": sel2v,
            "ident": np.eye(128, dtype=np.float32),
            "mband": mbandv, "hv": s1_host_hv(sb == 0),
            "hvw": s1_host_hvw(sb == 0),
            "ones_col": np.ones((128, 1), np.float32),
            "ones_row": np.ones((1, 128), np.float32),
            "ones_row5": np.ones((1, WIN), np.float32),
            "wq": np.ascontiguousarray(np.asarray(inputs["L0_attn_Wq"], np.float32)),
            "wk": np.ascontiguousarray(np.asarray(inputs["L0_attn_Wk"], np.float32)),
            "wv": np.ascontiguousarray(np.asarray(inputs["L0_attn_Wv"], np.float32)),
            "wo": np.ascontiguousarray(np.asarray(inputs["L0_attn_Wo"], np.float32)),
            "kqp": kqp, "kkp": kkp, "sel": selv, "tri": tri,
            "wrT": np.ascontiguousarray(np.asarray(inputs["L1_ffn_W_r"], np.float32).T),
            "wkT": np.ascontiguousarray(np.asarray(inputs["L1_ffn_W_k"], np.float32).T),
            "wvT": np.ascontiguousarray(np.asarray(inputs["L1_ffn_W_v"], np.float32).T),
            "mur": np.ascontiguousarray(mu_r.reshape(NCH, 128).T),
            "muk": np.ascontiguousarray(mu_k.reshape(NCH, 128).T),
            "pewT": np.ascontiguousarray(np.asarray(inputs["L1_attn_pe_w"], np.float32).T),
            "qkvwT": np.ascontiguousarray(np.asarray(inputs["L1_attn_qkv_w"], np.float32).T),
            "owT": np.ascontiguousarray(np.asarray(inputs["L1_attn_out_w"], np.float32).T),
            "taps": taps_t,
            "selL": slv, "selR": srv,
        })
    res = run_bass_kernel_spmd(nc, in_maps, core_ids=list(range(NCORES)))
    out = np.concatenate([res.results[c]["xo"].T for c in range(NCORES)], axis=0)
    return np.ascontiguousarray(out.reshape(2, 8192, D).astype(np.float32))


def kernel(**inputs):
    x = np.asarray(inputs["L0_attn_x"], np.float32)
    s1_args = (inputs["L0_ffn_wqkv"], inputs["L0_ffn_q_scale"],
               inputs["L0_ffn_k_scale"], inputs["L0_ffn_wo"],
               inputs["L0_ffn_ln_g"], inputs["L0_ffn_ln_b"],
               inputs["L0_ffn_w1"], inputs["L0_ffn_b1"],
               inputs["L0_ffn_w2"], inputs["L0_ffn_b2"])
    s4_args = (np.asarray(inputs["L1_ffn_W_r"], np.float32),
               np.asarray(inputs["L1_ffn_W_k"], np.float32),
               np.asarray(inputs["L1_ffn_W_v"], np.float32),
               np.asarray(inputs["L1_ffn_mu_r"], np.float32),
               np.asarray(inputs["L1_ffn_mu_k"], np.float32))
    del x, s1_args, s4_args
    return kernel_fused(inputs)
